# version 11
# baseline (speedup 1.0000x reference)
import numpy as np
import ml_dtypes
from contextlib import ExitStack

import concourse.bass as bass
import concourse.mybir as mybir
from concourse.bass_utils import run_bass_kernel_spmd
from concourse.aot_env import allow_ici_all_to_all

F32 = mybir.dt.float32
BF16 = mybir.dt.bfloat16
AF = mybir.ActivationFunctionType
ALU = mybir.AluOpType
AX = mybir.AxisListType

NCORES = 8
DM = 1024
DI = 2048
T = 2048
S = 16
DMT = DM // 128
DIT = DI // 128
FC = 512
NFC = T // FC
TSC = 256
NSC = T // TSC
FF = 2048
NEG = -1.0e30
BF = ml_dtypes.bfloat16


class Sem:
    def __init__(self, h):
        self.h = h
        self.c = 0


class Plan:

    def __init__(self, nc, auto=None):
        self.nc = nc
        self.items = []
        self.auto = auto or {}

    def op(self, eng, fn, incs=()):
        sem = self.auto.get(eng)
        if sem is not None:
            self.items.append(("wait", eng, sem.h, sem.c))
            if not any(s is sem for s, _ in incs):
                incs = tuple(incs) + ((sem, 1),)
        for s, n in incs:
            s.c += n
        self.items.append(("op", eng, fn, tuple(incs)))

    def wait(self, eng, sem, val=None):
        self.items.append(("wait", eng, sem.h, sem.c if val is None else val))

    def run(self, eng_name):
        eng = getattr(self.nc, eng_name)
        for it in self.items:
            if it[0] == "op":
                _, e, fn, incs = it
                if e == eng_name:
                    ins = fn()
                    for s, n in incs:
                        ins = ins.then_inc(s.h, n)
            else:
                _, e, h, val = it
                if e == eng_name:
                    eng.wait_ge(h, val)


class Arena:
    def __init__(self, ap, cols):
        self.ap = ap
        self.cols = cols
        self.off = 0

    def f32(self, cols, rows=128):
        assert self.off + cols <= self.cols, f"arena overflow {self.off}+{cols}>{self.cols}"
        v = self.ap[0:rows, self.off:self.off + cols]
        self.off += cols
        return v

    def bf16(self, cols, rows=128):
        n32 = (cols + 1) // 2
        return self.f32(n32, rows=rows).bitcast(BF16)

    def mark(self):
        return self.off

    def reset(self, off=0):
        self.off = off


class Banks:
    def __init__(self, plan, banks):
        self.banks = [{"ap": b, "consumer": None} for b in banks]
        self.plan = plan
        self.i = 0

    def acquire(self):
        b = self.banks[self.i % len(self.banks)]
        self.i += 1
        if b["consumer"] is not None:
            sem, cnt = b["consumer"]
            self.plan.wait("tensor", sem, cnt)
        return b

    def release(self, b, sem):
        b["consumer"] = (sem, sem.c)


def _w_load_view(w, c0, cw, ktiles):
    return w[:, c0:c0 + cw].rearrange("(k p) c -> p k c", p=128)


def _mk_fc(nc, e, maincols):
    dram = nc.dram_tensor
    fc = {"nc": nc}
    fc["mainbuf"] = e(nc.sbuf_tensor("mainbuf", [128, maincols], F32)).ap()
    fc["perbuf"] = e(nc.sbuf_tensor("perbuf", [128, 9920], F32)).ap()
    fc["psb"] = [e(nc.psum_tensor(f"ps{i}", [128, 512], F32)).ap() for i in range(6)]
    fc["psT"] = [e(nc.psum_tensor(f"psT{i}", [128, 1024], BF16)).ap() for i in range(2)]
    for nm in ("qS", "qG", "qA", "qC", "eP", "eV", "eS", "eG"):
        fc[nm] = Sem(e(nc.semaphore(nm)))
    plan = Plan(nc)
    plan.auto = {"vector": fc["eV"], "scalar": fc["eS"]}
    fc["plan"] = plan
    fc["bk"] = Banks(plan, fc["psb"])
    fc["bkT"] = Banks(plan, fc["psT"])
    fc["M"] = Arena(fc["mainbuf"], maincols)
    fc["P"] = Arena(fc["perbuf"], 9920)
    fc["ones_col"] = dram("ones_col", [128, 1], F32, kind="ExternalInput").ap()
    fc["ones_row"] = dram("ones_row", [1, 128], F32, kind="ExternalInput").ap()
    fc["ident_f32"] = dram("ident_f32", [128, 128], F32, kind="ExternalInput").ap()
    fc["ident_b16"] = dram("ident_b16", [128, 128], BF16, kind="ExternalInput").ap()
    fc["tri_lt"] = dram("tri_lt", [128, 64], F32, kind="ExternalInput").ap()

    def dma(eng, q, dst, src):
        plan.op(eng, lambda d=dst, s=src, g=eng: getattr(nc, g).dma_start(out=d, in_=s), [(q, 16)])

    fc["dma"] = dma
    _ccols = {}

    def ccol(val):
        if val not in _ccols:
            ap = fc["P"].f32(1)
            plan.op("vector", lambda ap=ap, val=val: nc.vector.memset(ap, float(val)), [(fc["eV"], 1)])
            _ccols[val] = (ap, fc["eV"].c)
        return _ccols[val][0]

    def ccol_wait(eng, val):
        plan.wait(eng, fc["eV"], _ccols[val][1])

    fc["ccol"] = ccol
    fc["ccol_wait"] = ccol_wait
    return fc


def _run_block(fc, e):
    block = e(fc["nc"].Block())
    plan = fc["plan"]
    block.sync(lambda eng: plan.run("sync"))
    block.gpsimd(lambda eng: plan.run("gpsimd"))
    block.scalar(lambda eng: plan.run("scalar"))
    block.vector(lambda eng: plan.run("vector"))
    block.tensor(lambda eng: plan.run("tensor"))



def _emit_l0(C, dbg=(), standalone=True):
    nc = C["nc"]; plan = C["plan"]; M = C["M"]; P = C["P"]; bk = C["bk"]
    qS, qG, qA = C["qS"], C["qG"], C["qA"]
    eP, eV, eS = C["eP"], C["eV"], C["eS"]
    dma, ccol, ccol_wait = C["dma"], C["ccol"], C["ccol_wait"]
    ones_col, ones_row = C["ones_col"], C["ones_row"]
    ident_f32, tri_lt = C["ident_f32"], C["tri_lt"]
    dram = nc.dram_tensor

    xT = dram("xT", [DM, T], F32, kind="ExternalInput").ap()
    W_in = dram("W_in", [DM, 2 * DI], F32, kind="ExternalInput").ap()
    conv_w = dram("conv_w", [DI, 4], F32, kind="ExternalInput").ap()
    conv_bc = dram("conv_bc", [128, DIT], F32, kind="ExternalInput").ap()
    W_x = dram("W_x", [DI, 2 * S], F32, kind="ExternalInput").ap()
    W_dt = dram("W_dt", [DI, DI], F32, kind="ExternalInput").ap()
    b_dtc = dram("b_dtc", [128, DIT], F32, kind="ExternalInput").ap()
    A_log = dram("A_log", [DI, S], F32, kind="ExternalInput").ap()
    D_c = dram("D_c", [128, DIT], F32, kind="ExternalInput").ap()
    W_out = dram("W_out", [DI, DM], F32, kind="ExternalInput").ap()
    Wg = dram("Wg", [DM, 8], F32, kind="ExternalInput").ap()
    Ws1 = dram("Ws1", [DM, FF], F32, kind="ExternalInput").ap()
    Ws3 = dram("Ws3", [DM, FF], F32, kind="ExternalInput").ap()
    Ws2 = dram("Ws2", [FF, DM], F32, kind="ExternalInput").ap()
    We1 = dram("We1", [8, DM, FF], F32, kind="ExternalInput").ap()
    We3 = dram("We3", [8, DM, FF], F32, kind="ExternalInput").ap()
    We2 = dram("We2", [8, FF, DM], F32, kind="ExternalInput").ap()

    x1_out = dram("x1_out", [DM, T], F32, kind="ExternalOutput").ap() if standalone else None

    dbg_outs = {}

    def mkdbg(name, shape, dtyp=F32):
        if name in dbg:
            dbg_outs[name] = dram("dbg_" + name, shape, dtyp, kind="ExternalOutput").ap()
        return dbg_outs.get(name)

    d_xn = mkdbg("xn", [DM, T], BF16)
    d_xc = mkdbg("xc", [DI, T], BF16)
    d_dt = mkdbg("dt", [DI, T], BF16)
    d_gz = mkdbg("gz", [DI, T], BF16)
    d_bc = mkdbg("bc", [2 * S, T], BF16)
    d_mo = mkdbg("mo", [DM, T])
    d_w = mkdbg("w", [T, 8])
    d_gT = mkdbg("gT", [8, T])
    d_gtm = mkdbg("gtm", [128, 128])

    xn_sp = dram("xn_sp", [DM, T], BF16, kind="Internal").ap()
    xc_sp = dram("xc_sp", [DI, T], BF16, kind="Internal").ap()
    dt_sp = dram("dt_sp", [DI, T], BF16, kind="Internal").ap()
    gz_sp = dram("gz_sp", [DI, T], BF16, kind="Internal").ap()
    bc_sp = dram("bc_sp", [2 * S, T], BF16, kind="Internal").ap()
    mo_sp = dram("mo_sp", [DM, T], F32, kind="Internal").ap()

    if True:
        xn16 = P.bf16(DMT * T)
        A_sb = P.f32(DIT * S)
        conv_w_sb = P.f32(DIT * 4)
        conv_b_sb = P.f32(DIT)
        b_dt_sb = P.f32(DIT)
        D_sb = P.f32(DIT)
        carry = P.f32(DIT * S)
        ones_col_sb = P.f32(1)
        ones_row_sb = P.f32(128, rows=1)
        tmp_pa = P.f32(DIT * S)

        dma("gpsimd", qG, conv_w_sb.rearrange("p (k c) -> p k c", c=4),
            conv_w.rearrange("(k p) c -> p k c", p=128))
        dma("gpsimd", qG, conv_b_sb, conv_bc)
        dma("gpsimd", qG, b_dt_sb, b_dtc)
        dma("gpsimd", qG, D_sb, D_c)
        dma("gpsimd", qG, tmp_pa.rearrange("p (k s) -> p k s", s=S),
            A_log.rearrange("(k p) s -> p k s", p=128))
        dma("gpsimd", qG, ones_col_sb, ones_col)
        dma("gpsimd", qG, ones_row_sb, ones_row)
        init_loads = qG.c
        plan.wait("scalar", qG, init_loads)
        plan.op("scalar", lambda: nc.scalar.activation(out=A_sb, in_=tmp_pa, func=AF.Exp), [(eS, 1)])
        plan.wait("vector", eS)
        plan.op("vector", lambda: nc.vector.tensor_scalar_mul(A_sb, A_sb, -1.0), [(eV, 1)])
        aexp_done_v = eV.c
        plan.op("vector", lambda: nc.vector.memset(carry, 0.0), [(eV, 1)])
        carry_init_v = eV.c

        xtiles = [M.f32(T) for _ in range(DMT)]
        m_row = M.f32(T, rows=1)
        rstd_row = M.f32(T, rows=1)
        m_b = M.f32(T)
        rstd_b = M.f32(T)
        sqbuf = M.f32(T)

        for dmt in range(DMT):
            dma("sync", qS, xtiles[dmt], xT[dmt * 128:(dmt + 1) * 128, :])
        x_loaded = qS.c
        plan.wait("tensor", qS, x_loaded)

        mean_banks = []
        for fc in range(NFC):
            b = bk.acquire()
            mean_banks.append(b)
            for dmt in range(DMT):
                plan.op("tensor", lambda b=b, dmt=dmt, fc=fc: nc.tensor.matmul(
                    b["ap"][0:1, :], ones_col_sb, xtiles[dmt][:, fc * FC:(fc + 1) * FC],
                    start=(dmt == 0), stop=(dmt == DMT - 1)),
                    [(eP, 1)] if dmt == DMT - 1 else ())
        for fc in range(NFC):
            plan.wait("vector", eP)
            plan.op("vector", lambda fc=fc, b=mean_banks[fc]: nc.vector.tensor_scalar_mul(
                m_row[:, fc * FC:(fc + 1) * FC], b["ap"][0:1, :], 1.0 / DM), [(eV, 1)])
            bk.release(mean_banks[fc], eV)
        m_row_v = eV.c
        plan.wait("tensor", eV, m_row_v)
        for fc in range(NFC):
            b = bk.acquire()
            plan.op("tensor", lambda b=b, fc=fc: nc.tensor.matmul(
                b["ap"], ones_row_sb, m_row[:, fc * FC:(fc + 1) * FC], start=True, stop=True), [(eP, 1)])
            plan.wait("vector", eP)
            plan.op("vector", lambda b=b, fc=fc: nc.vector.tensor_copy(
                out=m_b[:, fc * FC:(fc + 1) * FC], in_=b["ap"]), [(eV, 1)])
            bk.release(b, eV)
        mb_done_v = eV.c
        for dmt in range(DMT):
            plan.op("vector", lambda dmt=dmt: nc.vector.tensor_sub(
                xtiles[dmt], xtiles[dmt], m_b), [(eV, 1)])
        xc_done_v = eV.c
        var_banks = [bk.acquire() for _ in range(NFC)]
        for dmt in range(DMT):
            plan.wait("scalar", eV, xc_done_v)
            if dmt > 0:
                plan.wait("scalar", eP)
            plan.op("scalar", lambda dmt=dmt: nc.scalar.activation(
                out=sqbuf, in_=xtiles[dmt], func=AF.Square), [(eS, 1)])
            plan.wait("tensor", eS)
            for fc in range(NFC):
                plan.op("tensor", lambda b=var_banks[fc], fc=fc, dmt=dmt: nc.tensor.matmul(
                    b["ap"][0:1, :], ones_col_sb, sqbuf[:, fc * FC:(fc + 1) * FC],
                    start=(dmt == 0), stop=(dmt == DMT - 1)),
                    [(eP, 1)])
        c_invdm = ccol(1.0 / DM)
        c_eps = ccol(1e-5)
        c_nhalf = ccol(-0.5)
        ccol_wait("scalar", 1.0 / DM)
        ccol_wait("scalar", 1e-5)
        ccol_wait("scalar", -0.5)
        plan.wait("scalar", eP)
        for fc in range(NFC):
            plan.op("scalar", lambda b=var_banks[fc], fc=fc: nc.scalar.activation(
                out=rstd_row[:, fc * FC:(fc + 1) * FC], in_=b["ap"][0:1, :], func=AF.Ln,
                scale=c_invdm[0:1, :], bias=c_eps[0:1, :]), [(eS, 1)])
        plan.wait("scalar", eS)
        plan.op("scalar", lambda: nc.scalar.activation(
            out=rstd_row, in_=rstd_row, func=AF.Exp, scale=c_nhalf[0:1, :]), [(eS, 1)])
        rstd_done_s = eS.c
        for fc in range(NFC):
            bk.release(var_banks[fc], eS)
        plan.wait("tensor", eS, rstd_done_s)
        for fc in range(NFC):
            b = bk.acquire()
            plan.op("tensor", lambda b=b, fc=fc: nc.tensor.matmul(
                b["ap"], ones_row_sb, rstd_row[:, fc * FC:(fc + 1) * FC], start=True, stop=True), [(eP, 1)])
            plan.wait("vector", eP)
            plan.op("vector", lambda b=b, fc=fc: nc.vector.tensor_copy(
                out=rstd_b[:, fc * FC:(fc + 1) * FC], in_=b["ap"]), [(eV, 1)])
            bk.release(b, eV)
        for dmt in range(DMT):
            plan.op("vector", lambda dmt=dmt: nc.vector.tensor_mul(
                xn16[:, dmt * T:(dmt + 1) * T], xtiles[dmt], rstd_b), [(eV, 1)])
            plan.wait("gpsimd", eV)
            dma("gpsimd", qG, xn_sp[dmt * 128:(dmt + 1) * 128, :], xn16[:, dmt * T:(dmt + 1) * T])
        xn_done_v = eV.c

        M.reset()
        wstg = [M.f32(DIT * 128) for _ in range(2)]
        wstg16 = [M.bf16(DIT * 128) for _ in range(2)]
        xc16 = M.bf16(DIT * T)
        xp_sb = M.f32(T)
        xcb_sb = M.f32(T)
        gz16_t = M.bf16(T)
        dt16_t = M.bf16(T)
        e_sb = M.f32(FC)
        bc_sb = M.bf16(T, rows=32)

        stg_free_v = [eV.c, eV.c]
        stg_free_p = [eP.c, eP.c]

        def stream_w(idx, wap, c0, cw, kt):
            i = idx % 2
            plan.wait("sync", eV, stg_free_v[i])
            dma("sync", qS, wstg[i][:, :kt * cw].rearrange("p (k c) -> p k c", c=cw),
                _w_load_view(wap, c0, cw, kt))
            plan.wait("vector", qS)
            plan.wait("vector", eP, stg_free_p[i])
            plan.op("vector", lambda i=i, n=kt * cw: nc.vector.tensor_copy(
                out=wstg16[i][:, :n], in_=wstg[i][:, :n]), [(eV, 1)])
            stg_free_v[i] = eV.c
            return i, eV.c

        def mm_chain(bankobj, w16ap, rhs_fn, nkt, fcw, wait_v, wcol=128):
            plan.wait("tensor", eV, wait_v)
            for kt in range(nkt):
                plan.op("tensor", lambda b=bankobj, w=w16ap, kt=kt, f=rhs_fn, fcw=fcw, wcol=wcol: nc.tensor.matmul(
                    b["ap"][0:wcol, :fcw], w[:, kt * wcol:(kt + 1) * wcol], f(kt),
                    start=(kt == 0), stop=(kt == nkt - 1)), [(eP, 1)] if kt == nkt - 1 else ())

        for ot in range(32):
            i, wv = stream_w(ot, W_in, ot * 128, 128, DMT)
            fcbanks = []
            for fc in range(NFC):
                b = bk.acquire()
                fcbanks.append(b)
                mm_chain(b, wstg16[i], lambda kt, fc=fc: xn16[:, kt * T + fc * FC: kt * T + fc * FC + FC],
                         DMT, FC, wv)
            stg_free_p[i] = eP.c
            if ot < DIT:
                dit = ot
                for fc in range(NFC):
                    plan.wait("scalar", eP)
                    plan.op("scalar", lambda b=fcbanks[fc], fc=fc: nc.scalar.copy(
                        out=xp_sb[:, fc * FC:(fc + 1) * FC], in_=b["ap"]), [(eS, 1)])
                    bk.release(fcbanks[fc], eS)
                xp_done = eS.c
                plan.wait("vector", eS, xp_done)
                plan.wait("vector", qG, init_loads)
                plan.op("vector", lambda dit=dit: nc.vector.tensor_scalar(
                    out=xcb_sb, in0=xp_sb, scalar1=conv_w_sb[:, dit * 4 + 3:dit * 4 + 4],
                    scalar2=conv_b_sb[:, dit:dit + 1], op0=ALU.mult, op1=ALU.add), [(eV, 1)])
                for k in (2, 1, 0):
                    sh = 3 - k
                    plan.op("vector", lambda dit=dit, k=k, sh=sh: nc.vector.scalar_tensor_tensor(
                        out=xcb_sb[:, sh:T], in0=xp_sb[:, 0:T - sh],
                        scalar=conv_w_sb[:, dit * 4 + k:dit * 4 + k + 1],
                        in1=xcb_sb[:, sh:T], op0=ALU.mult, op1=ALU.add), [(eV, 1)])
                conv_done = eV.c
                plan.wait("scalar", eV, conv_done)
                plan.op("scalar", lambda dit=dit: nc.scalar.activation(
                    out=xc16[:, dit * T:(dit + 1) * T], in_=xcb_sb, func=AF.Silu), [(eS, 1)])
                plan.wait("gpsimd", eS)
                dma("gpsimd", qG, xc_sp[dit * 128:(dit + 1) * 128, :], xc16[:, dit * T:(dit + 1) * T])
            else:
                zt = ot - DIT
                for fc in range(NFC):
                    plan.wait("scalar", eP)
                    if fc == 0:
                        plan.wait("scalar", qG)
                    plan.op("scalar", lambda b=fcbanks[fc], fc=fc: nc.scalar.activation(
                        out=gz16_t[:, fc * FC:(fc + 1) * FC], in_=b["ap"], func=AF.Silu), [(eS, 1)])
                    bk.release(fcbanks[fc], eS)
                plan.wait("gpsimd", eS)
                dma("gpsimd", qG, gz_sp[zt * 128:(zt + 1) * 128, :], gz16_t)

        plan.wait("tensor", eS)
        for ot in range(DIT):
            i, wv = stream_w(ot, W_dt, ot * 128, 128, DIT)
            for fc in range(NFC):
                b = bk.acquire()
                mm_chain(b, wstg16[i], lambda kt, fc=fc: xc16[:, kt * T + fc * FC: kt * T + fc * FC + FC],
                         DIT, FC, wv)
                if fc == NFC - 1:
                    stg_free_p[i] = eP.c
                plan.wait("scalar", eP)
                if fc == 0:
                    plan.wait("scalar", qG)
                plan.op("scalar", lambda b=b, ot=ot: nc.scalar.activation(
                    out=e_sb, in_=b["ap"], func=AF.Exp, bias=b_dt_sb[:, ot:ot + 1]), [(eS, 1)])
                bk.release(b, eS)
                plan.wait("scalar", eS)
                plan.op("scalar", lambda fc=fc: nc.scalar.activation(
                    out=dt16_t[:, fc * FC:(fc + 1) * FC], in_=e_sb, func=AF.Ln, bias=1.0), [(eS, 1)])
            plan.wait("gpsimd", eS)
            dma("gpsimd", qG, dt_sp[ot * 128:(ot + 1) * 128, :], dt16_t)

        i, wv = stream_w(0, W_x, 0, 2 * S, DIT)
        for fc in range(NFC):
            b = bk.acquire()
            mm_chain(b, wstg16[i],
                     lambda kt, fc=fc: xc16[:, kt * T + fc * FC: kt * T + fc * FC + FC],
                     DIT, FC, wv, wcol=2 * S)
            plan.wait("vector", eP)
            plan.op("vector", lambda b=b, fc=fc: nc.vector.tensor_copy(
                out=bc_sb[0:32, fc * FC:(fc + 1) * FC], in_=b["ap"][0:32, :]), [(eV, 1)])
            bk.release(b, eV)
        plan.wait("gpsimd", eV)
        dma("gpsimd", qG, bc_sp, bc_sb[0:32, :])
        bc_spilled = qG.c

        M.reset()
        eG = C["eG"]
        wo_stg = M.f32(DIT * 128)
        Wout16 = M.bf16(DMT * DIT * 128)
        Bb = M.bf16(S * TSC)
        Cb = M.bf16(S * TSC)
        dA = [M.bf16(S * TSC) for _ in range(2)]
        dtuB = [M.bf16(S * TSC) for _ in range(2)]
        hbuf = [M.bf16(S * TSC) for _ in range(2)]
        hC = [M.bf16(S * TSC) for _ in range(2)]
        fold16 = M.bf16(S)
        uc = wo_stg.bitcast(BF16)
        dtc = M.bf16(DIT * TSC)
        gzc = M.bf16(DIT * TSC)
        dtu2 = [M.bf16(TSC) for _ in range(2)]
        y_dit = M.f32(TSC)
        ygz16 = M.bf16(DIT * TSC)
        mo_t = [M.f32(TSC) for _ in range(2)]

        wo_cast_v = eV.c
        for ot in range(DMT):
            plan.wait("sync", eV, wo_cast_v)
            dma("sync", qS, wo_stg.rearrange("p (k c) -> p k c", c=128),
                _w_load_view(W_out, ot * 128, 128, DIT))
            plan.wait("vector", qS)
            plan.op("vector", lambda ot=ot: nc.vector.tensor_copy(
                out=Wout16[:, ot * DIT * 128:(ot + 1) * DIT * 128], in_=wo_stg), [(eV, 1)])
            wo_cast_v = eV.c

        def v3(ap, n=S, t=TSC):
            return ap.rearrange("p (s t) -> p s t", s=n)

        plan.wait("sync", qG)
        scans_done = {}
        carry_ready = {}
        fold_done = {}
        out_done = {}
        mo_free = [0, 0]
        for tc in range(NSC):
            c0 = tc * TSC
            plan.wait("sync", eV)
            plan.wait("sync", eS)
            dma("sync", qS, uc.rearrange("p (k t) -> p k t", t=TSC),
                xc_sp[:, c0:c0 + TSC].rearrange("(k p) t -> p k t", p=128))
            dma("sync", qS, dtc.rearrange("p (k t) -> p k t", t=TSC),
                dt_sp[:, c0:c0 + TSC].rearrange("(k p) t -> p k t", p=128))
            dma("sync", qS, gzc.rearrange("p (k t) -> p k t", t=TSC),
                gz_sp[:, c0:c0 + TSC].rearrange("(k p) t -> p k t", p=128))
            chunk_q = qS.c
            if tc == 0:
                plan.wait("gpsimd", qG, bc_spilled)
            plan.wait("gpsimd", eV)
            plan.wait("gpsimd", eG)
            dma("gpsimd", qG, v3(Bb), bc_sp[0:S, c0:c0 + TSC].unsqueeze(0).partition_broadcast(128))
            dma("gpsimd", qG, v3(Cb), bc_sp[S:2 * S, c0:c0 + TSC].unsqueeze(0).partition_broadcast(128))
            chunk_g = qG.c
            plan.wait("vector", qS, chunk_q)
            if tc == 0:
                plan.wait("vector", eV, max(aexp_done_v, carry_init_v))
                plan.wait("scalar", eV, aexp_done_v)

            hC_g = {}

            def st_prep(dit):
                pi = dit % 2
                plan.wait("scalar", qS, chunk_q)
                plan.wait("scalar", eV, scans_done.get(dit - 2, 0))
                plan.wait("scalar", eG, fold_done.get(dit - 2, 0))
                for s in range(S):
                    plan.op("scalar", lambda s=s, dit=dit, pi=pi: nc.scalar.activation(
                        out=dA[pi][:, s * TSC:(s + 1) * TSC],
                        in_=dtc[:, dit * TSC:(dit + 1) * TSC], func=AF.Exp,
                        scale=A_sb[:, dit * S + s:dit * S + s + 1]), [(eS, 1)])
                dA_s = eS.c
                plan.wait("gpsimd", eV, scans_done.get(dit - 2, 0))
                plan.wait("gpsimd", qG, chunk_g)
                plan.wait("gpsimd", qS, chunk_q)
                plan.op("gpsimd", lambda dit=dit, pi=pi: nc.gpsimd.tensor_mul(
                    dtu2[pi], dtc[:, dit * TSC:(dit + 1) * TSC],
                    uc[:, dit * TSC:(dit + 1) * TSC]), [(eG, 1)])
                plan.op("gpsimd", lambda dit=dit, pi=pi: nc.gpsimd.tensor_tensor(
                    out=v3(dtuB[pi]),
                    in0=dtu2[pi].unsqueeze(1).to_broadcast([128, S, TSC]),
                    in1=v3(Bb), op=ALU.mult), [(eG, 1)])
                plan.wait("gpsimd", eS, dA_s)
                plan.wait("gpsimd", eV, carry_ready.get(dit, 0))
                plan.op("gpsimd", lambda dit=dit, pi=pi: nc.gpsimd.tensor_mul(
                    fold16, v3(dA[pi])[:, :, 0:1].squeeze(2),
                    carry[:, dit * S:(dit + 1) * S]), [(eG, 1)])
                plan.op("gpsimd", lambda dit=dit, pi=pi: nc.gpsimd.tensor_add(
                    v3(dtuB[pi])[:, :, 0:1].squeeze(2),
                    v3(dtuB[pi])[:, :, 0:1].squeeze(2), fold16), [(eG, 1)])
                plan.op("gpsimd", lambda pi=pi: nc.gpsimd.memset(
                    v3(dA[pi])[:, :, 0:1].squeeze(2), 0.0), [(eG, 1)])
                fold_done[dit] = eG.c

            def st_hC(dit):
                pi = dit % 2
                plan.wait("gpsimd", eV, carry_ready[(tc, dit)])
                plan.wait("gpsimd", eV, out_done.get(dit - 2, 0))
                plan.op("gpsimd", lambda pi=pi: nc.gpsimd.tensor_mul(
                    hC[pi], hbuf[pi], Cb), [(eG, 1)])
                hC_g[dit] = eG.c

            def st_scan(dit):
                pi = dit % 2
                plan.wait("vector", eG, fold_done[dit])
                plan.op("vector", lambda dit=dit, pi=pi: nc.vector.tensor_tensor_scan(
                    out=hbuf[pi], data0=dA[pi], data1=dtuB[pi], initial=0.0,
                    op0=ALU.mult, op1=ALU.add), ())
                plan.op("vector", lambda dit=dit, pi=pi: nc.vector.tensor_copy(
                    out=carry[:, dit * S:(dit + 1) * S],
                    in_=v3(hbuf[pi])[:, :, TSC - 1:TSC].squeeze(2)), [(eV, 1)])
                carry_ready[dit] = eV.c
                carry_ready[(tc, dit)] = eV.c
                scans_done[dit] = eV.c

            def st_out(dit):
                pi = dit % 2
                uc_s = uc[:, dit * TSC:(dit + 1) * TSC]
                h = hC[pi]
                plan.wait("vector", eG, hC_g[dit])
                plan.op("vector", lambda h=h: nc.vector.tensor_add(
                    h[:, 0:8 * TSC], h[:, 0:8 * TSC], h[:, 8 * TSC:16 * TSC]), [(eV, 1)])
                plan.op("vector", lambda h=h: nc.vector.tensor_add(
                    h[:, 0:4 * TSC], h[:, 0:4 * TSC], h[:, 4 * TSC:8 * TSC]), [(eV, 1)])
                plan.op("vector", lambda h=h: nc.vector.tensor_add(
                    h[:, 0:2 * TSC], h[:, 0:2 * TSC], h[:, 2 * TSC:4 * TSC]), [(eV, 1)])
                plan.op("vector", lambda h=h: nc.vector.tensor_tensor(
                    out=y_dit, in0=h[:, 0:TSC], in1=h[:, TSC:2 * TSC], op=ALU.add), [(eV, 1)])
                plan.op("vector", lambda dit=dit, uc_s=uc_s: nc.vector.scalar_tensor_tensor(
                    out=y_dit, in0=uc_s, scalar=D_sb[:, dit:dit + 1],
                    in1=y_dit, op0=ALU.mult, op1=ALU.add), [(eV, 1)])
                plan.op("vector", lambda dit=dit: nc.vector.tensor_mul(
                    ygz16[:, dit * TSC:(dit + 1) * TSC], y_dit,
                    gzc[:, dit * TSC:(dit + 1) * TSC]), [(eV, 1)])
                out_done[dit] = eV.c

            for dit in range(DIT):
                st_prep(dit)
                if dit >= 1:
                    st_hC(dit - 1)
                if dit >= 2:
                    st_out(dit - 2)
                st_scan(dit)
            st_hC(DIT - 1)
            st_out(DIT - 2)
            st_out(DIT - 1)
            ygz_v = eV.c
            for ot in range(DMT):
                b = bk.acquire()
                plan.wait("tensor", eV, ygz_v)
                for kt in range(DIT):
                    plan.op("tensor", lambda b=b, ot=ot, kt=kt: nc.tensor.matmul(
                        b["ap"][:, :TSC], Wout16[:, ot * DIT * 128 + kt * 128: ot * DIT * 128 + (kt + 1) * 128],
                        ygz16[:, kt * TSC:(kt + 1) * TSC],
                        start=(kt == 0), stop=(kt == DIT - 1)), [(eP, 1)] if kt == DIT - 1 else ())
                mi = ot % 2
                plan.wait("scalar", eP)
                plan.wait("scalar", qG, mo_free[mi])
                plan.op("scalar", lambda b=b, mi=mi: nc.scalar.copy(out=mo_t[mi], in_=b["ap"][:, :TSC]), [(eS, 1)])
                bk.release(b, eS)
                plan.wait("gpsimd", eS)
                dma("gpsimd", qG, mo_sp[ot * 128:(ot + 1) * 128, c0:c0 + TSC], mo_t[mi])
                mo_free[mi] = qG.c

        M.reset()
        x1_acc = M.f32(DMT * T)
        h1reg = M.f32(8 * T // 2)
        h1buf = h1reg.bitcast(BF16)
        w1stg = [M.f32(DMT * 128) for _ in range(3)]
        w3stg = [M.f32(DMT * 128) for _ in range(3)]
        w1c16 = [M.bf16(DMT * 128) for _ in range(3)]
        w3c16 = [M.bf16(DMT * 128) for _ in range(3)]
        wgstg = M.f32(DMT * 8)
        wgc16 = M.bf16(DMT * 8)
        w_b16 = M.bf16(T)
        s1_sb = M.f32(FC)
        tmp16 = M.bf16(FC)
        gatesT = h1reg[0:8, 0:T]
        gates_tm = M.f32(128)
        w_tm = M.f32(128)
        gt3 = M.f32(64)
        eq3 = M.f32(64)
        rank8 = M.f32(8)
        kw8 = M.f32(8)
        col1 = M.f32(1)
        tri_sb = M.f32(64)
        ident_sb = M.f32(128)
        mo_rd = h1reg[:, 0:T]
        w_row = mo_rd[0:1, :]

        plan.wait("sync", qG)
        dma("sync", qS, tri_sb, tri_lt)
        dma("sync", qS, ident_sb, ident_f32)
        for dmt in range(DMT):
            plan.wait("sync", eV)
            dma("sync", qS, x1_acc[:, dmt * T:(dmt + 1) * T], xT[dmt * 128:(dmt + 1) * 128, :])
            dma("sync", qS, mo_rd, mo_sp[dmt * 128:(dmt + 1) * 128, :])
            plan.wait("vector", qS)
            plan.op("vector", lambda dmt=dmt: nc.vector.tensor_add(
                x1_acc[:, dmt * T:(dmt + 1) * T], x1_acc[:, dmt * T:(dmt + 1) * T], mo_rd), [(eV, 1)])
        x1_init_v = eV.c

        dma("sync", qS, wgstg.rearrange("p (k c) -> p k c", c=8), _w_load_view(Wg, 0, 8, DMT))
        plan.wait("vector", qS)
        plan.op("vector", lambda: nc.vector.tensor_copy(out=wgc16, in_=wgstg), [(eV, 1)])
        wg_v = eV.c
        for fc in range(NFC):
            b = bk.acquire()
            mm_chain(b, wgc16, lambda kt, fc=fc: xn16[:, kt * T + fc * FC: kt * T + fc * FC + FC],
                     DMT, FC, wg_v, wcol=8)
            plan.wait("scalar", eP)
            plan.op("scalar", lambda b=b, fc=fc: nc.scalar.activation(
                out=gatesT[0:8, fc * FC:(fc + 1) * FC], in_=b["ap"][0:8, :], func=AF.Sigmoid), [(eS, 1)])
            bk.release(b, eS)
        gates_s = eS.c
        plan.wait("tensor", eS, gates_s)
        plan.wait("tensor", qS)
        bT = bk.acquire()
        for tt in range(16):
            plan.op("tensor", lambda tt=tt, b=bT: nc.tensor.transpose(
                out=b["ap"][:, tt * 8:(tt + 1) * 8], in_=gatesT[0:8, tt * 128:(tt + 1) * 128],
                identity=ident_sb[0:8, 0:8]), [(eP, 1)] if tt == 15 else ())
        plan.wait("vector", eP)
        plan.op("vector", lambda b=bT: nc.vector.tensor_copy(out=gates_tm, in_=b["ap"][:, 0:128]), [(eV, 1)])
        bk.release(bT, eV)
        if d_gT is not None:
            plan.wait("gpsimd", eS, gates_s)
            dma("gpsimd", qG, d_gT, gatesT)
        if d_gtm is not None:
            plan.wait("gpsimd", eV)
            dma("gpsimd", qG, d_gtm, gates_tm)
        plan.wait("vector", qS)
        for tt in range(16):
            g = gates_tm[:, tt * 8:(tt + 1) * 8]
            g_i = g.unsqueeze(1).to_broadcast([128, 8, 8])
            g_j = g.unsqueeze(2).to_broadcast([128, 8, 8])
            t3 = lambda ap: ap.rearrange("p (j i) -> p j i", j=8)
            plan.op("vector", lambda g_i=g_i, g_j=g_j: nc.vector.tensor_tensor(
                out=t3(gt3), in0=g_i, in1=g_j, op=ALU.is_gt), [(eV, 1)])
            plan.op("vector", lambda g_i=g_i, g_j=g_j: nc.vector.tensor_tensor(
                out=t3(eq3), in0=g_i, in1=g_j, op=ALU.is_equal), [(eV, 1)])
            plan.op("vector", lambda: nc.vector.tensor_mul(eq3, eq3, tri_sb), [(eV, 1)])
            plan.op("vector", lambda: nc.vector.tensor_add(gt3, gt3, eq3), [(eV, 1)])
            plan.op("vector", lambda: nc.vector.tensor_reduce(
                out=rank8, in_=t3(gt3), axis=AX.X, op=ALU.add), [(eV, 1)])
            plan.op("vector", lambda: nc.vector.tensor_scalar(
                out=rank8, in0=rank8, scalar1=1.5, scalar2=None, op0=ALU.is_le), [(eV, 1)])
            plan.op("vector", lambda g=g: nc.vector.tensor_mul(kw8, rank8, g), [(eV, 1)])
            plan.op("vector", lambda: nc.vector.tensor_reduce(
                out=col1, in_=kw8, axis=AX.X, op=ALU.add), [(eV, 1)])
            plan.op("vector", lambda: nc.vector.reciprocal(col1, col1), [(eV, 1)])
            plan.op("vector", lambda: nc.vector.tensor_scalar_mul(col1, col1, 1.0 / 3.0), [(eV, 1)])
            rden_v = eV.c
            plan.wait("vector", eV, rden_v)
            plan.op("vector", lambda tt=tt: nc.vector.tensor_scalar_mul(
                w_tm[:, tt * 8:(tt + 1) * 8], kw8, col1), [(eV, 1)])
        wtm_v = eV.c
        if d_w is not None:
            plan.wait("gpsimd", eV, wtm_v)
            dma("gpsimd", qG, d_w.rearrange("(tt p) e -> p tt e", p=128),
                w_tm.rearrange("p (tt e) -> p tt e", e=8))

        stg_v = {(n_, i_): 0 for n_ in ("w1", "w3") for i_ in range(3)}
        stg_p = {(n_, i_): 0 for n_ in ("w1", "w3") for i_ in range(3)}
        stg_i = {"w1": 0, "w3": 0}

        def stage(name, stgaps, c16aps, view, n):
            i = stg_i[name] % len(stgaps)
            stg_i[name] += 1
            plan.wait("sync", eV, stg_v[(name, i)])
            dma("sync", qS, stgaps[i][:, :n].rearrange("p (k c) -> p k c", c=128), view)
            plan.wait("vector", qS)
            plan.wait("vector", eP, stg_p[(name, i)])
            plan.op("vector", lambda a=c16aps[i], b=stgaps[i], n=n: nc.vector.tensor_copy(
                out=a[:, :n], in_=b[:, :n]), [(eV, 1)])
            stg_v[(name, i)] = eV.c
            return i, eV.c

        def stage_done(name, i):
            stg_p[(name, i)] = eP.c

        for ei in range(9):
            shared = (ei == 0)
            eidx = ei - 1
            if not shared:
                plan.wait("tensor", eV, wtm_v)
                for half4 in range(4):
                    bT = bk.acquire()
                    for q in range(4):
                        tt = half4 * 4 + q
                        plan.op("tensor", lambda tt=tt, q=q, b=bT, eidx=eidx: nc.tensor.transpose(
                            out=b["ap"][0:1, q * 128:(q + 1) * 128],
                            in_=w_tm[:, tt * 8 + eidx:tt * 8 + eidx + 1],
                            identity=ident_sb), [(eP, 1)] if q == 3 else ())
                    plan.wait("scalar", eP)
                    plan.wait("scalar", eV, x1_init_v)
                    plan.op("scalar", lambda b=bT, half4=half4: nc.scalar.copy(
                        out=w_row[:, half4 * FC:(half4 + 1) * FC], in_=b["ap"][0:1, :]), [(eS, 1)])
                    bk.release(bT, eS)
                wrow_s = eS.c
                plan.wait("tensor", eS, wrow_s)
                for fc in range(NFC):
                    b = bk.acquire()
                    plan.op("tensor", lambda b=b, fc=fc: nc.tensor.matmul(
                        b["ap"], ones_row_sb, w_row[:, fc * FC:(fc + 1) * FC],
                        start=True, stop=True), [(eP, 1)])
                    plan.wait("scalar", eP)
                    plan.op("scalar", lambda b=b, fc=fc: nc.scalar.copy(
                        out=w_b16[:, fc * FC:(fc + 1) * FC], in_=b["ap"]), [(eS, 1)])
                    bk.release(b, eS)
                wb_s = eS.c
            for half in range(2):
                for fsl in range(8):
                    fs = half * 8 + fsl
                    if shared:
                        v1 = _w_load_view(Ws1, fs * 128, 128, DMT)
                        v3w = _w_load_view(Ws3, fs * 128, 128, DMT)
                    else:
                        v1 = _w_load_view(We1[eidx], fs * 128, 128, DMT)
                        v3w = _w_load_view(We3[eidx], fs * 128, 128, DMT)
                    i1, r1 = stage("w1", w1stg, w1c16, v1, DMT * 128)
                    i3, r3 = stage("w3", w3stg, w3c16, v3w, DMT * 128)
                    for fc in range(NFC):
                        ba = bk.acquire()
                        mm_chain(ba, w1c16[i1], lambda kt, fc=fc: xn16[:, kt * T + fc * FC: kt * T + fc * FC + FC],
                                 DMT, FC, r1)
                        bb = bk.acquire()
                        mm_chain(bb, w3c16[i3], lambda kt, fc=fc: xn16[:, kt * T + fc * FC: kt * T + fc * FC + FC],
                                 DMT, FC, r3)
                        plan.wait("scalar", eP)
                        plan.op("scalar", lambda ba=ba: nc.scalar.activation(
                            out=s1_sb, in_=ba["ap"], func=AF.Silu), [(eS, 1)])
                        bk.release(ba, eS)
                        s1_s = eS.c
                        plan.wait("vector", eS, s1_s)
                        dst = h1buf[:, fsl * T + fc * FC: fsl * T + fc * FC + FC]
                        if shared:
                            plan.op("vector", lambda bb=bb, dst=dst: nc.vector.scalar_tensor_tensor(
                                out=dst, in0=s1_sb, scalar=1.0 / 3.0, in1=bb["ap"],
                                op0=ALU.mult, op1=ALU.mult), [(eV, 1)])
                        else:
                            plan.op("vector", lambda bb=bb: nc.vector.tensor_mul(
                                tmp16, s1_sb, bb["ap"]), [(eV, 1)])
                            plan.wait("vector", eV)
                            plan.wait("vector", eS, wb_s)
                            plan.op("vector", lambda dst=dst, fc=fc: nc.vector.tensor_mul(
                                dst, tmp16, w_b16[:, fc * FC:(fc + 1) * FC]), [(eV, 1)])
                        bk.release(bb, eV)
                    stage_done("w1", i1)
                    stage_done("w3", i3)
                h1_v = eV.c
                for ot in range(DMT):
                    if shared:
                        v2 = Ws2[half * 1024:(half + 1) * 1024, ot * 128:(ot + 1) * 128].rearrange(
                            "(k p) c -> p k c", p=128)
                    else:
                        v2 = We2[eidx][half * 1024:(half + 1) * 1024, ot * 128:(ot + 1) * 128].rearrange(
                            "(k p) c -> p k c", p=128)
                    i2, r2 = stage("w1", w1stg, w1c16, v2, DMT * 128)
                    for tc4 in range(NFC):
                        b = bk.acquire()
                        mm_chain(b, w1c16[i2], lambda kt, tc4=tc4: h1buf[:, kt * T + tc4 * FC: kt * T + tc4 * FC + FC],
                                 8, FC, max(r2, h1_v))
                        plan.wait("vector", eP)
                        plan.wait("vector", eV, x1_init_v)
                        plan.op("vector", lambda b=b, ot=ot, tc4=tc4: nc.vector.tensor_add(
                            x1_acc[:, ot * T + tc4 * FC: ot * T + tc4 * FC + FC],
                            x1_acc[:, ot * T + tc4 * FC: ot * T + tc4 * FC + FC],
                            b["ap"]), [(eV, 1)])
                        bk.release(b, eV)
                    stage_done("w1", i2)

        if standalone:
            plan.wait("gpsimd", eV)
            for dmt in range(DMT):
                dma("gpsimd", qG, x1_out[dmt * 128:(dmt + 1) * 128, :], x1_acc[:, dmt * T:(dmt + 1) * T])

        plan.wait("gpsimd", qG)
        plan.wait("gpsimd", qS)
        if d_xn is not None:
            dma("gpsimd", qG, d_xn, xn_sp)
        if d_xc is not None:
            dma("gpsimd", qG, d_xc, xc_sp)
        if d_dt is not None:
            dma("gpsimd", qG, d_dt, dt_sp)
        if d_gz is not None:
            dma("gpsimd", qG, d_gz, gz_sp)
        if d_bc is not None:
            dma("gpsimd", qG, d_bc, bc_sp)
        if d_mo is not None:
            dma("gpsimd", qG, d_mo, mo_sp)

        plan.wait("gpsimd", qG)
        plan.wait("gpsimd", qS)

    return {"x1_acc": x1_acc}


def build_l0(dbg=()):
    nc = bass.Bass("TRN2", target_bir_lowering=False)
    with ExitStack() as ctx:
        fc = _mk_fc(nc, ctx.enter_context, maincols=43264)
        _emit_l0(fc, dbg, standalone=True)
        _run_block(fc, ctx.enter_context)
    return nc


def _emit_l1(C, dbg=(), standalone=True):
    nc = C["nc"]; plan = C["plan"]; M = C["M"]; P = C["P"]
    bk = C["bk"]; bkT = C["bkT"]
    qS, qG, qC = C["qS"], C["qG"], C["qC"]
    eP, eV, eS = C["eP"], C["eV"], C["eS"]
    dma, ccol, ccol_wait = C["dma"], C["ccol"], C["ccol_wait"]
    ones_col, ones_row = C["ones_col"], C["ones_row"]
    ident_f32, ident_b16, tri_lt = C["ident_f32"], C["ident_b16"], C["tri_lt"]
    dram = nc.dram_tensor
    M.reset(0)

    x1T = dram("x1T", [DM, T], F32, kind="ExternalInput").ap() if standalone else None
    Wq = dram("Wq", [DM, DM], F32, kind="ExternalInput").ap()
    Wk = dram("Wk", [DM, DM], F32, kind="ExternalInput").ap()
    Wv = dram("Wv", [DM, DM], F32, kind="ExternalInput").ap()
    Wo = dram("Wo", [DM, DM], F32, kind="ExternalInput").ap()
    bq_c = dram("bq_c", [128, DMT], F32, kind="ExternalInput").ap()
    bk_c = dram("bk_c", [128, DMT], F32, kind="ExternalInput").ap()
    bv_c = dram("bv_c", [128, DMT], F32, kind="ExternalInput").ap()
    bo_c = dram("bo_c", [128, DMT], F32, kind="ExternalInput").ap()
    Mmask = dram("Mmask", [128, 128], F32, kind="ExternalInput").ap()
    nw_c = dram("nw_c", [128, DMT], F32, kind="ExternalInput").ap()
    gateWT = dram("gateWT", [DM, 8], BF16, kind="ExternalInput").ap()
    gateB = dram("gateB", [8, 1], BF16, kind="ExternalInput").ap()
    m1gT = dram("m1gT", [8, DM, DM], BF16, kind="ExternalInput").ap()
    m1lT = dram("m1lT", [8, DM, DM], BF16, kind="ExternalInput").ap()
    b1g_c = dram("b1g_c", [8, 128, DMT], BF16, kind="ExternalInput").ap()
    b1l_c = dram("b1l_c", [8, 128, DMT], BF16, kind="ExternalInput").ap()
    m2T = dram("m2T", [8, DM, DM], BF16, kind="ExternalInput").ap()
    b2_c = dram("b2_c", [8, 128, DMT], BF16, kind="ExternalInput").ap()

    x2_out = dram("x2_out", [DM, T], F32, kind="ExternalOutput").ap()

    dbg_outs = {}

    def mkdbg(name, shape, dtyp=F32):
        if name in dbg:
            dbg_outs[name] = dram("dbg_" + name, shape, dtyp, kind="ExternalOutput").ap()
        return dbg_outs.get(name)

    d_xn1 = mkdbg("xn1", [DM, T], BF16)
    d_at = mkdbg("at", [DM, T])
    d_nrm = mkdbg("nrm", [DM, T], BF16)
    d_cw = mkdbg("cw", [T, 8])

    xn1_sp = dram("xn1_sp", [DM, T], BF16, kind="Internal").ap()

    if True:
        ones_col_sb = P.f32(1)
        ones_row_sb = P.f32(128, rows=1)
        ident_sb = P.f32(128)
        identb_sb = P.bf16(128)
        tri_sb = P.f32(64)
        mask_sb = P.f32(128)
        bq_sb = P.f32(DMT)
        bk_sb = P.f32(DMT)
        bv_sb = P.f32(DMT)
        bo_sb = P.f32(DMT)
        nw_sb = P.f32(DMT)
        gb_sb = P.f32(1, rows=8)
        gb16_sb = P.bf16(2, rows=8)[:, 0:1]

        dma("gpsimd", qG, ones_col_sb, ones_col)
        dma("gpsimd", qG, ones_row_sb, ones_row)
        dma("gpsimd", qG, ident_sb, ident_f32)
        dma("gpsimd", qG, identb_sb, ident_b16)
        dma("gpsimd", qG, tri_sb, tri_lt)
        dma("gpsimd", qG, mask_sb, Mmask)
        dma("gpsimd", qG, bq_sb, bq_c)
        dma("gpsimd", qG, bk_sb, bk_c)
        dma("gpsimd", qG, bv_sb, bv_c)
        dma("gpsimd", qG, bo_sb, bo_c)
        dma("gpsimd", qG, nw_sb, nw_c)
        dma("gpsimd", qG, gb16_sb, gateB)
        init_q = qG.c
        plan.wait("vector", qG, init_q)
        plan.op("vector", lambda: nc.vector.tensor_copy(out=gb_sb, in_=gb16_sb), [(eV, 1)])

        x1g = M.f32(DMT * T)
        xn1 = M.bf16(DMT * T)
        ph0 = M.mark()

        m_b = M.f32(T)
        rstd_b = M.f32(T)
        sqbuf = M.f32(T)
        m_row = m_b[0:1, :]
        rstd_row = rstd_b[0:1, :]

        if standalone:
            for dmt in range(DMT):
                dma("sync", qS, x1g[:, dmt * T:(dmt + 1) * T], x1T[dmt * 128:(dmt + 1) * 128, :])
            x_loaded = qS.c
            plan.wait("tensor", qS, x_loaded)
            plan.wait("vector", qS, x_loaded)
        else:
            x1_acc = C["x1_acc"]
            a2a_in = dram("a2a_in", [NCORES * DM, T // NCORES], F32, kind="Internal").ap()
            a2a_out = dram("a2a_out", [NCORES * DM, T // NCORES], F32, kind="Internal").ap()
            a2a_in_v = a2a_in.rearrange("(j d) b -> j d b", j=NCORES)
            a2a_out_v = a2a_out.rearrange("(l d) b -> l d b", l=NCORES)
            plan.wait("sync", eV)
            for dmt in range(DMT):
                dma("sync", qS, a2a_in_v[:, dmt * 128:(dmt + 1) * 128, :].transpose([1, 0, 2]),
                    x1_acc[:, dmt * T:(dmt + 1) * T].rearrange("p (j b) -> p j b", j=NCORES))
            plan.wait("gpsimd", qS)

            def _cc():
                with allow_ici_all_to_all():
                    return nc.gpsimd.collective_compute(
                        "AllToAll", ALU.bypass, replica_groups=[list(range(NCORES))],
                        ins=[a2a_in.opt()], outs=[a2a_out.opt()])

            plan.op("gpsimd", _cc, [(qC, 1)])
            plan.wait("gpsimd", qC)
            for dmt in range(DMT):
                dma("gpsimd", qG, x1g[:, dmt * T:(dmt + 1) * T].rearrange("p (l b) -> p l b", l=NCORES),
                    a2a_out_v[:, dmt * 128:(dmt + 1) * 128, :].transpose([1, 0, 2]))
            x_loaded = qG.c
            plan.wait("tensor", qG, x_loaded)
            plan.wait("vector", qG, x_loaded)
            plan.wait("scalar", qG, x_loaded)
        mean_banks = []
        for fc in range(NFC):
            b = bk.acquire()
            mean_banks.append(b)
            for dmt in range(DMT):
                plan.op("tensor", lambda b=b, dmt=dmt, fc=fc: nc.tensor.matmul(
                    b["ap"][0:1, :], ones_col_sb, x1g[:, dmt * T + fc * FC: dmt * T + fc * FC + FC],
                    start=(dmt == 0), stop=(dmt == DMT - 1)),
                    [(eP, 1)] if dmt == DMT - 1 else ())
        for fc in range(NFC):
            plan.wait("vector", eP)
            plan.op("vector", lambda fc=fc, b=mean_banks[fc]: nc.vector.tensor_scalar_mul(
                m_row[:, fc * FC:(fc + 1) * FC], b["ap"][0:1, :], 1.0 / DM), [(eV, 1)])
            bk.release(mean_banks[fc], eV)
        m_row_v = eV.c
        plan.wait("tensor", eV, m_row_v)
        for fc in range(NFC):
            b = bk.acquire()
            plan.op("tensor", lambda b=b, fc=fc: nc.tensor.matmul(
                b["ap"], ones_row_sb, m_row[:, fc * FC:(fc + 1) * FC], start=True, stop=True), [(eP, 1)])
            plan.wait("vector", eP)
            plan.op("vector", lambda b=b, fc=fc: nc.vector.tensor_copy(
                out=m_b[:, fc * FC:(fc + 1) * FC], in_=b["ap"]), [(eV, 1)])
            bk.release(b, eV)
        mb_v = eV.c
        var_banks = [bk.acquire() for _ in range(NFC)]
        for dmt in range(DMT):
            plan.wait("vector", eV, mb_v)
            if dmt > 0:
                plan.wait("vector", eP)
            plan.op("vector", lambda dmt=dmt: nc.vector.tensor_sub(
                sqbuf, x1g[:, dmt * T:(dmt + 1) * T], m_b), [(eV, 1)])
            plan.wait("scalar", eV)
            plan.op("scalar", lambda: nc.scalar.activation(
                out=sqbuf, in_=sqbuf, func=AF.Square), [(eS, 1)])
            plan.wait("tensor", eS)
            for fc in range(NFC):
                plan.op("tensor", lambda b=var_banks[fc], fc=fc, dmt=dmt: nc.tensor.matmul(
                    b["ap"][0:1, :], ones_col_sb, sqbuf[:, fc * FC:(fc + 1) * FC],
                    start=(dmt == 0), stop=(dmt == DMT - 1)), [(eP, 1)])
        c_invdm = ccol(1.0 / DM)
        c_eps = ccol(1e-5)
        c_nhalf = ccol(-0.5)
        plan.wait("scalar", eV)
        plan.wait("scalar", eP)
        for fc in range(NFC):
            plan.op("scalar", lambda b=var_banks[fc], fc=fc: nc.scalar.activation(
                out=rstd_row[:, fc * FC:(fc + 1) * FC], in_=b["ap"][0:1, :], func=AF.Ln,
                scale=c_invdm[0:1, :], bias=c_eps[0:1, :]), [(eS, 1)])
        plan.op("scalar", lambda: nc.scalar.activation(
            out=rstd_row, in_=rstd_row, func=AF.Exp, scale=c_nhalf[0:1, :]), [(eS, 1)])
        rstd_s = eS.c
        for fc in range(NFC):
            bk.release(var_banks[fc], eS)
        plan.wait("tensor", eS, rstd_s)
        for fc in range(NFC):
            b = bk.acquire()
            plan.op("tensor", lambda b=b, fc=fc: nc.tensor.matmul(
                b["ap"], ones_row_sb, rstd_row[:, fc * FC:(fc + 1) * FC], start=True, stop=True), [(eP, 1)])
            plan.wait("vector", eP)
            plan.op("vector", lambda b=b, fc=fc: nc.vector.tensor_copy(
                out=rstd_b[:, fc * FC:(fc + 1) * FC], in_=b["ap"]), [(eV, 1)])
            bk.release(b, eV)
        for dmt in range(DMT):
            plan.op("vector", lambda dmt=dmt: nc.vector.tensor_sub(
                sqbuf, x1g[:, dmt * T:(dmt + 1) * T], m_b), [(eV, 1)])
            plan.op("vector", lambda dmt=dmt: nc.vector.tensor_mul(
                xn1[:, dmt * T:(dmt + 1) * T], sqbuf, rstd_b), [(eV, 1)])
            plan.wait("gpsimd", eV)
            dma("gpsimd", qG, xn1_sp[dmt * 128:(dmt + 1) * 128, :], xn1[:, dmt * T:(dmt + 1) * T])
        xn1_v = eV.c
        if d_xn1 is not None:
            plan.wait("gpsimd", qG)
            dma("gpsimd", qG, d_xn1, xn1_sp)

        M.reset(ph0)
        wstgf = M.f32(DMT * 128)
        wstg16 = M.bf16(DMT * 128)
        q16 = M.bf16(T)
        k16 = M.bf16(T)
        v16 = M.bf16(T)
        atn16 = M.bf16(T)
        sf_all = M.f32(T)
        mx16 = M.f32(16)
        pt_all = M.bf16(T)
        ptT_all = M.bf16(T)
        vt_all = M.bf16(1024)

        c_scale = ccol(0.125)

        def bmaj_out(ap, fc):
            return ap.rearrange("p (b l) -> p l b", l=8)[:, 2 * fc:2 * fc + 2, :]

        stw_v, stw_p = xn1_v, 0
        for hg in range(DMT):
            outs = []
            for (wap, bias_sb, dst) in ((Wq, bq_sb, q16), (Wk, bk_sb, k16), (Wv, bv_sb, v16)):
                plan.wait("sync", eV, stw_v)
                dma("sync", qS, wstgf.rearrange("p (k c) -> p k c", c=128),
                    _w_load_view(wap, hg * 128, 128, DMT))
                plan.wait("vector", qS)
                plan.wait("vector", eP, stw_p)
                plan.op("vector", lambda: nc.vector.tensor_copy(out=wstg16, in_=wstgf), [(eV, 1)])
                stw_v = eV.c
                plan.wait("tensor", eV, stw_v)
                plan.wait("tensor", eV, xn1_v)
                for fc in range(NFC):
                    b = bk.acquire()
                    for kt in range(DMT):
                        plan.op("tensor", lambda b=b, kt=kt, fc=fc: nc.tensor.matmul(
                            b["ap"], wstg16[:, kt * 128:(kt + 1) * 128],
                            xn1[:, kt * T + fc * FC: kt * T + fc * FC + FC],
                            start=(kt == 0), stop=(kt == DMT - 1)), [(eP, 1)] if kt == DMT - 1 else ())
                    plan.wait("scalar", eP)
                    plan.op("scalar", lambda b=b, dst=dst, fc=fc, bias_sb=bias_sb, hg=hg: nc.scalar.activation(
                        out=bmaj_out(dst, fc), in_=b["ap"], func=AF.Identity,
                        bias=bias_sb[:, hg:hg + 1]), [(eS, 1)])
                    bk.release(b, eS)
                stw_p = eP.c
            qkv_s = eS.c
            for h2 in range(2):
                lo, hi = h2 * 64, h2 * 64 + 64
                idnt = identb_sb[lo:hi, 0:64] if lo == 0 else identb_sb[lo:hi, 64:128]
                sbanks = []
                for b4 in range(4):
                    b = bk.acquire()
                    sbanks.append(b)
                    plan.wait("tensor", eS, qkv_s)
                    for q4 in range(4):
                        qt = b4 * 4 + q4
                        plan.op("tensor", lambda b=b, lo=lo, hi=hi, qt=qt, q4=q4: nc.tensor.matmul(
                            b["ap"][:, q4 * 128:(q4 + 1) * 128], q16[lo:hi, qt * 128:(qt + 1) * 128],
                            k16[lo:hi, qt * 128:(qt + 1) * 128], start=True, stop=True),
                            [(eP, 1)] if q4 == 3 else ())
                plan.wait("vector", eP)
                plan.wait("vector", qG, init_q)
                for b4 in range(4):
                    plan.op("vector", lambda b=sbanks[b4], b4=b4: nc.vector.tensor_tensor(
                        out=sf_all.rearrange("p (q c) -> p q c", c=128)[:, b4 * 4:(b4 + 1) * 4, :],
                        in0=b["ap"].rearrange("p (q c) -> p q c", c=128),
                        in1=mask_sb.unsqueeze(1).to_broadcast([128, 4, 128]),
                        op=ALU.add), [(eV, 1)])
                    bk.release(sbanks[b4], eV)
                plan.op("vector", lambda: nc.vector.tensor_reduce(
                    out=mx16, in_=sf_all.rearrange("p (q c) -> p q c", c=128),
                    axis=AX.X, op=ALU.max), [(eV, 1)])
                plan.op("vector", lambda: nc.vector.tensor_tensor(
                    out=sf_all.rearrange("p (q c) -> p q c", c=128),
                    in0=sf_all.rearrange("p (q c) -> p q c", c=128),
                    in1=mx16.unsqueeze(2).to_broadcast([128, 16, 128]),
                    op=ALU.subtract), [(eV, 1)])
                sub_v = eV.c
                plan.wait("scalar", eV, sub_v)
                plan.op("scalar", lambda: nc.scalar.activation(
                    out=sf_all, in_=sf_all, func=AF.Exp, scale=c_scale[:, :]), [(eS, 1)])
                exp_s = eS.c
                plan.wait("vector", eS, exp_s)
                plan.op("vector", lambda: nc.vector.tensor_reduce(
                    out=mx16, in_=sf_all.rearrange("p (q c) -> p q c", c=128),
                    axis=AX.X, op=ALU.add), [(eV, 1)])
                plan.op("vector", lambda: nc.vector.reciprocal(mx16, mx16), [(eV, 1)])
                plan.op("vector", lambda: nc.vector.tensor_tensor(
                    out=pt_all.rearrange("p (q c) -> p q c", c=128),
                    in0=sf_all.rearrange("p (q c) -> p q c", c=128),
                    in1=mx16.unsqueeze(2).to_broadcast([128, 16, 128]),
                    op=ALU.mult), [(eV, 1)])
                p_v = eV.c
                for half8 in range(2):
                    bT = bkT.acquire()
                    plan.wait("tensor", eV, p_v)
                    for t8 in range(8):
                        qt = half8 * 8 + t8
                        plan.op("tensor", lambda bT=bT, qt=qt, t8=t8: nc.tensor.transpose(
                            out=bT["ap"][:, t8 * 128:(t8 + 1) * 128],
                            in_=pt_all[:, qt * 128:(qt + 1) * 128], identity=identb_sb),
                            [(eP, 1)] if t8 == 7 else ())
                    plan.wait("vector", eP)
                    plan.op("vector", lambda bT=bT, half8=half8: nc.vector.tensor_copy(
                        out=ptT_all[:, half8 * 1024:(half8 + 1) * 1024], in_=bT["ap"][:, 0:1024]), [(eV, 1)])
                    bkT.release(bT, eV)
                bT = bkT.acquire()
                plan.wait("tensor", eS, qkv_s)
                for qt in range(16):
                    plan.op("tensor", lambda bT=bT, lo=lo, hi=hi, qt=qt, idnt=idnt: nc.tensor.transpose(
                        out=bT["ap"][:, qt * 64:(qt + 1) * 64],
                        in_=v16[lo:hi, qt * 128:(qt + 1) * 128], identity=idnt),
                        [(eP, 1)] if qt == 15 else ())
                plan.wait("vector", eP)
                plan.op("vector", lambda bT=bT: nc.vector.tensor_copy(
                    out=vt_all, in_=bT["ap"][:, 0:1024]), [(eV, 1)])
                bkT.release(bT, eV)
                pv_v = eV.c
                for b4 in range(4):
                    b = bk.acquire()
                    plan.wait("tensor", eV, pv_v)
                    for q4 in range(4):
                        qt = b4 * 4 + q4
                        plan.op("tensor", lambda b=b, qt=qt, q4=q4: nc.tensor.matmul(
                            b["ap"][0:64, q4 * 128:(q4 + 1) * 128], vt_all[:, qt * 64:(qt + 1) * 64],
                            ptT_all[:, qt * 128:(qt + 1) * 128], start=True, stop=True),
                            [(eP, 1)] if q4 == 3 else ())
                    plan.wait("scalar", eP)
                    plan.op("scalar", lambda b=b, lo=lo, hi=hi, b4=b4: nc.scalar.copy(
                        out=atn16[lo:hi, b4 * 512:(b4 + 1) * 512], in_=b["ap"][0:64, 0:512]), [(eS, 1)])
                    bk.release(b, eS)
            atn_s = eS.c
            plan.wait("sync", eV, stw_v)
            dma("sync", qS, wstgf.rearrange("p (k c) -> p k c", c=128),
                Wo[hg * 128:(hg + 1) * 128, :].rearrange("p (k c) -> p k c", c=128))
            plan.wait("vector", qS)
            plan.wait("vector", eP, stw_p)
            plan.op("vector", lambda: nc.vector.tensor_copy(out=wstg16, in_=wstgf), [(eV, 1)])
            stw_v = eV.c
            plan.wait("tensor", eV, stw_v)
            plan.wait("tensor", eS, atn_s)
            for ot in range(DMT):
                for fc in range(NFC):
                    b = bk.acquire()
                    plan.op("tensor", lambda b=b, ot=ot, fc=fc: nc.tensor.matmul(
                        b["ap"], wstg16[:, ot * 128:(ot + 1) * 128],
                        atn16[:, fc * FC:(fc + 1) * FC], start=True, stop=True), [(eP, 1)])
                    plan.wait("vector", eP)

                    def x1gv(ot=ot, fc=fc):
                        return x1g[:, ot * T:(ot + 1) * T].rearrange(
                            "p (l b) -> p b l", l=8)[:, 64 * fc:64 * (fc + 1), :]

                    plan.op("vector", lambda b=b, x1gv=x1gv: nc.vector.tensor_tensor(
                        out=x1gv(), in0=x1gv(),
                        in1=b["ap"].rearrange("p (b l) -> p b l", l=8),
                        op=ALU.add), [(eV, 1)])
                    bk.release(b, eV)
            stw_p = eP.c
        for dmt in range(DMT):
            plan.op("vector", lambda dmt=dmt: nc.vector.tensor_scalar_add(
                x1g[:, dmt * T:(dmt + 1) * T], x1g[:, dmt * T:(dmt + 1) * T],
                bo_sb[:, dmt:dmt + 1]), [(eV, 1)])
        attn_done_v = eV.c
        if d_at is not None:
            plan.wait("gpsimd", eV, attn_done_v)
            for dmt in range(DMT):
                dma("gpsimd", qG, d_at[dmt * 128:(dmt + 1) * 128, :], x1g[:, dmt * T:(dmt + 1) * T])

        M.reset(ph0)
        sqb = M.f32(T)
        rms_b = M.f32(T)
        rms_row = rms_b[0:1, :]
        w1g_16 = M.bf16(DMT * 128)
        w1l_16 = M.bf16(DMT * 128)
        w2_16 = M.bf16(DMT * 128)
        pg_all = M.f32(T)
        pl_all = M.f32(T)
        sg_all = M.f32(T)
        actreg = M.f32(T)
        act16 = [actreg.bitcast(BF16), M.bf16(2 * T)]
        glT = actreg[0:8, :]
        normed = xn1
        cb16 = rms_b[:, 0:T // 2].bitcast(BF16)
        wg16 = rms_b[:, T // 2:T // 2 + DMT * 4].bitcast(BF16)
        b1g_sb = P.f32(DMT)
        b1l_sb = P.f32(DMT)
        b2_sb = P.f32(DMT)
        b1g16 = P.bf16(DMT)
        b1l16 = P.bf16(DMT)
        b216 = P.bf16(DMT)
        tmp16g = M.bf16(FC, rows=8)
        g_tm = M.f32(128)
        c_tm = M.f32(128)
        gt3 = M.f32(64)
        eq3 = M.f32(64)
        rank8 = M.f32(8)
        kw8 = M.f32(8)
        col1 = M.f32(1)
        colv1 = M.f32(1)
        w_row = sqb[0:1, :]

        w1p_st = {"p": stw_p}
        stw_p_st = {"p": stw_p}
        var_banks = [bk.acquire() for _ in range(NFC)]
        for dmt in range(DMT):
            plan.wait("scalar", eP)
            plan.wait("scalar", eV, xn1_v)
            plan.op("scalar", lambda dmt=dmt: nc.scalar.activation(
                out=sqb, in_=xn1[:, dmt * T:(dmt + 1) * T], func=AF.Square), [(eS, 1)])
            plan.wait("tensor", eS)
            for fc in range(NFC):
                plan.op("tensor", lambda b=var_banks[fc], fc=fc, dmt=dmt: nc.tensor.matmul(
                    b["ap"][0:1, :], ones_col_sb, sqb[:, fc * FC:(fc + 1) * FC],
                    start=(dmt == 0), stop=(dmt == DMT - 1)), [(eP, 1)])
        plan.wait("scalar", eP)
        for fc in range(NFC):
            plan.op("scalar", lambda b=var_banks[fc], fc=fc: nc.scalar.activation(
                out=rms_row[:, fc * FC:(fc + 1) * FC], in_=b["ap"][0:1, :], func=AF.Ln,
                scale=c_invdm[0:1, :], bias=c_eps[0:1, :]), [(eS, 1)])
        plan.op("scalar", lambda: nc.scalar.activation(
            out=rms_row, in_=rms_row, func=AF.Exp, scale=c_nhalf[0:1, :]), [(eS, 1)])
        rms_s = eS.c
        for fc in range(NFC):
            bk.release(var_banks[fc], eS)
        plan.wait("tensor", eS, rms_s)
        for fc in range(NFC):
            b = bk.acquire()
            plan.op("tensor", lambda b=b, fc=fc: nc.tensor.matmul(
                b["ap"], ones_row_sb, rms_row[:, fc * FC:(fc + 1) * FC], start=True, stop=True), [(eP, 1)])
            plan.wait("vector", eP)
            plan.op("vector", lambda b=b, fc=fc: nc.vector.tensor_copy(
                out=rms_b[:, fc * FC:(fc + 1) * FC], in_=b["ap"]), [(eV, 1)])
            bk.release(b, eV)
        plan.wait("vector", qG, init_q)
        for dmt in range(DMT):
            plan.op("vector", lambda dmt=dmt: nc.vector.tensor_add(
                x1g[:, dmt * T:(dmt + 1) * T], x1g[:, dmt * T:(dmt + 1) * T],
                xn1[:, dmt * T:(dmt + 1) * T]), [(eV, 1)])
            plan.op("vector", lambda dmt=dmt: nc.vector.tensor_mul(
                sqb, xn1[:, dmt * T:(dmt + 1) * T], rms_b), [(eV, 1)])
            plan.op("vector", lambda dmt=dmt: nc.vector.tensor_scalar_mul(
                normed[:, dmt * T:(dmt + 1) * T], sqb, nw_sb[:, dmt:dmt + 1]), [(eV, 1)])
        normed_v = eV.c
        if d_nrm is not None:
            plan.wait("gpsimd", eV, normed_v)
            for dmt in range(DMT):
                dma("gpsimd", qG, d_nrm[dmt * 128:(dmt + 1) * 128, :], normed[:, dmt * T:(dmt + 1) * T])

        plan.wait("sync", eV, normed_v)
        dma("sync", qS, wg16.rearrange("p (k c) -> p k c", c=8),
            gateWT.rearrange("(k p) c -> p k c", p=128))
        wg_q = qS.c
        plan.wait("tensor", qS, wg_q)
        plan.wait("tensor", eV, normed_v)
        for fc in range(NFC):
            b = bk.acquire()
            for kt in range(DMT):
                plan.op("tensor", lambda b=b, kt=kt, fc=fc: nc.tensor.matmul(
                    b["ap"][0:8, :], wg16[:, kt * 8:(kt + 1) * 8],
                    normed[:, kt * T + fc * FC: kt * T + fc * FC + FC],
                    start=(kt == 0), stop=(kt == DMT - 1)), [(eP, 1)] if kt == DMT - 1 else ())
            plan.wait("scalar", eP)
            plan.op("scalar", lambda b=b: nc.scalar.activation(
                out=tmp16g, in_=b["ap"][0:8, :], func=AF.Identity, bias=gb_sb), [(eS, 1)])
            bk.release(b, eS)
            plan.op("scalar", lambda fc=fc: nc.scalar.copy(
                out=glT[:, fc * FC:(fc + 1) * FC], in_=tmp16g), [(eS, 1)])
        logits_s = eS.c

        plan.wait("tensor", eS, logits_s)
        bT = bk.acquire()
        for tt in range(16):
            plan.op("tensor", lambda tt=tt, b=bT: nc.tensor.transpose(
                out=b["ap"][:, tt * 8:(tt + 1) * 8], in_=glT[0:8, tt * 128:(tt + 1) * 128],
                identity=ident_sb[0:8, 0:8]), [(eP, 1)] if tt == 15 else ())
        plan.wait("vector", eP)
        plan.op("vector", lambda b=bT: nc.vector.tensor_copy(out=g_tm, in_=b["ap"][:, 0:128]), [(eV, 1)])
        bk.release(bT, eV)
        for tt in range(16):
            g = g_tm[:, tt * 8:(tt + 1) * 8]
            g_i = g.unsqueeze(1).to_broadcast([128, 8, 8])
            g_j = g.unsqueeze(2).to_broadcast([128, 8, 8])

            def t3(ap):
                return ap.rearrange("p (j i) -> p j i", j=8)

            plan.op("vector", lambda g_i=g_i, g_j=g_j: nc.vector.tensor_tensor(
                out=t3(gt3), in0=g_i, in1=g_j, op=ALU.is_gt), [(eV, 1)])
            plan.op("vector", lambda g_i=g_i, g_j=g_j: nc.vector.tensor_tensor(
                out=t3(eq3), in0=g_i, in1=g_j, op=ALU.is_equal), [(eV, 1)])
            plan.op("vector", lambda: nc.vector.tensor_mul(eq3, eq3, tri_sb), [(eV, 1)])
            plan.op("vector", lambda: nc.vector.tensor_add(gt3, gt3, eq3), [(eV, 1)])
            plan.op("vector", lambda: nc.vector.tensor_reduce(
                out=rank8, in_=t3(gt3), axis=AX.X, op=ALU.add), [(eV, 1)])
            plan.op("vector", lambda: nc.vector.tensor_scalar(
                out=rank8, in0=rank8, scalar1=1.5, scalar2=None, op0=ALU.is_le), [(eV, 1)])
            plan.op("vector", lambda g=g: nc.vector.tensor_reduce(
                out=colv1, in_=g, axis=AX.X, op=ALU.max), [(eV, 1)])
            plan.op("vector", lambda: nc.vector.tensor_scalar_mul(colv1, colv1, -1.0), [(eV, 1)])
            plan.wait("scalar", eV)
            plan.op("scalar", lambda g=g: nc.scalar.activation(
                out=kw8, in_=g, func=AF.Exp, bias=colv1), [(eS, 1)])
            plan.wait("vector", eS)
            plan.op("vector", lambda: nc.vector.tensor_mul(kw8, kw8, rank8), [(eV, 1)])
            plan.op("vector", lambda: nc.vector.tensor_reduce(
                out=col1, in_=kw8, axis=AX.X, op=ALU.add), [(eV, 1)])
            plan.op("vector", lambda: nc.vector.reciprocal(col1, col1), [(eV, 1)])
            plan.op("vector", lambda tt=tt: nc.vector.tensor_scalar_mul(
                c_tm[:, tt * 8:(tt + 1) * 8], kw8, col1), [(eV, 1)])
        ctm_v = eV.c
        if d_cw is not None:
            plan.wait("gpsimd", eV, ctm_v)
            dma("gpsimd", qG, d_cw.rearrange("(tt p) e -> p tt e", p=128),
                c_tm.rearrange("p (tt e) -> p tt e", e=8))

        w1_p = stw_p
        for eidx in range(8):
            plan.wait("tensor", eV, ctm_v)
            plan.wait("scalar", eV)
            for half4 in range(4):
                bT = bk.acquire()
                for q in range(4):
                    tt = half4 * 4 + q
                    plan.op("tensor", lambda tt=tt, q=q, b=bT, eidx=eidx: nc.tensor.transpose(
                        out=b["ap"][0:1, q * 128:(q + 1) * 128],
                        in_=c_tm[:, tt * 8 + eidx:tt * 8 + eidx + 1],
                        identity=ident_sb), [(eP, 1)] if q == 3 else ())
                plan.wait("scalar", eP)
                plan.op("scalar", lambda b=bT, half4=half4: nc.scalar.copy(
                    out=w_row[:, half4 * FC:(half4 + 1) * FC], in_=b["ap"][0:1, :]), [(eS, 1)])
                bk.release(bT, eS)
            wrow_s = eS.c
            plan.wait("tensor", eS, wrow_s)
            for fc in range(NFC):
                b = bk.acquire()
                plan.op("tensor", lambda b=b, fc=fc: nc.tensor.matmul(
                    b["ap"], ones_row_sb, w_row[0:1, fc * FC:(fc + 1) * FC],
                    start=True, stop=True), [(eP, 1)])
                plan.wait("scalar", eP)
                plan.op("scalar", lambda b=b, fc=fc: nc.scalar.copy(
                    out=cb16[:, fc * FC:(fc + 1) * FC], in_=b["ap"]), [(eS, 1)])
                bk.release(b, eS)
            cb_s = eS.c
            plan.wait("gpsimd", eV)
            dma("gpsimd", qG, b1g16, b1g_c[eidx])
            dma("gpsimd", qG, b1l16, b1l_c[eidx])
            dma("gpsimd", qG, b216, b2_c[eidx])
            bias_q = qG.c
            plan.wait("vector", qG, bias_q)
            plan.op("vector", lambda: nc.vector.tensor_copy(out=b1g_sb, in_=b1g16), [(eV, 1)])
            plan.op("vector", lambda: nc.vector.tensor_copy(out=b1l_sb, in_=b1l16), [(eV, 1)])
            plan.op("vector", lambda: nc.vector.tensor_copy(out=b2_sb, in_=b216), [(eV, 1)])
            bias_v = eV.c
            def ffn_mlp1(quarter):
                ab = act16[quarter % 2]
                for il in range(2):
                    it = quarter * 2 + il
                    plan.wait("sync", eP, w1p_st["p"])
                    dma("sync", qS, w1g_16.rearrange("p (k c) -> p k c", c=128),
                        m1gT[eidx][:, it * 128:(it + 1) * 128].rearrange("(k p) c -> p k c", p=128))
                    dma("sync", qS, w1l_16.rearrange("p (k c) -> p k c", c=128),
                        m1lT[eidx][:, it * 128:(it + 1) * 128].rearrange("(k p) c -> p k c", p=128))
                    w1_q = qS.c
                    plan.wait("tensor", qS, w1_q)
                    plan.wait("tensor", eV, normed_v)
                    c_alpha = ccol(1.702)
                    plan.wait("scalar", eV)
                    for fc in range(NFC):
                        bg = bk.acquire()
                        for kt in range(DMT):
                            plan.op("tensor", lambda b=bg, kt=kt, fc=fc: nc.tensor.matmul(
                                b["ap"], w1g_16[:, kt * 128:(kt + 1) * 128],
                                normed[:, kt * T + fc * FC: kt * T + fc * FC + FC],
                                start=(kt == 0), stop=(kt == DMT - 1)), [(eP, 1)] if kt == DMT - 1 else ())
                        bl = bk.acquire()
                        for kt in range(DMT):
                            plan.op("tensor", lambda b=bl, kt=kt, fc=fc: nc.tensor.matmul(
                                b["ap"], w1l_16[:, kt * 128:(kt + 1) * 128],
                                normed[:, kt * T + fc * FC: kt * T + fc * FC + FC],
                                start=(kt == 0), stop=(kt == DMT - 1)), [(eP, 1)] if kt == DMT - 1 else ())
                        plan.wait("scalar", eP)
                        plan.op("scalar", lambda b=bg, fc=fc: nc.scalar.copy(
                            out=pg_all[:, fc * FC:(fc + 1) * FC], in_=b["ap"]), [(eS, 1)])
                        bk.release(bg, eS)
                        plan.op("scalar", lambda b=bl, fc=fc: nc.scalar.copy(
                            out=pl_all[:, fc * FC:(fc + 1) * FC], in_=b["ap"]), [(eS, 1)])
                        bk.release(bl, eS)
                    w1p_st["p"] = eP.c
                    evac_s = eS.c
                    plan.wait("vector", eS, evac_s)
                    plan.wait("vector", eV, bias_v)
                    plan.op("vector", lambda it=it: nc.vector.tensor_scalar(
                        out=pg_all, in0=pg_all, scalar1=b1g_sb[:, it:it + 1], scalar2=7.0,
                        op0=ALU.add, op1=ALU.min), [(eV, 1)])
                    plan.op("vector", lambda it=it: nc.vector.tensor_scalar(
                        out=pl_all, in0=pl_all, scalar1=b1l_sb[:, it:it + 1], scalar2=7.0,
                        op0=ALU.add, op1=ALU.min), [(eV, 1)])
                    plan.op("vector", lambda: nc.vector.tensor_scalar(
                        out=pl_all, in0=pl_all, scalar1=-7.0, scalar2=1.0,
                        op0=ALU.max, op1=ALU.add), [(eV, 1)])
                    g_v = eV.c
                    plan.wait("scalar", eV, g_v)
                    plan.op("scalar", lambda: nc.scalar.activation(
                        out=sg_all, in_=pg_all, func=AF.Sigmoid, scale=c_alpha[:, :]), [(eS, 1)])
                    plan.wait("vector", eS)
                    plan.op("vector", lambda: nc.vector.tensor_mul(pg_all, pg_all, sg_all), [(eV, 1)])
                    plan.op("vector", lambda il=il, ab=ab: nc.vector.tensor_mul(
                        ab[:, il * T:(il + 1) * T], pg_all, pl_all), [(eV, 1)])
                for il in range(2):
                    plan.wait("vector", eS, cb_s)
                    plan.op("vector", lambda il=il, ab=ab: nc.vector.tensor_mul(
                        ab[:, il * T:(il + 1) * T], ab[:, il * T:(il + 1) * T], cb16), [(eV, 1)])
                return eV.c

            def ffn_p2(quarter, actc_v):
                ab = act16[quarter % 2]
                for ot in range(DMT):
                    plan.wait("sync", eP, stw_p_st["p"])
                    dma("sync", qS, w2_16[:, 0:2 * 128].rearrange("p (k c) -> p k c", c=128),
                        m2T[eidx][quarter * 256:(quarter + 1) * 256, ot * 128:(ot + 1) * 128].rearrange(
                            "(k p) c -> p k c", p=128))
                    w2_q = qS.c
                    plan.wait("tensor", qS, w2_q)
                    plan.wait("tensor", eV, actc_v)
                    for fc in range(NFC):
                        b = bk.acquire()
                        for kt in range(2):
                            plan.op("tensor", lambda b=b, kt=kt, fc=fc, ab=ab: nc.tensor.matmul(
                                b["ap"], w2_16[:, kt * 128:(kt + 1) * 128],
                                ab[:, kt * T + fc * FC: kt * T + fc * FC + FC],
                                start=(kt == 0), stop=(kt == 1)), [(eP, 1)] if kt == 1 else ())
                        plan.wait("vector", eP)
                        plan.op("vector", lambda b=b, ot=ot, fc=fc: nc.vector.tensor_add(
                            x1g[:, ot * T + fc * FC: ot * T + fc * FC + FC],
                            x1g[:, ot * T + fc * FC: ot * T + fc * FC + FC], b["ap"]), [(eV, 1)])
                        bk.release(b, eV)
                    stw_p_st["p"] = eP.c

            prev = None
            for quarter in range(4):
                av = ffn_mlp1(quarter)
                if prev is not None:
                    ffn_p2(prev[0], prev[1])
                prev = (quarter, av)
            ffn_p2(prev[0], prev[1])
            for ot in range(DMT):
                plan.wait("vector", eS, cb_s)
                plan.op("vector", lambda ot=ot: nc.vector.scalar_tensor_tensor(
                    out=x1g[:, ot * T:(ot + 1) * T], in0=cb16,
                    scalar=b2_sb[:, ot:ot + 1], in1=x1g[:, ot * T:(ot + 1) * T],
                    op0=ALU.mult, op1=ALU.add), [(eV, 1)])

        plan.wait("gpsimd", eV)
        for dmt in range(DMT):
            dma("gpsimd", qG, x2_out[dmt * 128:(dmt + 1) * 128, :], x1g[:, dmt * T:(dmt + 1) * T])

        plan.wait("gpsimd", qG)
        plan.wait("gpsimd", qS)

    return {}


def build_l1(dbg=()):
    nc = bass.Bass("TRN2", target_bir_lowering=False)
    with ExitStack() as ctx:
        fc = _mk_fc(nc, ctx.enter_context, maincols=43264)
        _emit_l1(fc, dbg, standalone=True)
        _run_block(fc, ctx.enter_context)
    return nc


def build_fused(dbg=()):
    nc = bass.Bass("TRN2", target_bir_lowering=False)
    with ExitStack() as ctx:
        fc = _mk_fc(nc, ctx.enter_context, maincols=43264)
        r0 = _emit_l0(fc, dbg=(), standalone=False)
        fc["x1_acc"] = r0["x1_acc"]
        _emit_l1(fc, dbg=dbg, standalone=False)
        _run_block(fc, ctx.enter_context)
    return nc


def _col_layout(v, ntiles):
    return np.ascontiguousarray(np.asarray(v, dtype=np.float32).reshape(ntiles, 128).T)


def prep_l0_inputs(inputs):
    shared = {
        "W_in": np.ascontiguousarray(inputs["L0_attn_W_in"]),
        "conv_w": np.ascontiguousarray(inputs["L0_attn_conv_w"]),
        "conv_bc": _col_layout(inputs["L0_attn_conv_b"], DIT),
        "W_x": np.ascontiguousarray(inputs["L0_attn_W_x"]),
        "W_dt": np.ascontiguousarray(inputs["L0_attn_W_dt"]),
        "b_dtc": _col_layout(inputs["L0_attn_b_dt"], DIT),
        "A_log": np.ascontiguousarray(inputs["L0_attn_A_log"]),
        "D_c": _col_layout(inputs["L0_attn_D"], DIT),
        "W_out": np.ascontiguousarray(inputs["L0_attn_W_out"]),
        "Wg": np.ascontiguousarray(inputs["L0_ffn_Wg"]),
        "Ws1": np.ascontiguousarray(inputs["L0_ffn_Ws1"]),
        "Ws3": np.ascontiguousarray(inputs["L0_ffn_Ws3"]),
        "Ws2": np.ascontiguousarray(inputs["L0_ffn_Ws2"]),
        "We1": np.ascontiguousarray(inputs["L0_ffn_We1"]),
        "We3": np.ascontiguousarray(inputs["L0_ffn_We3"]),
        "We2": np.ascontiguousarray(inputs["L0_ffn_We2"]),
        "ones_col": np.ones((128, 1), np.float32),
        "ones_row": np.ones((1, 128), np.float32),
        "ident_f32": np.eye(128, dtype=np.float32),
        "ident_b16": np.eye(128, dtype=BF),
        "tri_lt": np.broadcast_to(np.tril(np.ones((8, 8), np.float32), -1).reshape(64), (128, 64)).copy(),
    }
    x = np.asarray(inputs["L0_attn_x"])
    maps = []
    for i in range(NCORES):
        m = dict(shared)
        m["xT"] = np.ascontiguousarray(x[i].T)
        maps.append(m)
    return maps


def prep_l1_inputs(inputs, x1s=None):
    rel = np.asarray(inputs["L1_attn_rel"], np.float32)
    li = np.arange(8)
    bi = np.arange(16)
    ii_b = np.repeat(bi, 8)
    ii_l = np.tile(li, 16)
    samb = ii_b[:, None] == ii_b[None, :]
    relidx = (ii_l[None, :] - ii_l[:, None]) + 2047
    Mm = np.where(samb, rel[relidx], np.float32(NEG)).astype(np.float32)

    m1 = np.asarray(inputs["L1_ffn_mlp1_weight"])
    b1 = np.asarray(inputs["L1_ffn_mlp1_bias"])
    m2 = np.asarray(inputs["L1_ffn_mlp2_weight"])
    b2 = np.asarray(inputs["L1_ffn_mlp2_bias"])

    def colb(b):
        return np.ascontiguousarray(b.reshape(8, 8, 128).transpose(0, 2, 1))

    shared = {
        "Wq": np.ascontiguousarray(inputs["L1_attn_Wq"]),
        "Wk": np.ascontiguousarray(inputs["L1_attn_Wk"]),
        "Wv": np.ascontiguousarray(inputs["L1_attn_Wv"]),
        "Wo": np.ascontiguousarray(inputs["L1_attn_Wo"]),
        "bq_c": _col_layout(inputs["L1_attn_bq"], DMT),
        "bk_c": _col_layout(inputs["L1_attn_bk"], DMT),
        "bv_c": _col_layout(inputs["L1_attn_bv"], DMT),
        "bo_c": _col_layout(inputs["L1_attn_bo"], DMT),
        "Mmask": Mm,
        "nw_c": _col_layout(inputs["L1_ffn_norm_weight"], DMT),
        "gateWT": np.ascontiguousarray(np.asarray(inputs["L1_ffn_gate_weight"]).T),
        "gateB": np.ascontiguousarray(np.asarray(inputs["L1_ffn_gate_bias"]).reshape(8, 1)),
        "m1gT": np.ascontiguousarray(m1[:, 0::2, :].transpose(0, 2, 1)),
        "m1lT": np.ascontiguousarray(m1[:, 1::2, :].transpose(0, 2, 1)),
        "b1g_c": colb(b1[:, 0::2]),
        "b1l_c": colb(b1[:, 1::2]),
        "m2T": np.ascontiguousarray(m2.transpose(0, 2, 1)),
        "b2_c": colb(b2),
        "ones_col": np.ones((128, 1), np.float32),
        "ones_row": np.ones((1, 128), np.float32),
        "ident_f32": np.eye(128, dtype=np.float32),
        "ident_b16": np.eye(128, dtype=BF),
        "tri_lt": np.broadcast_to(np.tril(np.ones((8, 8), np.float32), -1).reshape(64), (128, 64)).copy(),
    }
    maps = []
    for j in range(NCORES):
        m = dict(shared)
        if x1s is not None:
            m["x1T"] = np.ascontiguousarray(
                np.concatenate([x1s[l][:, j * 256:(j + 1) * 256] for l in range(NCORES)], axis=1))
        maps.append(m)
    return maps


_NC_CACHE = {}


def kernel(**inputs):
    if "fused" not in _NC_CACHE:
        _NC_CACHE["fused"] = build_fused()
    mapsA = prep_l0_inputs(inputs)
    mapsB = prep_l1_inputs(inputs)
    maps = [{**a, **b} for a, b in zip(mapsA, mapsB)]
    res = run_bass_kernel_spmd(_NC_CACHE["fused"], maps, core_ids=list(range(NCORES)))
    out = np.empty((NCORES, T, DM), np.float32)
    for j in range(NCORES):
        xj = np.asarray(res.results[j]["x2_out"])
        for l in range(NCORES):
            out[l, j * 256:(j + 1) * 256, :] = xj[:, l * 256:(l + 1) * 256].T
    return out


def kernel_unfused(**inputs):
    if "l0" not in _NC_CACHE:
        _NC_CACHE["l0"] = build_l0()
    if "l1" not in _NC_CACHE:
        _NC_CACHE["l1"] = build_l1()
    mapsA = prep_l0_inputs(inputs)
    resA = run_bass_kernel_spmd(_NC_CACHE["l0"], mapsA, core_ids=list(range(NCORES)))
    x1s = [np.asarray(resA.results[i]["x1_out"]) for i in range(NCORES)]
    mapsB = prep_l1_inputs(inputs, x1s)
    resB = run_bass_kernel_spmd(_NC_CACHE["l1"], mapsB, core_ids=list(range(NCORES)))
    out = np.empty((NCORES, T, DM), np.float32)
    for j in range(NCORES):
        xj = np.asarray(resB.results[j]["x2_out"])
        for l in range(NCORES):
            out[l, j * 256:(j + 1) * 256, :] = xj[:, l * 256:(l + 1) * 256].T
    return out


# version 12
# speedup vs baseline: 1.1982x; 1.1982x over previous
import numpy as np
import ml_dtypes
from contextlib import ExitStack

import concourse.bass as bass
import concourse.mybir as mybir
from concourse.bass_utils import run_bass_kernel_spmd
from concourse.aot_env import allow_ici_all_to_all

F32 = mybir.dt.float32
BF16 = mybir.dt.bfloat16
AF = mybir.ActivationFunctionType
ALU = mybir.AluOpType
AX = mybir.AxisListType

NCORES = 8
DM = 1024
DI = 2048
T = 2048
S = 16
DMT = DM // 128
DIT = DI // 128
FC = 512
NFC = T // FC
TSC = 256
NSC = T // TSC
FF = 2048
NEG = -1.0e30
BF = ml_dtypes.bfloat16


class Sem:
    def __init__(self, h):
        self.h = h
        self.c = 0


class Plan:

    def __init__(self, nc, auto=None):
        self.nc = nc
        self.items = []
        self.auto = auto or {}

    def op(self, eng, fn, incs=()):
        sem = self.auto.get(eng)
        if sem is not None:
            self.items.append(("wait", eng, sem.h, sem.c))
            if not any(s is sem for s, _ in incs):
                incs = tuple(incs) + ((sem, 1),)
        for s, n in incs:
            s.c += n
        self.items.append(("op", eng, fn, tuple(incs)))

    def wait(self, eng, sem, val=None):
        self.items.append(("wait", eng, sem.h, sem.c if val is None else val))

    def run(self, eng_name):
        eng = getattr(self.nc, eng_name)
        for it in self.items:
            if it[0] == "op":
                _, e, fn, incs = it
                if e == eng_name:
                    ins = fn()
                    for s, n in incs:
                        ins = ins.then_inc(s.h, n)
            else:
                _, e, h, val = it
                if e == eng_name:
                    eng.wait_ge(h, val)


class Arena:
    def __init__(self, ap, cols):
        self.ap = ap
        self.cols = cols
        self.off = 0

    def f32(self, cols, rows=128):
        assert self.off + cols <= self.cols, f"arena overflow {self.off}+{cols}>{self.cols}"
        v = self.ap[0:rows, self.off:self.off + cols]
        self.off += cols
        return v

    def bf16(self, cols, rows=128):
        n32 = (cols + 1) // 2
        return self.f32(n32, rows=rows).bitcast(BF16)

    def mark(self):
        return self.off

    def reset(self, off=0):
        self.off = off


class Banks:
    def __init__(self, plan, banks):
        self.banks = [{"ap": b, "consumer": None} for b in banks]
        self.plan = plan
        self.i = 0

    def acquire(self):
        b = self.banks[self.i % len(self.banks)]
        self.i += 1
        if b["consumer"] is not None:
            sem, cnt = b["consumer"]
            self.plan.wait("tensor", sem, cnt)
        return b

    def release(self, b, sem):
        b["consumer"] = (sem, sem.c)


def _w_load_view(w, c0, cw, ktiles):
    return w[:, c0:c0 + cw].rearrange("(k p) c -> p k c", p=128)


def _mk_fc(nc, e, maincols):
    dram = nc.dram_tensor
    fc = {"nc": nc}
    fc["mainbuf"] = e(nc.sbuf_tensor("mainbuf", [128, maincols], F32)).ap()
    fc["perbuf"] = e(nc.sbuf_tensor("perbuf", [128, 9920], F32)).ap()
    fc["psb"] = [e(nc.psum_tensor(f"ps{i}", [128, 512], F32)).ap() for i in range(6)]
    fc["psT"] = [e(nc.psum_tensor(f"psT{i}", [128, 1024], BF16)).ap() for i in range(2)]
    for nm in ("qS", "qG", "qA", "qC", "eP", "eV", "eS", "eG"):
        fc[nm] = Sem(e(nc.semaphore(nm)))
    plan = Plan(nc)
    plan.auto = {"vector": fc["eV"], "scalar": fc["eS"]}
    fc["plan"] = plan
    fc["bk"] = Banks(plan, fc["psb"])
    fc["bkT"] = Banks(plan, fc["psT"])
    fc["M"] = Arena(fc["mainbuf"], maincols)
    fc["P"] = Arena(fc["perbuf"], 9920)
    fc["ones_col"] = dram("ones_col", [128, 1], F32, kind="ExternalInput").ap()
    fc["ones_row"] = dram("ones_row", [1, 128], F32, kind="ExternalInput").ap()
    fc["ident_f32"] = dram("ident_f32", [128, 128], F32, kind="ExternalInput").ap()
    fc["ident_b16"] = dram("ident_b16", [128, 128], BF16, kind="ExternalInput").ap()
    fc["tri_lt"] = dram("tri_lt", [128, 64], F32, kind="ExternalInput").ap()

    def dma(eng, q, dst, src):
        plan.op(eng, lambda d=dst, s=src, g=eng: getattr(nc, g).dma_start(out=d, in_=s), [(q, 16)])

    fc["dma"] = dma
    _ccols = {}

    def ccol(val):
        if val not in _ccols:
            ap = fc["P"].f32(1)
            plan.op("vector", lambda ap=ap, val=val: nc.vector.memset(ap, float(val)), [(fc["eV"], 1)])
            _ccols[val] = (ap, fc["eV"].c)
        return _ccols[val][0]

    def ccol_wait(eng, val):
        plan.wait(eng, fc["eV"], _ccols[val][1])

    fc["ccol"] = ccol
    fc["ccol_wait"] = ccol_wait
    return fc


def _run_block(fc, e):
    block = e(fc["nc"].Block())
    plan = fc["plan"]
    block.sync(lambda eng: plan.run("sync"))
    block.gpsimd(lambda eng: plan.run("gpsimd"))
    block.scalar(lambda eng: plan.run("scalar"))
    block.vector(lambda eng: plan.run("vector"))
    block.tensor(lambda eng: plan.run("tensor"))



def _emit_l0(C, dbg=(), standalone=True):
    nc = C["nc"]; plan = C["plan"]; M = C["M"]; P = C["P"]; bk = C["bk"]
    qS, qG, qA = C["qS"], C["qG"], C["qA"]
    eP, eV, eS = C["eP"], C["eV"], C["eS"]
    dma, ccol, ccol_wait = C["dma"], C["ccol"], C["ccol_wait"]
    ones_col, ones_row = C["ones_col"], C["ones_row"]
    ident_f32, tri_lt = C["ident_f32"], C["tri_lt"]
    dram = nc.dram_tensor

    xT = dram("xT", [DM, T], F32, kind="ExternalInput").ap()
    W_in = dram("W_in", [DM, 2 * DI], F32, kind="ExternalInput").ap()
    conv_w = dram("conv_w", [DI, 4], F32, kind="ExternalInput").ap()
    conv_bc = dram("conv_bc", [128, DIT], F32, kind="ExternalInput").ap()
    W_x = dram("W_x", [DI, 2 * S], F32, kind="ExternalInput").ap()
    W_dt = dram("W_dt", [DI, DI], F32, kind="ExternalInput").ap()
    b_dtc = dram("b_dtc", [128, DIT], F32, kind="ExternalInput").ap()
    A_log = dram("A_log", [DI, S], F32, kind="ExternalInput").ap()
    D_c = dram("D_c", [128, DIT], F32, kind="ExternalInput").ap()
    W_out = dram("W_out", [DI, DM], F32, kind="ExternalInput").ap()
    Wg = dram("Wg", [DM, 8], F32, kind="ExternalInput").ap()
    Ws1 = dram("Ws1", [DM, FF], F32, kind="ExternalInput").ap()
    Ws3 = dram("Ws3", [DM, FF], F32, kind="ExternalInput").ap()
    Ws2 = dram("Ws2", [FF, DM], F32, kind="ExternalInput").ap()
    We1 = dram("We1", [8, DM, FF], F32, kind="ExternalInput").ap()
    We3 = dram("We3", [8, DM, FF], F32, kind="ExternalInput").ap()
    We2 = dram("We2", [8, FF, DM], F32, kind="ExternalInput").ap()

    x1_out = dram("x1_out", [DM, T], F32, kind="ExternalOutput").ap() if standalone else None

    dbg_outs = {}

    def mkdbg(name, shape, dtyp=F32):
        if name in dbg:
            dbg_outs[name] = dram("dbg_" + name, shape, dtyp, kind="ExternalOutput").ap()
        return dbg_outs.get(name)

    d_xn = mkdbg("xn", [DM, T], BF16)
    d_xc = mkdbg("xc", [DI, T], BF16)
    d_dt = mkdbg("dt", [DI, T], BF16)
    d_gz = mkdbg("gz", [DI, T], BF16)
    d_bc = mkdbg("bc", [2 * S, T], BF16)
    d_mo = mkdbg("mo", [DM, T])
    d_w = mkdbg("w", [T, 8])
    d_gT = mkdbg("gT", [8, T])
    d_gtm = mkdbg("gtm", [128, 128])

    xn_sp = dram("xn_sp", [DM, T], BF16, kind="Internal").ap()
    xc_sp = dram("xc_sp", [DI, T], BF16, kind="Internal").ap()
    dt_sp = dram("dt_sp", [DI, T], BF16, kind="Internal").ap()
    gz_sp = dram("gz_sp", [DI, T], BF16, kind="Internal").ap()
    bc_sp = dram("bc_sp", [2 * S, T], BF16, kind="Internal").ap()
    mo_sp = dram("mo_sp", [DM, T], F32, kind="Internal").ap()

    if True:
        xn16 = P.bf16(DMT * T)
        A_sb = P.f32(DIT * S)
        conv_w_sb = P.f32(DIT * 4)
        conv_b_sb = P.f32(DIT)
        b_dt_sb = P.f32(DIT)
        D_sb = P.f32(DIT)
        carry = P.f32(DIT * S)
        ones_col_sb = P.f32(1)
        ones_row_sb = P.f32(128, rows=1)
        tmp_pa = P.f32(DIT * S)

        dma("gpsimd", qG, conv_w_sb.rearrange("p (k c) -> p k c", c=4),
            conv_w.rearrange("(k p) c -> p k c", p=128))
        dma("gpsimd", qG, conv_b_sb, conv_bc)
        dma("gpsimd", qG, b_dt_sb, b_dtc)
        dma("gpsimd", qG, D_sb, D_c)
        dma("gpsimd", qG, tmp_pa.rearrange("p (k s) -> p k s", s=S),
            A_log.rearrange("(k p) s -> p k s", p=128))
        dma("gpsimd", qG, ones_col_sb, ones_col)
        dma("gpsimd", qG, ones_row_sb, ones_row)
        init_loads = qG.c
        plan.wait("scalar", qG, init_loads)
        plan.op("scalar", lambda: nc.scalar.activation(out=A_sb, in_=tmp_pa, func=AF.Exp), [(eS, 1)])
        plan.wait("vector", eS)
        plan.op("vector", lambda: nc.vector.tensor_scalar_mul(A_sb, A_sb, -1.0), [(eV, 1)])
        aexp_done_v = eV.c
        plan.op("vector", lambda: nc.vector.memset(carry, 0.0), [(eV, 1)])
        carry_init_v = eV.c

        xtiles = [M.f32(T) for _ in range(DMT)]
        m_row = M.f32(T, rows=1)
        rstd_row = M.f32(T, rows=1)
        m_b = M.f32(T)
        rstd_b = M.f32(T)
        sqbuf = M.f32(T)

        for dmt in range(DMT):
            dma("sync", qS, xtiles[dmt], xT[dmt * 128:(dmt + 1) * 128, :])
        x_loaded = qS.c
        plan.wait("tensor", qS, x_loaded)

        mean_banks = []
        for fc in range(NFC):
            b = bk.acquire()
            mean_banks.append(b)
            for dmt in range(DMT):
                plan.op("tensor", lambda b=b, dmt=dmt, fc=fc: nc.tensor.matmul(
                    b["ap"][0:1, :], ones_col_sb, xtiles[dmt][:, fc * FC:(fc + 1) * FC],
                    start=(dmt == 0), stop=(dmt == DMT - 1)),
                    [(eP, 1)] if dmt == DMT - 1 else ())
        for fc in range(NFC):
            plan.wait("vector", eP)
            plan.op("vector", lambda fc=fc, b=mean_banks[fc]: nc.vector.tensor_scalar_mul(
                m_row[:, fc * FC:(fc + 1) * FC], b["ap"][0:1, :], 1.0 / DM), [(eV, 1)])
            bk.release(mean_banks[fc], eV)
        m_row_v = eV.c
        plan.wait("tensor", eV, m_row_v)
        for fc in range(NFC):
            b = bk.acquire()
            plan.op("tensor", lambda b=b, fc=fc: nc.tensor.matmul(
                b["ap"], ones_row_sb, m_row[:, fc * FC:(fc + 1) * FC], start=True, stop=True), [(eP, 1)])
            plan.wait("vector", eP)
            plan.op("vector", lambda b=b, fc=fc: nc.vector.tensor_copy(
                out=m_b[:, fc * FC:(fc + 1) * FC], in_=b["ap"]), [(eV, 1)])
            bk.release(b, eV)
        mb_done_v = eV.c
        for dmt in range(DMT):
            plan.op("vector", lambda dmt=dmt: nc.vector.tensor_sub(
                xtiles[dmt], xtiles[dmt], m_b), [(eV, 1)])
        xc_done_v = eV.c
        var_banks = [bk.acquire() for _ in range(NFC)]
        for dmt in range(DMT):
            plan.wait("scalar", eV, xc_done_v)
            if dmt > 0:
                plan.wait("scalar", eP)
            plan.op("scalar", lambda dmt=dmt: nc.scalar.activation(
                out=sqbuf, in_=xtiles[dmt], func=AF.Square), [(eS, 1)])
            plan.wait("tensor", eS)
            for fc in range(NFC):
                plan.op("tensor", lambda b=var_banks[fc], fc=fc, dmt=dmt: nc.tensor.matmul(
                    b["ap"][0:1, :], ones_col_sb, sqbuf[:, fc * FC:(fc + 1) * FC],
                    start=(dmt == 0), stop=(dmt == DMT - 1)),
                    [(eP, 1)])
        c_invdm = ccol(1.0 / DM)
        c_eps = ccol(1e-5)
        c_nhalf = ccol(-0.5)
        ccol_wait("scalar", 1.0 / DM)
        ccol_wait("scalar", 1e-5)
        ccol_wait("scalar", -0.5)
        plan.wait("scalar", eP)
        for fc in range(NFC):
            plan.op("scalar", lambda b=var_banks[fc], fc=fc: nc.scalar.activation(
                out=rstd_row[:, fc * FC:(fc + 1) * FC], in_=b["ap"][0:1, :], func=AF.Ln,
                scale=c_invdm[0:1, :], bias=c_eps[0:1, :]), [(eS, 1)])
        plan.wait("scalar", eS)
        plan.op("scalar", lambda: nc.scalar.activation(
            out=rstd_row, in_=rstd_row, func=AF.Exp, scale=c_nhalf[0:1, :]), [(eS, 1)])
        rstd_done_s = eS.c
        for fc in range(NFC):
            bk.release(var_banks[fc], eS)
        plan.wait("tensor", eS, rstd_done_s)
        for fc in range(NFC):
            b = bk.acquire()
            plan.op("tensor", lambda b=b, fc=fc: nc.tensor.matmul(
                b["ap"], ones_row_sb, rstd_row[:, fc * FC:(fc + 1) * FC], start=True, stop=True), [(eP, 1)])
            plan.wait("vector", eP)
            plan.op("vector", lambda b=b, fc=fc: nc.vector.tensor_copy(
                out=rstd_b[:, fc * FC:(fc + 1) * FC], in_=b["ap"]), [(eV, 1)])
            bk.release(b, eV)
        for dmt in range(DMT):
            plan.op("vector", lambda dmt=dmt: nc.vector.tensor_mul(
                xn16[:, dmt * T:(dmt + 1) * T], xtiles[dmt], rstd_b), [(eV, 1)])
            plan.wait("gpsimd", eV)
            dma("gpsimd", qG, xn_sp[dmt * 128:(dmt + 1) * 128, :], xn16[:, dmt * T:(dmt + 1) * T])
        xn_done_v = eV.c

        M.reset()
        wstg = [M.f32(DIT * 128) for _ in range(2)]
        wstg16 = [M.bf16(DIT * 128) for _ in range(2)]
        xc16 = M.bf16(DIT * T)
        xp_sb = M.f32(T)
        xcb_sb = M.f32(T)
        gz16_t = M.bf16(T)
        dt16_t = M.bf16(T)
        e_sb = M.f32(FC)
        bc_sb = M.bf16(T, rows=32)

        stg_free_v = [eV.c, eV.c]
        stg_free_p = [eP.c, eP.c]

        def stream_w(idx, wap, c0, cw, kt):
            i = idx % 2
            plan.wait("sync", eV, stg_free_v[i])
            dma("sync", qS, wstg[i][:, :kt * cw].rearrange("p (k c) -> p k c", c=cw),
                _w_load_view(wap, c0, cw, kt))
            plan.wait("vector", qS)
            plan.wait("vector", eP, stg_free_p[i])
            plan.op("vector", lambda i=i, n=kt * cw: nc.vector.tensor_copy(
                out=wstg16[i][:, :n], in_=wstg[i][:, :n]), [(eV, 1)])
            stg_free_v[i] = eV.c
            return i, eV.c

        def mm_chain(bankobj, w16ap, rhs_fn, nkt, fcw, wait_v, wcol=128):
            plan.wait("tensor", eV, wait_v)
            for kt in range(nkt):
                plan.op("tensor", lambda b=bankobj, w=w16ap, kt=kt, f=rhs_fn, fcw=fcw, wcol=wcol: nc.tensor.matmul(
                    b["ap"][0:wcol, :fcw], w[:, kt * wcol:(kt + 1) * wcol], f(kt),
                    start=(kt == 0), stop=(kt == nkt - 1)), [(eP, 1)] if kt == nkt - 1 else ())

        for ot in range(32):
            i, wv = stream_w(ot, W_in, ot * 128, 128, DMT)
            fcbanks = []
            for fc in range(NFC):
                b = bk.acquire()
                fcbanks.append(b)
                mm_chain(b, wstg16[i], lambda kt, fc=fc: xn16[:, kt * T + fc * FC: kt * T + fc * FC + FC],
                         DMT, FC, wv)
            stg_free_p[i] = eP.c
            if ot < DIT:
                dit = ot
                for fc in range(NFC):
                    plan.wait("scalar", eP)
                    plan.op("scalar", lambda b=fcbanks[fc], fc=fc: nc.scalar.copy(
                        out=xp_sb[:, fc * FC:(fc + 1) * FC], in_=b["ap"]), [(eS, 1)])
                    bk.release(fcbanks[fc], eS)
                xp_done = eS.c
                plan.wait("vector", eS, xp_done)
                plan.wait("vector", qG, init_loads)
                plan.op("vector", lambda dit=dit: nc.vector.tensor_scalar(
                    out=xcb_sb, in0=xp_sb, scalar1=conv_w_sb[:, dit * 4 + 3:dit * 4 + 4],
                    scalar2=conv_b_sb[:, dit:dit + 1], op0=ALU.mult, op1=ALU.add), [(eV, 1)])
                for k in (2, 1, 0):
                    sh = 3 - k
                    plan.op("vector", lambda dit=dit, k=k, sh=sh: nc.vector.scalar_tensor_tensor(
                        out=xcb_sb[:, sh:T], in0=xp_sb[:, 0:T - sh],
                        scalar=conv_w_sb[:, dit * 4 + k:dit * 4 + k + 1],
                        in1=xcb_sb[:, sh:T], op0=ALU.mult, op1=ALU.add), [(eV, 1)])
                conv_done = eV.c
                plan.wait("scalar", eV, conv_done)
                plan.op("scalar", lambda dit=dit: nc.scalar.activation(
                    out=xc16[:, dit * T:(dit + 1) * T], in_=xcb_sb, func=AF.Silu), [(eS, 1)])
                plan.wait("gpsimd", eS)
                dma("gpsimd", qG, xc_sp[dit * 128:(dit + 1) * 128, :], xc16[:, dit * T:(dit + 1) * T])
            else:
                zt = ot - DIT
                for fc in range(NFC):
                    plan.wait("scalar", eP)
                    if fc == 0:
                        plan.wait("scalar", qG)
                    plan.op("scalar", lambda b=fcbanks[fc], fc=fc: nc.scalar.activation(
                        out=gz16_t[:, fc * FC:(fc + 1) * FC], in_=b["ap"], func=AF.Silu), [(eS, 1)])
                    bk.release(fcbanks[fc], eS)
                plan.wait("gpsimd", eS)
                dma("gpsimd", qG, gz_sp[zt * 128:(zt + 1) * 128, :], gz16_t)

        plan.wait("tensor", eS)
        for ot in range(DIT):
            i, wv = stream_w(ot, W_dt, ot * 128, 128, DIT)
            for fc in range(NFC):
                b = bk.acquire()
                mm_chain(b, wstg16[i], lambda kt, fc=fc: xc16[:, kt * T + fc * FC: kt * T + fc * FC + FC],
                         DIT, FC, wv)
                if fc == NFC - 1:
                    stg_free_p[i] = eP.c
                plan.wait("scalar", eP)
                if fc == 0:
                    plan.wait("scalar", qG)
                plan.op("scalar", lambda b=b, ot=ot: nc.scalar.activation(
                    out=e_sb, in_=b["ap"], func=AF.Exp, bias=b_dt_sb[:, ot:ot + 1]), [(eS, 1)])
                bk.release(b, eS)
                plan.wait("scalar", eS)
                plan.op("scalar", lambda fc=fc: nc.scalar.activation(
                    out=dt16_t[:, fc * FC:(fc + 1) * FC], in_=e_sb, func=AF.Ln, bias=1.0), [(eS, 1)])
            plan.wait("gpsimd", eS)
            dma("gpsimd", qG, dt_sp[ot * 128:(ot + 1) * 128, :], dt16_t)

        i, wv = stream_w(0, W_x, 0, 2 * S, DIT)
        for fc in range(NFC):
            b = bk.acquire()
            mm_chain(b, wstg16[i],
                     lambda kt, fc=fc: xc16[:, kt * T + fc * FC: kt * T + fc * FC + FC],
                     DIT, FC, wv, wcol=2 * S)
            plan.wait("vector", eP)
            plan.op("vector", lambda b=b, fc=fc: nc.vector.tensor_copy(
                out=bc_sb[0:32, fc * FC:(fc + 1) * FC], in_=b["ap"][0:32, :]), [(eV, 1)])
            bk.release(b, eV)
        plan.wait("gpsimd", eV)
        dma("gpsimd", qG, bc_sp, bc_sb[0:32, :])
        bc_spilled = qG.c

        M.reset()
        eG = C["eG"]
        wo_stg = M.f32(DIT * 128)
        Wout16 = M.bf16(DMT * DIT * 128)
        Bb = M.bf16(S * TSC)
        Cb = M.bf16(S * TSC)
        dA = [M.bf16(S * TSC) for _ in range(2)]
        dtuB = [M.bf16(S * TSC) for _ in range(2)]
        hbuf = [M.bf16(S * TSC) for _ in range(2)]
        hC = [M.bf16(S * TSC) for _ in range(2)]
        fold16 = M.bf16(S)
        uc = wo_stg.bitcast(BF16)
        dtc = M.bf16(DIT * TSC)
        gzc = M.bf16(DIT * TSC)
        dtu2 = [M.bf16(TSC) for _ in range(2)]
        y_dit = M.f32(TSC)
        ygz16 = M.bf16(DIT * TSC)
        mo_t = [M.f32(TSC) for _ in range(2)]

        wo_cast_v = eV.c
        for ot in range(DMT):
            plan.wait("sync", eV, wo_cast_v)
            dma("sync", qS, wo_stg.rearrange("p (k c) -> p k c", c=128),
                _w_load_view(W_out, ot * 128, 128, DIT))
            plan.wait("vector", qS)
            plan.op("vector", lambda ot=ot: nc.vector.tensor_copy(
                out=Wout16[:, ot * DIT * 128:(ot + 1) * DIT * 128], in_=wo_stg), [(eV, 1)])
            wo_cast_v = eV.c

        def v3(ap, n=S, t=TSC):
            return ap.rearrange("p (s t) -> p s t", s=n)

        plan.wait("sync", qG)
        scans_done = {}
        carry_ready = {}
        fold_done = {}
        out_done = {}
        mo_free = [0, 0]
        for tc in range(NSC):
            c0 = tc * TSC
            plan.wait("sync", eV)
            plan.wait("sync", eS)
            dma("sync", qS, uc.rearrange("p (k t) -> p k t", t=TSC),
                xc_sp[:, c0:c0 + TSC].rearrange("(k p) t -> p k t", p=128))
            dma("sync", qS, dtc.rearrange("p (k t) -> p k t", t=TSC),
                dt_sp[:, c0:c0 + TSC].rearrange("(k p) t -> p k t", p=128))
            dma("sync", qS, gzc.rearrange("p (k t) -> p k t", t=TSC),
                gz_sp[:, c0:c0 + TSC].rearrange("(k p) t -> p k t", p=128))
            chunk_q = qS.c
            if tc == 0:
                plan.wait("gpsimd", qG, bc_spilled)
            plan.wait("gpsimd", eV)
            plan.wait("gpsimd", eG)
            dma("gpsimd", qG, v3(Bb), bc_sp[0:S, c0:c0 + TSC].unsqueeze(0).partition_broadcast(128))
            dma("gpsimd", qG, v3(Cb), bc_sp[S:2 * S, c0:c0 + TSC].unsqueeze(0).partition_broadcast(128))
            chunk_g = qG.c
            plan.wait("vector", qS, chunk_q)
            if tc == 0:
                plan.wait("vector", eV, max(aexp_done_v, carry_init_v))
                plan.wait("scalar", eV, aexp_done_v)

            hC_g = {}

            def st_prep(dit):
                pi = dit % 2
                plan.wait("scalar", qS, chunk_q)
                plan.wait("scalar", eV, scans_done.get(dit - 2, 0))
                plan.wait("scalar", eG, fold_done.get(dit - 2, 0))
                for s in range(S):
                    plan.op("scalar", lambda s=s, dit=dit, pi=pi: nc.scalar.activation(
                        out=dA[pi][:, s * TSC:(s + 1) * TSC],
                        in_=dtc[:, dit * TSC:(dit + 1) * TSC], func=AF.Exp,
                        scale=A_sb[:, dit * S + s:dit * S + s + 1]), [(eS, 1)])
                dA_s = eS.c
                plan.wait("gpsimd", eV, scans_done.get(dit - 2, 0))
                plan.wait("gpsimd", qG, chunk_g)
                plan.wait("gpsimd", qS, chunk_q)
                plan.op("gpsimd", lambda dit=dit, pi=pi: nc.gpsimd.tensor_mul(
                    dtu2[pi], dtc[:, dit * TSC:(dit + 1) * TSC],
                    uc[:, dit * TSC:(dit + 1) * TSC]), [(eG, 1)])
                plan.op("gpsimd", lambda dit=dit, pi=pi: nc.gpsimd.tensor_tensor(
                    out=v3(dtuB[pi]),
                    in0=dtu2[pi].unsqueeze(1).to_broadcast([128, S, TSC]),
                    in1=v3(Bb), op=ALU.mult), [(eG, 1)])
                plan.wait("gpsimd", eS, dA_s)
                plan.wait("gpsimd", eV, carry_ready.get(dit, 0))
                plan.op("gpsimd", lambda dit=dit, pi=pi: nc.gpsimd.tensor_mul(
                    fold16, v3(dA[pi])[:, :, 0:1].squeeze(2),
                    carry[:, dit * S:(dit + 1) * S]), [(eG, 1)])
                plan.op("gpsimd", lambda dit=dit, pi=pi: nc.gpsimd.tensor_add(
                    v3(dtuB[pi])[:, :, 0:1].squeeze(2),
                    v3(dtuB[pi])[:, :, 0:1].squeeze(2), fold16), [(eG, 1)])
                plan.op("gpsimd", lambda pi=pi: nc.gpsimd.memset(
                    v3(dA[pi])[:, :, 0:1].squeeze(2), 0.0), [(eG, 1)])
                fold_done[dit] = eG.c

            def st_hC(dit):
                pi = dit % 2
                plan.wait("gpsimd", eV, carry_ready[(tc, dit)])
                plan.wait("gpsimd", eV, out_done.get(dit - 2, 0))
                plan.op("gpsimd", lambda pi=pi: nc.gpsimd.tensor_mul(
                    hC[pi], hbuf[pi], Cb), [(eG, 1)])
                hC_g[dit] = eG.c

            def st_scan(dit):
                pi = dit % 2
                plan.wait("vector", eG, fold_done[dit])
                plan.op("vector", lambda dit=dit, pi=pi: nc.vector.tensor_tensor_scan(
                    out=hbuf[pi], data0=dA[pi], data1=dtuB[pi], initial=0.0,
                    op0=ALU.mult, op1=ALU.add), ())
                plan.op("vector", lambda dit=dit, pi=pi: nc.vector.tensor_copy(
                    out=carry[:, dit * S:(dit + 1) * S],
                    in_=v3(hbuf[pi])[:, :, TSC - 1:TSC].squeeze(2)), [(eV, 1)])
                carry_ready[dit] = eV.c
                carry_ready[(tc, dit)] = eV.c
                scans_done[dit] = eV.c

            def st_out(dit):
                pi = dit % 2
                uc_s = uc[:, dit * TSC:(dit + 1) * TSC]
                h = hC[pi]
                plan.wait("vector", eG, hC_g[dit])
                plan.op("vector", lambda h=h: nc.vector.tensor_reduce(
                    out=y_dit, in_=h.rearrange("p (s t) -> p t s", s=S),
                    axis=AX.X, op=ALU.add), [(eV, 1)])
                plan.op("vector", lambda dit=dit, uc_s=uc_s: nc.vector.scalar_tensor_tensor(
                    out=y_dit, in0=uc_s, scalar=D_sb[:, dit:dit + 1],
                    in1=y_dit, op0=ALU.mult, op1=ALU.add), [(eV, 1)])
                plan.op("vector", lambda dit=dit: nc.vector.tensor_mul(
                    ygz16[:, dit * TSC:(dit + 1) * TSC], y_dit,
                    gzc[:, dit * TSC:(dit + 1) * TSC]), [(eV, 1)])
                out_done[dit] = eV.c

            for dit in range(DIT):
                st_prep(dit)
                if dit >= 1:
                    st_hC(dit - 1)
                if dit >= 2:
                    st_out(dit - 2)
                st_scan(dit)
            st_hC(DIT - 1)
            st_out(DIT - 2)
            st_out(DIT - 1)
            ygz_v = eV.c
            for ot in range(DMT):
                b = bk.acquire()
                plan.wait("tensor", eV, ygz_v)
                for kt in range(DIT):
                    plan.op("tensor", lambda b=b, ot=ot, kt=kt: nc.tensor.matmul(
                        b["ap"][:, :TSC], Wout16[:, ot * DIT * 128 + kt * 128: ot * DIT * 128 + (kt + 1) * 128],
                        ygz16[:, kt * TSC:(kt + 1) * TSC],
                        start=(kt == 0), stop=(kt == DIT - 1)), [(eP, 1)] if kt == DIT - 1 else ())
                mi = ot % 2
                plan.wait("scalar", eP)
                plan.wait("scalar", qG, mo_free[mi])
                plan.op("scalar", lambda b=b, mi=mi: nc.scalar.copy(out=mo_t[mi], in_=b["ap"][:, :TSC]), [(eS, 1)])
                bk.release(b, eS)
                plan.wait("gpsimd", eS)
                dma("gpsimd", qG, mo_sp[ot * 128:(ot + 1) * 128, c0:c0 + TSC], mo_t[mi])
                mo_free[mi] = qG.c

        M.reset()
        x1_acc = M.f32(DMT * T)
        h1reg = M.f32(8 * T // 2)
        h1buf = h1reg.bitcast(BF16)
        w1stg = [M.f32(DMT * 128) for _ in range(3)]
        w3stg = [M.f32(DMT * 128) for _ in range(3)]
        w1c16 = [M.bf16(DMT * 128) for _ in range(3)]
        w3c16 = [M.bf16(DMT * 128) for _ in range(3)]
        wgstg = M.f32(DMT * 8)
        wgc16 = M.bf16(DMT * 8)
        w_b16 = M.bf16(T)
        s1_sb = M.f32(FC)
        tmp16 = M.bf16(FC)
        gatesT = h1reg[0:8, 0:T]
        gates_tm = M.f32(128)
        w_tm = M.f32(128)
        gt3 = M.f32(64)
        eq3 = M.f32(64)
        rank8 = M.f32(8)
        kw8 = M.f32(8)
        col1 = M.f32(1)
        tri_sb = M.f32(64)
        ident_sb = M.f32(128)
        mo_rd = h1reg[:, 0:T]
        w_row = mo_rd[0:1, :]

        plan.wait("sync", qG)
        dma("sync", qS, tri_sb, tri_lt)
        dma("sync", qS, ident_sb, ident_f32)
        for dmt in range(DMT):
            plan.wait("sync", eV)
            dma("sync", qS, x1_acc[:, dmt * T:(dmt + 1) * T], xT[dmt * 128:(dmt + 1) * 128, :])
            dma("sync", qS, mo_rd, mo_sp[dmt * 128:(dmt + 1) * 128, :])
            plan.wait("vector", qS)
            plan.op("vector", lambda dmt=dmt: nc.vector.tensor_add(
                x1_acc[:, dmt * T:(dmt + 1) * T], x1_acc[:, dmt * T:(dmt + 1) * T], mo_rd), [(eV, 1)])
        x1_init_v = eV.c

        dma("sync", qS, wgstg.rearrange("p (k c) -> p k c", c=8), _w_load_view(Wg, 0, 8, DMT))
        plan.wait("vector", qS)
        plan.op("vector", lambda: nc.vector.tensor_copy(out=wgc16, in_=wgstg), [(eV, 1)])
        wg_v = eV.c
        for fc in range(NFC):
            b = bk.acquire()
            mm_chain(b, wgc16, lambda kt, fc=fc: xn16[:, kt * T + fc * FC: kt * T + fc * FC + FC],
                     DMT, FC, wg_v, wcol=8)
            plan.wait("scalar", eP)
            plan.op("scalar", lambda b=b, fc=fc: nc.scalar.activation(
                out=gatesT[0:8, fc * FC:(fc + 1) * FC], in_=b["ap"][0:8, :], func=AF.Sigmoid), [(eS, 1)])
            bk.release(b, eS)
        gates_s = eS.c
        plan.wait("tensor", eS, gates_s)
        plan.wait("tensor", qS)
        bT = bk.acquire()
        for tt in range(16):
            plan.op("tensor", lambda tt=tt, b=bT: nc.tensor.transpose(
                out=b["ap"][:, tt * 8:(tt + 1) * 8], in_=gatesT[0:8, tt * 128:(tt + 1) * 128],
                identity=ident_sb[0:8, 0:8]), [(eP, 1)] if tt == 15 else ())
        plan.wait("vector", eP)
        plan.op("vector", lambda b=bT: nc.vector.tensor_copy(out=gates_tm, in_=b["ap"][:, 0:128]), [(eV, 1)])
        bk.release(bT, eV)
        if d_gT is not None:
            plan.wait("gpsimd", eS, gates_s)
            dma("gpsimd", qG, d_gT, gatesT)
        if d_gtm is not None:
            plan.wait("gpsimd", eV)
            dma("gpsimd", qG, d_gtm, gates_tm)
        plan.wait("vector", qS)
        for tt in range(16):
            g = gates_tm[:, tt * 8:(tt + 1) * 8]
            g_i = g.unsqueeze(1).to_broadcast([128, 8, 8])
            g_j = g.unsqueeze(2).to_broadcast([128, 8, 8])
            t3 = lambda ap: ap.rearrange("p (j i) -> p j i", j=8)
            plan.op("vector", lambda g_i=g_i, g_j=g_j: nc.vector.tensor_tensor(
                out=t3(gt3), in0=g_i, in1=g_j, op=ALU.is_gt), [(eV, 1)])
            plan.op("vector", lambda g_i=g_i, g_j=g_j: nc.vector.tensor_tensor(
                out=t3(eq3), in0=g_i, in1=g_j, op=ALU.is_equal), [(eV, 1)])
            plan.op("vector", lambda: nc.vector.tensor_mul(eq3, eq3, tri_sb), [(eV, 1)])
            plan.op("vector", lambda: nc.vector.tensor_add(gt3, gt3, eq3), [(eV, 1)])
            plan.op("vector", lambda: nc.vector.tensor_reduce(
                out=rank8, in_=t3(gt3), axis=AX.X, op=ALU.add), [(eV, 1)])
            plan.op("vector", lambda: nc.vector.tensor_scalar(
                out=rank8, in0=rank8, scalar1=1.5, scalar2=None, op0=ALU.is_le), [(eV, 1)])
            plan.op("vector", lambda g=g: nc.vector.tensor_mul(kw8, rank8, g), [(eV, 1)])
            plan.op("vector", lambda: nc.vector.tensor_reduce(
                out=col1, in_=kw8, axis=AX.X, op=ALU.add), [(eV, 1)])
            plan.op("vector", lambda: nc.vector.reciprocal(col1, col1), [(eV, 1)])
            plan.op("vector", lambda: nc.vector.tensor_scalar_mul(col1, col1, 1.0 / 3.0), [(eV, 1)])
            rden_v = eV.c
            plan.wait("vector", eV, rden_v)
            plan.op("vector", lambda tt=tt: nc.vector.tensor_scalar_mul(
                w_tm[:, tt * 8:(tt + 1) * 8], kw8, col1), [(eV, 1)])
        wtm_v = eV.c
        if d_w is not None:
            plan.wait("gpsimd", eV, wtm_v)
            dma("gpsimd", qG, d_w.rearrange("(tt p) e -> p tt e", p=128),
                w_tm.rearrange("p (tt e) -> p tt e", e=8))

        stg_v = {(n_, i_): 0 for n_ in ("w1", "w3") for i_ in range(3)}
        stg_p = {(n_, i_): 0 for n_ in ("w1", "w3") for i_ in range(3)}
        stg_i = {"w1": 0, "w3": 0}

        def stage(name, stgaps, c16aps, view, n):
            i = stg_i[name] % len(stgaps)
            stg_i[name] += 1
            plan.wait("sync", eV, stg_v[(name, i)])
            dma("sync", qS, stgaps[i][:, :n].rearrange("p (k c) -> p k c", c=128), view)
            plan.wait("vector", qS)
            plan.wait("vector", eP, stg_p[(name, i)])
            plan.op("vector", lambda a=c16aps[i], b=stgaps[i], n=n: nc.vector.tensor_copy(
                out=a[:, :n], in_=b[:, :n]), [(eV, 1)])
            stg_v[(name, i)] = eV.c
            return i, eV.c

        def stage_done(name, i):
            stg_p[(name, i)] = eP.c

        for ei in range(9):
            shared = (ei == 0)
            eidx = ei - 1
            if not shared:
                plan.wait("tensor", eV, wtm_v)
                for half4 in range(4):
                    bT = bk.acquire()
                    for q in range(4):
                        tt = half4 * 4 + q
                        plan.op("tensor", lambda tt=tt, q=q, b=bT, eidx=eidx: nc.tensor.transpose(
                            out=b["ap"][0:1, q * 128:(q + 1) * 128],
                            in_=w_tm[:, tt * 8 + eidx:tt * 8 + eidx + 1],
                            identity=ident_sb), [(eP, 1)] if q == 3 else ())
                    plan.wait("scalar", eP)
                    plan.wait("scalar", eV, x1_init_v)
                    plan.op("scalar", lambda b=bT, half4=half4: nc.scalar.copy(
                        out=w_row[:, half4 * FC:(half4 + 1) * FC], in_=b["ap"][0:1, :]), [(eS, 1)])
                    bk.release(bT, eS)
                wrow_s = eS.c
                plan.wait("tensor", eS, wrow_s)
                for fc in range(NFC):
                    b = bk.acquire()
                    plan.op("tensor", lambda b=b, fc=fc: nc.tensor.matmul(
                        b["ap"], ones_row_sb, w_row[:, fc * FC:(fc + 1) * FC],
                        start=True, stop=True), [(eP, 1)])
                    plan.wait("scalar", eP)
                    plan.op("scalar", lambda b=b, fc=fc: nc.scalar.copy(
                        out=w_b16[:, fc * FC:(fc + 1) * FC], in_=b["ap"]), [(eS, 1)])
                    bk.release(b, eS)
                wb_s = eS.c
            for half in range(2):
                for fsl in range(8):
                    fs = half * 8 + fsl
                    if shared:
                        v1 = _w_load_view(Ws1, fs * 128, 128, DMT)
                        v3w = _w_load_view(Ws3, fs * 128, 128, DMT)
                    else:
                        v1 = _w_load_view(We1[eidx], fs * 128, 128, DMT)
                        v3w = _w_load_view(We3[eidx], fs * 128, 128, DMT)
                    i1, r1 = stage("w1", w1stg, w1c16, v1, DMT * 128)
                    i3, r3 = stage("w3", w3stg, w3c16, v3w, DMT * 128)
                    for fc in range(NFC):
                        ba = bk.acquire()
                        mm_chain(ba, w1c16[i1], lambda kt, fc=fc: xn16[:, kt * T + fc * FC: kt * T + fc * FC + FC],
                                 DMT, FC, r1)
                        bb = bk.acquire()
                        mm_chain(bb, w3c16[i3], lambda kt, fc=fc: xn16[:, kt * T + fc * FC: kt * T + fc * FC + FC],
                                 DMT, FC, r3)
                        plan.wait("scalar", eP)
                        plan.op("scalar", lambda ba=ba: nc.scalar.activation(
                            out=s1_sb, in_=ba["ap"], func=AF.Silu), [(eS, 1)])
                        bk.release(ba, eS)
                        s1_s = eS.c
                        plan.wait("vector", eS, s1_s)
                        dst = h1buf[:, fsl * T + fc * FC: fsl * T + fc * FC + FC]
                        if shared:
                            plan.op("vector", lambda bb=bb, dst=dst: nc.vector.scalar_tensor_tensor(
                                out=dst, in0=s1_sb, scalar=1.0 / 3.0, in1=bb["ap"],
                                op0=ALU.mult, op1=ALU.mult), [(eV, 1)])
                        else:
                            plan.op("vector", lambda bb=bb: nc.vector.tensor_mul(
                                tmp16, s1_sb, bb["ap"]), [(eV, 1)])
                            plan.wait("vector", eV)
                            plan.wait("vector", eS, wb_s)
                            plan.op("vector", lambda dst=dst, fc=fc: nc.vector.tensor_mul(
                                dst, tmp16, w_b16[:, fc * FC:(fc + 1) * FC]), [(eV, 1)])
                        bk.release(bb, eV)
                    stage_done("w1", i1)
                    stage_done("w3", i3)
                h1_v = eV.c
                for ot in range(DMT):
                    if shared:
                        v2 = Ws2[half * 1024:(half + 1) * 1024, ot * 128:(ot + 1) * 128].rearrange(
                            "(k p) c -> p k c", p=128)
                    else:
                        v2 = We2[eidx][half * 1024:(half + 1) * 1024, ot * 128:(ot + 1) * 128].rearrange(
                            "(k p) c -> p k c", p=128)
                    i2, r2 = stage("w1", w1stg, w1c16, v2, DMT * 128)
                    for tc4 in range(NFC):
                        b = bk.acquire()
                        mm_chain(b, w1c16[i2], lambda kt, tc4=tc4: h1buf[:, kt * T + tc4 * FC: kt * T + tc4 * FC + FC],
                                 8, FC, max(r2, h1_v))
                        plan.wait("vector", eP)
                        plan.wait("vector", eV, x1_init_v)
                        plan.op("vector", lambda b=b, ot=ot, tc4=tc4: nc.vector.tensor_add(
                            x1_acc[:, ot * T + tc4 * FC: ot * T + tc4 * FC + FC],
                            x1_acc[:, ot * T + tc4 * FC: ot * T + tc4 * FC + FC],
                            b["ap"]), [(eV, 1)])
                        bk.release(b, eV)
                    stage_done("w1", i2)

        if standalone:
            plan.wait("gpsimd", eV)
            for dmt in range(DMT):
                dma("gpsimd", qG, x1_out[dmt * 128:(dmt + 1) * 128, :], x1_acc[:, dmt * T:(dmt + 1) * T])

        plan.wait("gpsimd", qG)
        plan.wait("gpsimd", qS)
        if d_xn is not None:
            dma("gpsimd", qG, d_xn, xn_sp)
        if d_xc is not None:
            dma("gpsimd", qG, d_xc, xc_sp)
        if d_dt is not None:
            dma("gpsimd", qG, d_dt, dt_sp)
        if d_gz is not None:
            dma("gpsimd", qG, d_gz, gz_sp)
        if d_bc is not None:
            dma("gpsimd", qG, d_bc, bc_sp)
        if d_mo is not None:
            dma("gpsimd", qG, d_mo, mo_sp)

        plan.wait("gpsimd", qG)
        plan.wait("gpsimd", qS)

    return {"x1_acc": x1_acc}


def build_l0(dbg=()):
    nc = bass.Bass("TRN2", target_bir_lowering=False)
    with ExitStack() as ctx:
        fc = _mk_fc(nc, ctx.enter_context, maincols=43264)
        _emit_l0(fc, dbg, standalone=True)
        _run_block(fc, ctx.enter_context)
    return nc


def _emit_l1(C, dbg=(), standalone=True):
    nc = C["nc"]; plan = C["plan"]; M = C["M"]; P = C["P"]
    bk = C["bk"]; bkT = C["bkT"]
    qS, qG, qC = C["qS"], C["qG"], C["qC"]
    eP, eV, eS = C["eP"], C["eV"], C["eS"]
    dma, ccol, ccol_wait = C["dma"], C["ccol"], C["ccol_wait"]
    ones_col, ones_row = C["ones_col"], C["ones_row"]
    ident_f32, ident_b16, tri_lt = C["ident_f32"], C["ident_b16"], C["tri_lt"]
    dram = nc.dram_tensor
    M.reset(0)

    x1T = dram("x1T", [DM, T], F32, kind="ExternalInput").ap() if standalone else None
    Wq = dram("Wq", [DM, DM], F32, kind="ExternalInput").ap()
    Wk = dram("Wk", [DM, DM], F32, kind="ExternalInput").ap()
    Wv = dram("Wv", [DM, DM], F32, kind="ExternalInput").ap()
    Wo = dram("Wo", [DM, DM], F32, kind="ExternalInput").ap()
    bq_c = dram("bq_c", [128, DMT], F32, kind="ExternalInput").ap()
    bk_c = dram("bk_c", [128, DMT], F32, kind="ExternalInput").ap()
    bv_c = dram("bv_c", [128, DMT], F32, kind="ExternalInput").ap()
    bo_c = dram("bo_c", [128, DMT], F32, kind="ExternalInput").ap()
    Mmask = dram("Mmask", [128, 128], F32, kind="ExternalInput").ap()
    nw_c = dram("nw_c", [128, DMT], F32, kind="ExternalInput").ap()
    gateWT = dram("gateWT", [DM, 8], BF16, kind="ExternalInput").ap()
    gateB = dram("gateB", [8, 1], BF16, kind="ExternalInput").ap()
    m1gT = dram("m1gT", [8, DM, DM], BF16, kind="ExternalInput").ap()
    m1lT = dram("m1lT", [8, DM, DM], BF16, kind="ExternalInput").ap()
    b1g_c = dram("b1g_c", [8, 128, DMT], BF16, kind="ExternalInput").ap()
    b1l_c = dram("b1l_c", [8, 128, DMT], BF16, kind="ExternalInput").ap()
    m2T = dram("m2T", [8, DM, DM], BF16, kind="ExternalInput").ap()
    b2_c = dram("b2_c", [8, 128, DMT], BF16, kind="ExternalInput").ap()

    x2_out = dram("x2_out", [DM, T], F32, kind="ExternalOutput").ap()

    dbg_outs = {}

    def mkdbg(name, shape, dtyp=F32):
        if name in dbg:
            dbg_outs[name] = dram("dbg_" + name, shape, dtyp, kind="ExternalOutput").ap()
        return dbg_outs.get(name)

    d_xn1 = mkdbg("xn1", [DM, T], BF16)
    d_at = mkdbg("at", [DM, T])
    d_nrm = mkdbg("nrm", [DM, T], BF16)
    d_cw = mkdbg("cw", [T, 8])

    xn1_sp = dram("xn1_sp", [DM, T], BF16, kind="Internal").ap()

    if True:
        ones_col_sb = P.f32(1)
        ones_row_sb = P.f32(128, rows=1)
        ident_sb = P.f32(128)
        identb_sb = P.bf16(128)
        tri_sb = P.f32(64)
        mask_sb = P.f32(128)
        bq_sb = P.f32(DMT)
        bk_sb = P.f32(DMT)
        bv_sb = P.f32(DMT)
        bo_sb = P.f32(DMT)
        nw_sb = P.f32(DMT)
        gb_sb = P.f32(1, rows=8)
        gb16_sb = P.bf16(2, rows=8)[:, 0:1]

        dma("gpsimd", qG, ones_col_sb, ones_col)
        dma("gpsimd", qG, ones_row_sb, ones_row)
        dma("gpsimd", qG, ident_sb, ident_f32)
        dma("gpsimd", qG, identb_sb, ident_b16)
        dma("gpsimd", qG, tri_sb, tri_lt)
        dma("gpsimd", qG, mask_sb, Mmask)
        dma("gpsimd", qG, bq_sb, bq_c)
        dma("gpsimd", qG, bk_sb, bk_c)
        dma("gpsimd", qG, bv_sb, bv_c)
        dma("gpsimd", qG, bo_sb, bo_c)
        dma("gpsimd", qG, nw_sb, nw_c)
        dma("gpsimd", qG, gb16_sb, gateB)
        init_q = qG.c
        plan.wait("vector", qG, init_q)
        plan.op("vector", lambda: nc.vector.tensor_copy(out=gb_sb, in_=gb16_sb), [(eV, 1)])

        x1g = M.f32(DMT * T)
        xn1 = M.bf16(DMT * T)
        ph0 = M.mark()

        m_b = M.f32(T)
        rstd_b = M.f32(T)
        sqbuf = M.f32(T)
        m_row = m_b[0:1, :]
        rstd_row = rstd_b[0:1, :]

        if standalone:
            for dmt in range(DMT):
                dma("sync", qS, x1g[:, dmt * T:(dmt + 1) * T], x1T[dmt * 128:(dmt + 1) * 128, :])
            x_loaded = qS.c
            plan.wait("tensor", qS, x_loaded)
            plan.wait("vector", qS, x_loaded)
        else:
            x1_acc = C["x1_acc"]
            a2a_in = dram("a2a_in", [NCORES * DM, T // NCORES], F32, kind="Internal").ap()
            a2a_out = dram("a2a_out", [NCORES * DM, T // NCORES], F32, kind="Internal").ap()
            a2a_in_v = a2a_in.rearrange("(j d) b -> j d b", j=NCORES)
            a2a_out_v = a2a_out.rearrange("(l d) b -> l d b", l=NCORES)
            plan.wait("sync", eV)
            for dmt in range(DMT):
                dma("sync", qS, a2a_in_v[:, dmt * 128:(dmt + 1) * 128, :].transpose([1, 0, 2]),
                    x1_acc[:, dmt * T:(dmt + 1) * T].rearrange("p (j b) -> p j b", j=NCORES))
            plan.wait("gpsimd", qS)

            def _cc():
                with allow_ici_all_to_all():
                    return nc.gpsimd.collective_compute(
                        "AllToAll", ALU.bypass, replica_groups=[list(range(NCORES))],
                        ins=[a2a_in.opt()], outs=[a2a_out.opt()])

            plan.op("gpsimd", _cc, [(qC, 1)])
            plan.wait("gpsimd", qC)
            for dmt in range(DMT):
                dma("gpsimd", qG, x1g[:, dmt * T:(dmt + 1) * T].rearrange("p (l b) -> p l b", l=NCORES),
                    a2a_out_v[:, dmt * 128:(dmt + 1) * 128, :].transpose([1, 0, 2]))
            x_loaded = qG.c
            plan.wait("tensor", qG, x_loaded)
            plan.wait("vector", qG, x_loaded)
            plan.wait("scalar", qG, x_loaded)
        mean_banks = []
        for fc in range(NFC):
            b = bk.acquire()
            mean_banks.append(b)
            for dmt in range(DMT):
                plan.op("tensor", lambda b=b, dmt=dmt, fc=fc: nc.tensor.matmul(
                    b["ap"][0:1, :], ones_col_sb, x1g[:, dmt * T + fc * FC: dmt * T + fc * FC + FC],
                    start=(dmt == 0), stop=(dmt == DMT - 1)),
                    [(eP, 1)] if dmt == DMT - 1 else ())
        for fc in range(NFC):
            plan.wait("vector", eP)
            plan.op("vector", lambda fc=fc, b=mean_banks[fc]: nc.vector.tensor_scalar_mul(
                m_row[:, fc * FC:(fc + 1) * FC], b["ap"][0:1, :], 1.0 / DM), [(eV, 1)])
            bk.release(mean_banks[fc], eV)
        m_row_v = eV.c
        plan.wait("tensor", eV, m_row_v)
        for fc in range(NFC):
            b = bk.acquire()
            plan.op("tensor", lambda b=b, fc=fc: nc.tensor.matmul(
                b["ap"], ones_row_sb, m_row[:, fc * FC:(fc + 1) * FC], start=True, stop=True), [(eP, 1)])
            plan.wait("vector", eP)
            plan.op("vector", lambda b=b, fc=fc: nc.vector.tensor_copy(
                out=m_b[:, fc * FC:(fc + 1) * FC], in_=b["ap"]), [(eV, 1)])
            bk.release(b, eV)
        mb_v = eV.c
        var_banks = [bk.acquire() for _ in range(NFC)]
        for dmt in range(DMT):
            plan.wait("vector", eV, mb_v)
            if dmt > 0:
                plan.wait("vector", eP)
            plan.op("vector", lambda dmt=dmt: nc.vector.tensor_sub(
                sqbuf, x1g[:, dmt * T:(dmt + 1) * T], m_b), [(eV, 1)])
            plan.wait("scalar", eV)
            plan.op("scalar", lambda: nc.scalar.activation(
                out=sqbuf, in_=sqbuf, func=AF.Square), [(eS, 1)])
            plan.wait("tensor", eS)
            for fc in range(NFC):
                plan.op("tensor", lambda b=var_banks[fc], fc=fc, dmt=dmt: nc.tensor.matmul(
                    b["ap"][0:1, :], ones_col_sb, sqbuf[:, fc * FC:(fc + 1) * FC],
                    start=(dmt == 0), stop=(dmt == DMT - 1)), [(eP, 1)])
        c_invdm = ccol(1.0 / DM)
        c_eps = ccol(1e-5)
        c_nhalf = ccol(-0.5)
        plan.wait("scalar", eV)
        plan.wait("scalar", eP)
        for fc in range(NFC):
            plan.op("scalar", lambda b=var_banks[fc], fc=fc: nc.scalar.activation(
                out=rstd_row[:, fc * FC:(fc + 1) * FC], in_=b["ap"][0:1, :], func=AF.Ln,
                scale=c_invdm[0:1, :], bias=c_eps[0:1, :]), [(eS, 1)])
        plan.op("scalar", lambda: nc.scalar.activation(
            out=rstd_row, in_=rstd_row, func=AF.Exp, scale=c_nhalf[0:1, :]), [(eS, 1)])
        rstd_s = eS.c
        for fc in range(NFC):
            bk.release(var_banks[fc], eS)
        plan.wait("tensor", eS, rstd_s)
        for fc in range(NFC):
            b = bk.acquire()
            plan.op("tensor", lambda b=b, fc=fc: nc.tensor.matmul(
                b["ap"], ones_row_sb, rstd_row[:, fc * FC:(fc + 1) * FC], start=True, stop=True), [(eP, 1)])
            plan.wait("vector", eP)
            plan.op("vector", lambda b=b, fc=fc: nc.vector.tensor_copy(
                out=rstd_b[:, fc * FC:(fc + 1) * FC], in_=b["ap"]), [(eV, 1)])
            bk.release(b, eV)
        for dmt in range(DMT):
            plan.op("vector", lambda dmt=dmt: nc.vector.tensor_sub(
                sqbuf, x1g[:, dmt * T:(dmt + 1) * T], m_b), [(eV, 1)])
            plan.op("vector", lambda dmt=dmt: nc.vector.tensor_mul(
                xn1[:, dmt * T:(dmt + 1) * T], sqbuf, rstd_b), [(eV, 1)])
            plan.wait("gpsimd", eV)
            dma("gpsimd", qG, xn1_sp[dmt * 128:(dmt + 1) * 128, :], xn1[:, dmt * T:(dmt + 1) * T])
        xn1_v = eV.c
        if d_xn1 is not None:
            plan.wait("gpsimd", qG)
            dma("gpsimd", qG, d_xn1, xn1_sp)

        M.reset(ph0)
        wstgf = M.f32(DMT * 128)
        wstg16 = M.bf16(DMT * 128)
        q16 = M.bf16(T)
        k16 = M.bf16(T)
        v16 = M.bf16(T)
        atn16 = M.bf16(T)
        sf_all = M.f32(T)
        mx16 = M.f32(16)
        pt_all = M.bf16(T)
        ptT_all = M.bf16(T)
        vt_all = M.bf16(1024)

        c_scale = ccol(0.125)

        def bmaj_out(ap, fc):
            return ap.rearrange("p (b l) -> p l b", l=8)[:, 2 * fc:2 * fc + 2, :]

        stw_v, stw_p = xn1_v, 0
        for hg in range(DMT):
            outs = []
            for (wap, bias_sb, dst) in ((Wq, bq_sb, q16), (Wk, bk_sb, k16), (Wv, bv_sb, v16)):
                plan.wait("sync", eV, stw_v)
                dma("sync", qS, wstgf.rearrange("p (k c) -> p k c", c=128),
                    _w_load_view(wap, hg * 128, 128, DMT))
                plan.wait("vector", qS)
                plan.wait("vector", eP, stw_p)
                plan.op("vector", lambda: nc.vector.tensor_copy(out=wstg16, in_=wstgf), [(eV, 1)])
                stw_v = eV.c
                plan.wait("tensor", eV, stw_v)
                plan.wait("tensor", eV, xn1_v)
                for fc in range(NFC):
                    b = bk.acquire()
                    for kt in range(DMT):
                        plan.op("tensor", lambda b=b, kt=kt, fc=fc: nc.tensor.matmul(
                            b["ap"], wstg16[:, kt * 128:(kt + 1) * 128],
                            xn1[:, kt * T + fc * FC: kt * T + fc * FC + FC],
                            start=(kt == 0), stop=(kt == DMT - 1)), [(eP, 1)] if kt == DMT - 1 else ())
                    plan.wait("scalar", eP)
                    plan.op("scalar", lambda b=b, dst=dst, fc=fc, bias_sb=bias_sb, hg=hg: nc.scalar.activation(
                        out=bmaj_out(dst, fc), in_=b["ap"], func=AF.Identity,
                        bias=bias_sb[:, hg:hg + 1]), [(eS, 1)])
                    bk.release(b, eS)
                stw_p = eP.c
            qkv_s = eS.c
            for h2 in range(2):
                lo, hi = h2 * 64, h2 * 64 + 64
                idnt = identb_sb[lo:hi, 0:64] if lo == 0 else identb_sb[lo:hi, 64:128]
                sbanks = []
                for b4 in range(4):
                    b = bk.acquire()
                    sbanks.append(b)
                    plan.wait("tensor", eS, qkv_s)
                    for q4 in range(4):
                        qt = b4 * 4 + q4
                        plan.op("tensor", lambda b=b, lo=lo, hi=hi, qt=qt, q4=q4: nc.tensor.matmul(
                            b["ap"][:, q4 * 128:(q4 + 1) * 128], q16[lo:hi, qt * 128:(qt + 1) * 128],
                            k16[lo:hi, qt * 128:(qt + 1) * 128], start=True, stop=True),
                            [(eP, 1)] if q4 == 3 else ())
                plan.wait("vector", eP)
                plan.wait("vector", qG, init_q)
                for b4 in range(4):
                    plan.op("vector", lambda b=sbanks[b4], b4=b4: nc.vector.tensor_tensor(
                        out=sf_all.rearrange("p (q c) -> p q c", c=128)[:, b4 * 4:(b4 + 1) * 4, :],
                        in0=b["ap"].rearrange("p (q c) -> p q c", c=128),
                        in1=mask_sb.unsqueeze(1).to_broadcast([128, 4, 128]),
                        op=ALU.add), [(eV, 1)])
                    bk.release(sbanks[b4], eV)
                plan.op("vector", lambda: nc.vector.tensor_reduce(
                    out=mx16, in_=sf_all.rearrange("p (q c) -> p q c", c=128),
                    axis=AX.X, op=ALU.max), [(eV, 1)])
                plan.op("vector", lambda: nc.vector.tensor_tensor(
                    out=sf_all.rearrange("p (q c) -> p q c", c=128),
                    in0=sf_all.rearrange("p (q c) -> p q c", c=128),
                    in1=mx16.unsqueeze(2).to_broadcast([128, 16, 128]),
                    op=ALU.subtract), [(eV, 1)])
                sub_v = eV.c
                plan.wait("scalar", eV, sub_v)
                plan.op("scalar", lambda: nc.scalar.activation(
                    out=sf_all, in_=sf_all, func=AF.Exp, scale=c_scale[:, :]), [(eS, 1)])
                exp_s = eS.c
                plan.wait("vector", eS, exp_s)
                plan.op("vector", lambda: nc.vector.tensor_reduce(
                    out=mx16, in_=sf_all.rearrange("p (q c) -> p q c", c=128),
                    axis=AX.X, op=ALU.add), [(eV, 1)])
                plan.op("vector", lambda: nc.vector.reciprocal(mx16, mx16), [(eV, 1)])
                plan.op("vector", lambda: nc.vector.tensor_tensor(
                    out=pt_all.rearrange("p (q c) -> p q c", c=128),
                    in0=sf_all.rearrange("p (q c) -> p q c", c=128),
                    in1=mx16.unsqueeze(2).to_broadcast([128, 16, 128]),
                    op=ALU.mult), [(eV, 1)])
                p_v = eV.c
                for half8 in range(2):
                    bT = bkT.acquire()
                    plan.wait("tensor", eV, p_v)
                    for t8 in range(8):
                        qt = half8 * 8 + t8
                        plan.op("tensor", lambda bT=bT, qt=qt, t8=t8: nc.tensor.transpose(
                            out=bT["ap"][:, t8 * 128:(t8 + 1) * 128],
                            in_=pt_all[:, qt * 128:(qt + 1) * 128], identity=identb_sb),
                            [(eP, 1)] if t8 == 7 else ())
                    plan.wait("vector", eP)
                    plan.op("vector", lambda bT=bT, half8=half8: nc.vector.tensor_copy(
                        out=ptT_all[:, half8 * 1024:(half8 + 1) * 1024], in_=bT["ap"][:, 0:1024]), [(eV, 1)])
                    bkT.release(bT, eV)
                bT = bkT.acquire()
                plan.wait("tensor", eS, qkv_s)
                for qt in range(16):
                    plan.op("tensor", lambda bT=bT, lo=lo, hi=hi, qt=qt, idnt=idnt: nc.tensor.transpose(
                        out=bT["ap"][:, qt * 64:(qt + 1) * 64],
                        in_=v16[lo:hi, qt * 128:(qt + 1) * 128], identity=idnt),
                        [(eP, 1)] if qt == 15 else ())
                plan.wait("vector", eP)
                plan.op("vector", lambda bT=bT: nc.vector.tensor_copy(
                    out=vt_all, in_=bT["ap"][:, 0:1024]), [(eV, 1)])
                bkT.release(bT, eV)
                pv_v = eV.c
                for b4 in range(4):
                    b = bk.acquire()
                    plan.wait("tensor", eV, pv_v)
                    for q4 in range(4):
                        qt = b4 * 4 + q4
                        plan.op("tensor", lambda b=b, qt=qt, q4=q4: nc.tensor.matmul(
                            b["ap"][0:64, q4 * 128:(q4 + 1) * 128], vt_all[:, qt * 64:(qt + 1) * 64],
                            ptT_all[:, qt * 128:(qt + 1) * 128], start=True, stop=True),
                            [(eP, 1)] if q4 == 3 else ())
                    plan.wait("scalar", eP)
                    plan.op("scalar", lambda b=b, lo=lo, hi=hi, b4=b4: nc.scalar.copy(
                        out=atn16[lo:hi, b4 * 512:(b4 + 1) * 512], in_=b["ap"][0:64, 0:512]), [(eS, 1)])
                    bk.release(b, eS)
            atn_s = eS.c
            plan.wait("sync", eV, stw_v)
            dma("sync", qS, wstgf.rearrange("p (k c) -> p k c", c=128),
                Wo[hg * 128:(hg + 1) * 128, :].rearrange("p (k c) -> p k c", c=128))
            plan.wait("vector", qS)
            plan.wait("vector", eP, stw_p)
            plan.op("vector", lambda: nc.vector.tensor_copy(out=wstg16, in_=wstgf), [(eV, 1)])
            stw_v = eV.c
            plan.wait("tensor", eV, stw_v)
            plan.wait("tensor", eS, atn_s)
            for ot in range(DMT):
                for fc in range(NFC):
                    b = bk.acquire()
                    plan.op("tensor", lambda b=b, ot=ot, fc=fc: nc.tensor.matmul(
                        b["ap"], wstg16[:, ot * 128:(ot + 1) * 128],
                        atn16[:, fc * FC:(fc + 1) * FC], start=True, stop=True), [(eP, 1)])
                    plan.wait("vector", eP)

                    def x1gv(ot=ot, fc=fc):
                        return x1g[:, ot * T:(ot + 1) * T].rearrange(
                            "p (l b) -> p b l", l=8)[:, 64 * fc:64 * (fc + 1), :]

                    plan.op("vector", lambda b=b, x1gv=x1gv: nc.vector.tensor_tensor(
                        out=x1gv(), in0=x1gv(),
                        in1=b["ap"].rearrange("p (b l) -> p b l", l=8),
                        op=ALU.add), [(eV, 1)])
                    bk.release(b, eV)
            stw_p = eP.c
        for dmt in range(DMT):
            plan.op("vector", lambda dmt=dmt: nc.vector.tensor_scalar_add(
                x1g[:, dmt * T:(dmt + 1) * T], x1g[:, dmt * T:(dmt + 1) * T],
                bo_sb[:, dmt:dmt + 1]), [(eV, 1)])
        attn_done_v = eV.c
        if d_at is not None:
            plan.wait("gpsimd", eV, attn_done_v)
            for dmt in range(DMT):
                dma("gpsimd", qG, d_at[dmt * 128:(dmt + 1) * 128, :], x1g[:, dmt * T:(dmt + 1) * T])

        M.reset(ph0)
        sqb = M.f32(T)
        rms_b = M.f32(T)
        rms_row = rms_b[0:1, :]
        w1g_16 = M.bf16(DMT * 128)
        w1l_16 = M.bf16(DMT * 128)
        w2_16 = M.bf16(DMT * 128)
        pg_all = M.f32(T)
        pl_all = M.f32(T)
        sg_all = M.f32(T)
        actreg = M.f32(T)
        act16 = [actreg.bitcast(BF16), M.bf16(2 * T)]
        glT = actreg[0:8, :]
        normed = xn1
        cb16 = rms_b[:, 0:T // 2].bitcast(BF16)
        wg16 = rms_b[:, T // 2:T // 2 + DMT * 4].bitcast(BF16)
        b1g_sb = P.f32(DMT)
        b1l_sb = P.f32(DMT)
        b2_sb = P.f32(DMT)
        b1g16 = P.bf16(DMT)
        b1l16 = P.bf16(DMT)
        b216 = P.bf16(DMT)
        tmp16g = M.bf16(FC, rows=8)
        g_tm = M.f32(128)
        c_tm = M.f32(128)
        gt3 = M.f32(64)
        eq3 = M.f32(64)
        rank8 = M.f32(8)
        kw8 = M.f32(8)
        col1 = M.f32(1)
        colv1 = M.f32(1)
        w_row = sqb[0:1, :]

        w1p_st = {"p": stw_p}
        stw_p_st = {"p": stw_p}
        var_banks = [bk.acquire() for _ in range(NFC)]
        for dmt in range(DMT):
            plan.wait("scalar", eP)
            plan.wait("scalar", eV, xn1_v)
            plan.op("scalar", lambda dmt=dmt: nc.scalar.activation(
                out=sqb, in_=xn1[:, dmt * T:(dmt + 1) * T], func=AF.Square), [(eS, 1)])
            plan.wait("tensor", eS)
            for fc in range(NFC):
                plan.op("tensor", lambda b=var_banks[fc], fc=fc, dmt=dmt: nc.tensor.matmul(
                    b["ap"][0:1, :], ones_col_sb, sqb[:, fc * FC:(fc + 1) * FC],
                    start=(dmt == 0), stop=(dmt == DMT - 1)), [(eP, 1)])
        plan.wait("scalar", eP)
        for fc in range(NFC):
            plan.op("scalar", lambda b=var_banks[fc], fc=fc: nc.scalar.activation(
                out=rms_row[:, fc * FC:(fc + 1) * FC], in_=b["ap"][0:1, :], func=AF.Ln,
                scale=c_invdm[0:1, :], bias=c_eps[0:1, :]), [(eS, 1)])
        plan.op("scalar", lambda: nc.scalar.activation(
            out=rms_row, in_=rms_row, func=AF.Exp, scale=c_nhalf[0:1, :]), [(eS, 1)])
        rms_s = eS.c
        for fc in range(NFC):
            bk.release(var_banks[fc], eS)
        plan.wait("tensor", eS, rms_s)
        for fc in range(NFC):
            b = bk.acquire()
            plan.op("tensor", lambda b=b, fc=fc: nc.tensor.matmul(
                b["ap"], ones_row_sb, rms_row[:, fc * FC:(fc + 1) * FC], start=True, stop=True), [(eP, 1)])
            plan.wait("vector", eP)
            plan.op("vector", lambda b=b, fc=fc: nc.vector.tensor_copy(
                out=rms_b[:, fc * FC:(fc + 1) * FC], in_=b["ap"]), [(eV, 1)])
            bk.release(b, eV)
        plan.wait("vector", qG, init_q)
        for dmt in range(DMT):
            plan.op("vector", lambda dmt=dmt: nc.vector.tensor_add(
                x1g[:, dmt * T:(dmt + 1) * T], x1g[:, dmt * T:(dmt + 1) * T],
                xn1[:, dmt * T:(dmt + 1) * T]), [(eV, 1)])
            plan.op("vector", lambda dmt=dmt: nc.vector.tensor_mul(
                sqb, xn1[:, dmt * T:(dmt + 1) * T], rms_b), [(eV, 1)])
            plan.op("vector", lambda dmt=dmt: nc.vector.tensor_scalar_mul(
                normed[:, dmt * T:(dmt + 1) * T], sqb, nw_sb[:, dmt:dmt + 1]), [(eV, 1)])
        normed_v = eV.c
        if d_nrm is not None:
            plan.wait("gpsimd", eV, normed_v)
            for dmt in range(DMT):
                dma("gpsimd", qG, d_nrm[dmt * 128:(dmt + 1) * 128, :], normed[:, dmt * T:(dmt + 1) * T])

        plan.wait("sync", eV, normed_v)
        dma("sync", qS, wg16.rearrange("p (k c) -> p k c", c=8),
            gateWT.rearrange("(k p) c -> p k c", p=128))
        wg_q = qS.c
        plan.wait("tensor", qS, wg_q)
        plan.wait("tensor", eV, normed_v)
        for fc in range(NFC):
            b = bk.acquire()
            for kt in range(DMT):
                plan.op("tensor", lambda b=b, kt=kt, fc=fc: nc.tensor.matmul(
                    b["ap"][0:8, :], wg16[:, kt * 8:(kt + 1) * 8],
                    normed[:, kt * T + fc * FC: kt * T + fc * FC + FC],
                    start=(kt == 0), stop=(kt == DMT - 1)), [(eP, 1)] if kt == DMT - 1 else ())
            plan.wait("scalar", eP)
            plan.op("scalar", lambda b=b: nc.scalar.activation(
                out=tmp16g, in_=b["ap"][0:8, :], func=AF.Identity, bias=gb_sb), [(eS, 1)])
            bk.release(b, eS)
            plan.op("scalar", lambda fc=fc: nc.scalar.copy(
                out=glT[:, fc * FC:(fc + 1) * FC], in_=tmp16g), [(eS, 1)])
        logits_s = eS.c

        plan.wait("tensor", eS, logits_s)
        bT = bk.acquire()
        for tt in range(16):
            plan.op("tensor", lambda tt=tt, b=bT: nc.tensor.transpose(
                out=b["ap"][:, tt * 8:(tt + 1) * 8], in_=glT[0:8, tt * 128:(tt + 1) * 128],
                identity=ident_sb[0:8, 0:8]), [(eP, 1)] if tt == 15 else ())
        plan.wait("vector", eP)
        plan.op("vector", lambda b=bT: nc.vector.tensor_copy(out=g_tm, in_=b["ap"][:, 0:128]), [(eV, 1)])
        bk.release(bT, eV)
        for tt in range(16):
            g = g_tm[:, tt * 8:(tt + 1) * 8]
            g_i = g.unsqueeze(1).to_broadcast([128, 8, 8])
            g_j = g.unsqueeze(2).to_broadcast([128, 8, 8])

            def t3(ap):
                return ap.rearrange("p (j i) -> p j i", j=8)

            plan.op("vector", lambda g_i=g_i, g_j=g_j: nc.vector.tensor_tensor(
                out=t3(gt3), in0=g_i, in1=g_j, op=ALU.is_gt), [(eV, 1)])
            plan.op("vector", lambda g_i=g_i, g_j=g_j: nc.vector.tensor_tensor(
                out=t3(eq3), in0=g_i, in1=g_j, op=ALU.is_equal), [(eV, 1)])
            plan.op("vector", lambda: nc.vector.tensor_mul(eq3, eq3, tri_sb), [(eV, 1)])
            plan.op("vector", lambda: nc.vector.tensor_add(gt3, gt3, eq3), [(eV, 1)])
            plan.op("vector", lambda: nc.vector.tensor_reduce(
                out=rank8, in_=t3(gt3), axis=AX.X, op=ALU.add), [(eV, 1)])
            plan.op("vector", lambda: nc.vector.tensor_scalar(
                out=rank8, in0=rank8, scalar1=1.5, scalar2=None, op0=ALU.is_le), [(eV, 1)])
            plan.op("vector", lambda g=g: nc.vector.tensor_reduce(
                out=colv1, in_=g, axis=AX.X, op=ALU.max), [(eV, 1)])
            plan.op("vector", lambda: nc.vector.tensor_scalar_mul(colv1, colv1, -1.0), [(eV, 1)])
            plan.wait("scalar", eV)
            plan.op("scalar", lambda g=g: nc.scalar.activation(
                out=kw8, in_=g, func=AF.Exp, bias=colv1), [(eS, 1)])
            plan.wait("vector", eS)
            plan.op("vector", lambda: nc.vector.tensor_mul(kw8, kw8, rank8), [(eV, 1)])
            plan.op("vector", lambda: nc.vector.tensor_reduce(
                out=col1, in_=kw8, axis=AX.X, op=ALU.add), [(eV, 1)])
            plan.op("vector", lambda: nc.vector.reciprocal(col1, col1), [(eV, 1)])
            plan.op("vector", lambda tt=tt: nc.vector.tensor_scalar_mul(
                c_tm[:, tt * 8:(tt + 1) * 8], kw8, col1), [(eV, 1)])
        ctm_v = eV.c
        if d_cw is not None:
            plan.wait("gpsimd", eV, ctm_v)
            dma("gpsimd", qG, d_cw.rearrange("(tt p) e -> p tt e", p=128),
                c_tm.rearrange("p (tt e) -> p tt e", e=8))

        w1_p = stw_p
        for eidx in range(8):
            plan.wait("tensor", eV, ctm_v)
            plan.wait("scalar", eV)
            for half4 in range(4):
                bT = bk.acquire()
                for q in range(4):
                    tt = half4 * 4 + q
                    plan.op("tensor", lambda tt=tt, q=q, b=bT, eidx=eidx: nc.tensor.transpose(
                        out=b["ap"][0:1, q * 128:(q + 1) * 128],
                        in_=c_tm[:, tt * 8 + eidx:tt * 8 + eidx + 1],
                        identity=ident_sb), [(eP, 1)] if q == 3 else ())
                plan.wait("scalar", eP)
                plan.op("scalar", lambda b=bT, half4=half4: nc.scalar.copy(
                    out=w_row[:, half4 * FC:(half4 + 1) * FC], in_=b["ap"][0:1, :]), [(eS, 1)])
                bk.release(bT, eS)
            wrow_s = eS.c
            plan.wait("tensor", eS, wrow_s)
            for fc in range(NFC):
                b = bk.acquire()
                plan.op("tensor", lambda b=b, fc=fc: nc.tensor.matmul(
                    b["ap"], ones_row_sb, w_row[0:1, fc * FC:(fc + 1) * FC],
                    start=True, stop=True), [(eP, 1)])
                plan.wait("scalar", eP)
                plan.op("scalar", lambda b=b, fc=fc: nc.scalar.copy(
                    out=cb16[:, fc * FC:(fc + 1) * FC], in_=b["ap"]), [(eS, 1)])
                bk.release(b, eS)
            cb_s = eS.c
            plan.wait("gpsimd", eV)
            dma("gpsimd", qG, b1g16, b1g_c[eidx])
            dma("gpsimd", qG, b1l16, b1l_c[eidx])
            dma("gpsimd", qG, b216, b2_c[eidx])
            bias_q = qG.c
            plan.wait("vector", qG, bias_q)
            plan.op("vector", lambda: nc.vector.tensor_copy(out=b1g_sb, in_=b1g16), [(eV, 1)])
            plan.op("vector", lambda: nc.vector.tensor_copy(out=b1l_sb, in_=b1l16), [(eV, 1)])
            plan.op("vector", lambda: nc.vector.tensor_copy(out=b2_sb, in_=b216), [(eV, 1)])
            bias_v = eV.c
            def ffn_mlp1(quarter):
                ab = act16[quarter % 2]
                for il in range(2):
                    it = quarter * 2 + il
                    plan.wait("sync", eP, w1p_st["p"])
                    dma("sync", qS, w1g_16.rearrange("p (k c) -> p k c", c=128),
                        m1gT[eidx][:, it * 128:(it + 1) * 128].rearrange("(k p) c -> p k c", p=128))
                    dma("sync", qS, w1l_16.rearrange("p (k c) -> p k c", c=128),
                        m1lT[eidx][:, it * 128:(it + 1) * 128].rearrange("(k p) c -> p k c", p=128))
                    w1_q = qS.c
                    plan.wait("tensor", qS, w1_q)
                    plan.wait("tensor", eV, normed_v)
                    c_alpha = ccol(1.702)
                    plan.wait("scalar", eV)
                    for fc in range(NFC):
                        bg = bk.acquire()
                        for kt in range(DMT):
                            plan.op("tensor", lambda b=bg, kt=kt, fc=fc: nc.tensor.matmul(
                                b["ap"], w1g_16[:, kt * 128:(kt + 1) * 128],
                                normed[:, kt * T + fc * FC: kt * T + fc * FC + FC],
                                start=(kt == 0), stop=(kt == DMT - 1)), [(eP, 1)] if kt == DMT - 1 else ())
                        bl = bk.acquire()
                        for kt in range(DMT):
                            plan.op("tensor", lambda b=bl, kt=kt, fc=fc: nc.tensor.matmul(
                                b["ap"], w1l_16[:, kt * 128:(kt + 1) * 128],
                                normed[:, kt * T + fc * FC: kt * T + fc * FC + FC],
                                start=(kt == 0), stop=(kt == DMT - 1)), [(eP, 1)] if kt == DMT - 1 else ())
                        plan.wait("scalar", eP)
                        plan.op("scalar", lambda b=bg, fc=fc: nc.scalar.copy(
                            out=pg_all[:, fc * FC:(fc + 1) * FC], in_=b["ap"]), [(eS, 1)])
                        bk.release(bg, eS)
                        plan.op("scalar", lambda b=bl, fc=fc: nc.scalar.copy(
                            out=pl_all[:, fc * FC:(fc + 1) * FC], in_=b["ap"]), [(eS, 1)])
                        bk.release(bl, eS)
                    w1p_st["p"] = eP.c
                    evac_s = eS.c
                    plan.wait("vector", eS, evac_s)
                    plan.wait("vector", eV, bias_v)
                    plan.op("vector", lambda it=it: nc.vector.tensor_scalar(
                        out=pg_all, in0=pg_all, scalar1=b1g_sb[:, it:it + 1], scalar2=7.0,
                        op0=ALU.add, op1=ALU.min), [(eV, 1)])
                    plan.op("vector", lambda it=it: nc.vector.tensor_scalar(
                        out=pl_all, in0=pl_all, scalar1=b1l_sb[:, it:it + 1], scalar2=7.0,
                        op0=ALU.add, op1=ALU.min), [(eV, 1)])
                    plan.op("vector", lambda: nc.vector.tensor_scalar(
                        out=pl_all, in0=pl_all, scalar1=-7.0, scalar2=1.0,
                        op0=ALU.max, op1=ALU.add), [(eV, 1)])
                    g_v = eV.c
                    plan.wait("scalar", eV, g_v)
                    plan.op("scalar", lambda: nc.scalar.activation(
                        out=sg_all, in_=pg_all, func=AF.Sigmoid, scale=c_alpha[:, :]), [(eS, 1)])
                    plan.wait("vector", eS)
                    plan.op("vector", lambda: nc.vector.tensor_mul(pg_all, pg_all, sg_all), [(eV, 1)])
                    plan.op("vector", lambda il=il, ab=ab: nc.vector.tensor_mul(
                        ab[:, il * T:(il + 1) * T], pg_all, pl_all), [(eV, 1)])
                for il in range(2):
                    plan.wait("vector", eS, cb_s)
                    plan.op("vector", lambda il=il, ab=ab: nc.vector.tensor_mul(
                        ab[:, il * T:(il + 1) * T], ab[:, il * T:(il + 1) * T], cb16), [(eV, 1)])
                return eV.c

            def ffn_p2(quarter, actc_v):
                ab = act16[quarter % 2]
                for ot in range(DMT):
                    plan.wait("sync", eP, stw_p_st["p"])
                    dma("sync", qS, w2_16[:, 0:2 * 128].rearrange("p (k c) -> p k c", c=128),
                        m2T[eidx][quarter * 256:(quarter + 1) * 256, ot * 128:(ot + 1) * 128].rearrange(
                            "(k p) c -> p k c", p=128))
                    w2_q = qS.c
                    plan.wait("tensor", qS, w2_q)
                    plan.wait("tensor", eV, actc_v)
                    for fc in range(NFC):
                        b = bk.acquire()
                        for kt in range(2):
                            plan.op("tensor", lambda b=b, kt=kt, fc=fc, ab=ab: nc.tensor.matmul(
                                b["ap"], w2_16[:, kt * 128:(kt + 1) * 128],
                                ab[:, kt * T + fc * FC: kt * T + fc * FC + FC],
                                start=(kt == 0), stop=(kt == 1)), [(eP, 1)] if kt == 1 else ())
                        plan.wait("vector", eP)
                        plan.op("vector", lambda b=b, ot=ot, fc=fc: nc.vector.tensor_add(
                            x1g[:, ot * T + fc * FC: ot * T + fc * FC + FC],
                            x1g[:, ot * T + fc * FC: ot * T + fc * FC + FC], b["ap"]), [(eV, 1)])
                        bk.release(b, eV)
                    stw_p_st["p"] = eP.c

            prev = None
            for quarter in range(4):
                av = ffn_mlp1(quarter)
                if prev is not None:
                    ffn_p2(prev[0], prev[1])
                prev = (quarter, av)
            ffn_p2(prev[0], prev[1])
            for ot in range(DMT):
                plan.wait("vector", eS, cb_s)
                plan.op("vector", lambda ot=ot: nc.vector.scalar_tensor_tensor(
                    out=x1g[:, ot * T:(ot + 1) * T], in0=cb16,
                    scalar=b2_sb[:, ot:ot + 1], in1=x1g[:, ot * T:(ot + 1) * T],
                    op0=ALU.mult, op1=ALU.add), [(eV, 1)])

        plan.wait("gpsimd", eV)
        for dmt in range(DMT):
            dma("gpsimd", qG, x2_out[dmt * 128:(dmt + 1) * 128, :], x1g[:, dmt * T:(dmt + 1) * T])

        plan.wait("gpsimd", qG)
        plan.wait("gpsimd", qS)

    return {}


def build_l1(dbg=()):
    nc = bass.Bass("TRN2", target_bir_lowering=False)
    with ExitStack() as ctx:
        fc = _mk_fc(nc, ctx.enter_context, maincols=43264)
        _emit_l1(fc, dbg, standalone=True)
        _run_block(fc, ctx.enter_context)
    return nc


def build_fused(dbg=()):
    nc = bass.Bass("TRN2", target_bir_lowering=False)
    with ExitStack() as ctx:
        fc = _mk_fc(nc, ctx.enter_context, maincols=43264)
        r0 = _emit_l0(fc, dbg=(), standalone=False)
        fc["x1_acc"] = r0["x1_acc"]
        _emit_l1(fc, dbg=dbg, standalone=False)
        _run_block(fc, ctx.enter_context)
    return nc


def _col_layout(v, ntiles):
    return np.ascontiguousarray(np.asarray(v, dtype=np.float32).reshape(ntiles, 128).T)


def prep_l0_inputs(inputs):
    shared = {
        "W_in": np.ascontiguousarray(inputs["L0_attn_W_in"]),
        "conv_w": np.ascontiguousarray(inputs["L0_attn_conv_w"]),
        "conv_bc": _col_layout(inputs["L0_attn_conv_b"], DIT),
        "W_x": np.ascontiguousarray(inputs["L0_attn_W_x"]),
        "W_dt": np.ascontiguousarray(inputs["L0_attn_W_dt"]),
        "b_dtc": _col_layout(inputs["L0_attn_b_dt"], DIT),
        "A_log": np.ascontiguousarray(inputs["L0_attn_A_log"]),
        "D_c": _col_layout(inputs["L0_attn_D"], DIT),
        "W_out": np.ascontiguousarray(inputs["L0_attn_W_out"]),
        "Wg": np.ascontiguousarray(inputs["L0_ffn_Wg"]),
        "Ws1": np.ascontiguousarray(inputs["L0_ffn_Ws1"]),
        "Ws3": np.ascontiguousarray(inputs["L0_ffn_Ws3"]),
        "Ws2": np.ascontiguousarray(inputs["L0_ffn_Ws2"]),
        "We1": np.ascontiguousarray(inputs["L0_ffn_We1"]),
        "We3": np.ascontiguousarray(inputs["L0_ffn_We3"]),
        "We2": np.ascontiguousarray(inputs["L0_ffn_We2"]),
        "ones_col": np.ones((128, 1), np.float32),
        "ones_row": np.ones((1, 128), np.float32),
        "ident_f32": np.eye(128, dtype=np.float32),
        "ident_b16": np.eye(128, dtype=BF),
        "tri_lt": np.broadcast_to(np.tril(np.ones((8, 8), np.float32), -1).reshape(64), (128, 64)).copy(),
    }
    x = np.asarray(inputs["L0_attn_x"])
    maps = []
    for i in range(NCORES):
        m = dict(shared)
        m["xT"] = np.ascontiguousarray(x[i].T)
        maps.append(m)
    return maps


def prep_l1_inputs(inputs, x1s=None):
    rel = np.asarray(inputs["L1_attn_rel"], np.float32)
    li = np.arange(8)
    bi = np.arange(16)
    ii_b = np.repeat(bi, 8)
    ii_l = np.tile(li, 16)
    samb = ii_b[:, None] == ii_b[None, :]
    relidx = (ii_l[None, :] - ii_l[:, None]) + 2047
    Mm = np.where(samb, rel[relidx], np.float32(NEG)).astype(np.float32)

    m1 = np.asarray(inputs["L1_ffn_mlp1_weight"])
    b1 = np.asarray(inputs["L1_ffn_mlp1_bias"])
    m2 = np.asarray(inputs["L1_ffn_mlp2_weight"])
    b2 = np.asarray(inputs["L1_ffn_mlp2_bias"])

    def colb(b):
        return np.ascontiguousarray(b.reshape(8, 8, 128).transpose(0, 2, 1))

    shared = {
        "Wq": np.ascontiguousarray(inputs["L1_attn_Wq"]),
        "Wk": np.ascontiguousarray(inputs["L1_attn_Wk"]),
        "Wv": np.ascontiguousarray(inputs["L1_attn_Wv"]),
        "Wo": np.ascontiguousarray(inputs["L1_attn_Wo"]),
        "bq_c": _col_layout(inputs["L1_attn_bq"], DMT),
        "bk_c": _col_layout(inputs["L1_attn_bk"], DMT),
        "bv_c": _col_layout(inputs["L1_attn_bv"], DMT),
        "bo_c": _col_layout(inputs["L1_attn_bo"], DMT),
        "Mmask": Mm,
        "nw_c": _col_layout(inputs["L1_ffn_norm_weight"], DMT),
        "gateWT": np.ascontiguousarray(np.asarray(inputs["L1_ffn_gate_weight"]).T),
        "gateB": np.ascontiguousarray(np.asarray(inputs["L1_ffn_gate_bias"]).reshape(8, 1)),
        "m1gT": np.ascontiguousarray(m1[:, 0::2, :].transpose(0, 2, 1)),
        "m1lT": np.ascontiguousarray(m1[:, 1::2, :].transpose(0, 2, 1)),
        "b1g_c": colb(b1[:, 0::2]),
        "b1l_c": colb(b1[:, 1::2]),
        "m2T": np.ascontiguousarray(m2.transpose(0, 2, 1)),
        "b2_c": colb(b2),
        "ones_col": np.ones((128, 1), np.float32),
        "ones_row": np.ones((1, 128), np.float32),
        "ident_f32": np.eye(128, dtype=np.float32),
        "ident_b16": np.eye(128, dtype=BF),
        "tri_lt": np.broadcast_to(np.tril(np.ones((8, 8), np.float32), -1).reshape(64), (128, 64)).copy(),
    }
    maps = []
    for j in range(NCORES):
        m = dict(shared)
        if x1s is not None:
            m["x1T"] = np.ascontiguousarray(
                np.concatenate([x1s[l][:, j * 256:(j + 1) * 256] for l in range(NCORES)], axis=1))
        maps.append(m)
    return maps


_NC_CACHE = {}


def kernel(**inputs):
    if "fused" not in _NC_CACHE:
        _NC_CACHE["fused"] = build_fused()
    mapsA = prep_l0_inputs(inputs)
    mapsB = prep_l1_inputs(inputs)
    maps = [{**a, **b} for a, b in zip(mapsA, mapsB)]
    res = run_bass_kernel_spmd(_NC_CACHE["fused"], maps, core_ids=list(range(NCORES)))
    out = np.empty((NCORES, T, DM), np.float32)
    for j in range(NCORES):
        xj = np.asarray(res.results[j]["x2_out"])
        for l in range(NCORES):
            out[l, j * 256:(j + 1) * 256, :] = xj[:, l * 256:(l + 1) * 256].T
    return out


def kernel_unfused(**inputs):
    if "l0" not in _NC_CACHE:
        _NC_CACHE["l0"] = build_l0()
    if "l1" not in _NC_CACHE:
        _NC_CACHE["l1"] = build_l1()
    mapsA = prep_l0_inputs(inputs)
    resA = run_bass_kernel_spmd(_NC_CACHE["l0"], mapsA, core_ids=list(range(NCORES)))
    x1s = [np.asarray(resA.results[i]["x1_out"]) for i in range(NCORES)]
    mapsB = prep_l1_inputs(inputs, x1s)
    resB = run_bass_kernel_spmd(_NC_CACHE["l1"], mapsB, core_ids=list(range(NCORES)))
    out = np.empty((NCORES, T, DM), np.float32)
    for j in range(NCORES):
        xj = np.asarray(resB.results[j]["x2_out"])
        for l in range(NCORES):
            out[l, j * 256:(j + 1) * 256, :] = xj[:, l * 256:(l + 1) * 256].T
    return out


# version 13 (final)
# speedup vs baseline: 1.2368x; 1.0322x over previous
import numpy as np
import ml_dtypes
from contextlib import ExitStack

import concourse.bass as bass
import concourse.mybir as mybir
from concourse.bass_utils import run_bass_kernel_spmd
from concourse.aot_env import allow_ici_all_to_all

F32 = mybir.dt.float32
BF16 = mybir.dt.bfloat16
AF = mybir.ActivationFunctionType
ALU = mybir.AluOpType
AX = mybir.AxisListType

NCORES = 8
DM = 1024
DI = 2048
T = 2048
S = 16
DMT = DM // 128
DIT = DI // 128
FC = 512
NFC = T // FC
TSC = 256
NSC = T // TSC
FF = 2048
NEG = -1.0e30
BF = ml_dtypes.bfloat16


class Sem:
    def __init__(self, h):
        self.h = h
        self.c = 0


class Plan:

    def __init__(self, nc, auto=None):
        self.nc = nc
        self.items = []
        self.auto = auto or {}

    def op(self, eng, fn, incs=()):
        sem = self.auto.get(eng)
        if sem is not None:
            self.items.append(("wait", eng, sem.h, sem.c))
            if not any(s is sem for s, _ in incs):
                incs = tuple(incs) + ((sem, 1),)
        for s, n in incs:
            s.c += n
        self.items.append(("op", eng, fn, tuple(incs)))

    def wait(self, eng, sem, val=None):
        self.items.append(("wait", eng, sem.h, sem.c if val is None else val))

    def run(self, eng_name):
        eng = getattr(self.nc, eng_name)
        for it in self.items:
            if it[0] == "op":
                _, e, fn, incs = it
                if e == eng_name:
                    ins = fn()
                    for s, n in incs:
                        ins = ins.then_inc(s.h, n)
            else:
                _, e, h, val = it
                if e == eng_name:
                    eng.wait_ge(h, val)


class Arena:
    def __init__(self, ap, cols):
        self.ap = ap
        self.cols = cols
        self.off = 0

    def f32(self, cols, rows=128):
        assert self.off + cols <= self.cols, f"arena overflow {self.off}+{cols}>{self.cols}"
        v = self.ap[0:rows, self.off:self.off + cols]
        self.off += cols
        return v

    def bf16(self, cols, rows=128):
        n32 = (cols + 1) // 2
        return self.f32(n32, rows=rows).bitcast(BF16)

    def mark(self):
        return self.off

    def reset(self, off=0):
        self.off = off


class Banks:
    def __init__(self, plan, banks):
        self.banks = [{"ap": b, "consumer": None} for b in banks]
        self.plan = plan
        self.i = 0

    def acquire(self):
        b = self.banks[self.i % len(self.banks)]
        self.i += 1
        if b["consumer"] is not None:
            sem, cnt = b["consumer"]
            self.plan.wait("tensor", sem, cnt)
        return b

    def release(self, b, sem):
        b["consumer"] = (sem, sem.c)


def _w_load_view(w, c0, cw, ktiles):
    return w[:, c0:c0 + cw].rearrange("(k p) c -> p k c", p=128)


def _mk_fc(nc, e, maincols):
    dram = nc.dram_tensor
    fc = {"nc": nc}
    fc["mainbuf"] = e(nc.sbuf_tensor("mainbuf", [128, maincols], F32)).ap()
    fc["perbuf"] = e(nc.sbuf_tensor("perbuf", [128, 9920], F32)).ap()
    fc["psb"] = [e(nc.psum_tensor(f"ps{i}", [128, 512], F32)).ap() for i in range(6)]
    fc["psT"] = [e(nc.psum_tensor(f"psT{i}", [128, 1024], BF16)).ap() for i in range(2)]
    for nm in ("qS", "qG", "qA", "qC", "eP", "eV", "eS", "eG"):
        fc[nm] = Sem(e(nc.semaphore(nm)))
    plan = Plan(nc)
    plan.auto = {"vector": fc["eV"], "scalar": fc["eS"]}
    fc["plan"] = plan
    fc["bk"] = Banks(plan, fc["psb"])
    fc["bkT"] = Banks(plan, fc["psT"])
    fc["M"] = Arena(fc["mainbuf"], maincols)
    fc["P"] = Arena(fc["perbuf"], 9920)
    fc["ones_col"] = dram("ones_col", [128, 1], F32, kind="ExternalInput").ap()
    fc["ones_row"] = dram("ones_row", [1, 128], F32, kind="ExternalInput").ap()
    fc["ident_f32"] = dram("ident_f32", [128, 128], F32, kind="ExternalInput").ap()
    fc["ident_b16"] = dram("ident_b16", [128, 128], BF16, kind="ExternalInput").ap()
    fc["tri_lt"] = dram("tri_lt", [128, 64], F32, kind="ExternalInput").ap()

    def dma(eng, q, dst, src):
        plan.op(eng, lambda d=dst, s=src, g=eng: getattr(nc, g).dma_start(out=d, in_=s), [(q, 16)])

    fc["dma"] = dma
    _ccols = {}

    def ccol(val):
        if val not in _ccols:
            ap = fc["P"].f32(1)
            plan.op("vector", lambda ap=ap, val=val: nc.vector.memset(ap, float(val)), [(fc["eV"], 1)])
            _ccols[val] = (ap, fc["eV"].c)
        return _ccols[val][0]

    def ccol_wait(eng, val):
        plan.wait(eng, fc["eV"], _ccols[val][1])

    fc["ccol"] = ccol
    fc["ccol_wait"] = ccol_wait
    return fc


def _run_block(fc, e):
    block = e(fc["nc"].Block())
    plan = fc["plan"]
    block.sync(lambda eng: plan.run("sync"))
    block.gpsimd(lambda eng: plan.run("gpsimd"))
    block.scalar(lambda eng: plan.run("scalar"))
    block.vector(lambda eng: plan.run("vector"))
    block.tensor(lambda eng: plan.run("tensor"))



def _emit_l0(C, dbg=(), standalone=True):
    nc = C["nc"]; plan = C["plan"]; M = C["M"]; P = C["P"]; bk = C["bk"]
    qS, qG, qA = C["qS"], C["qG"], C["qA"]
    eP, eV, eS = C["eP"], C["eV"], C["eS"]
    dma, ccol, ccol_wait = C["dma"], C["ccol"], C["ccol_wait"]
    ones_col, ones_row = C["ones_col"], C["ones_row"]
    ident_f32, tri_lt = C["ident_f32"], C["tri_lt"]
    dram = nc.dram_tensor

    xT = dram("xT", [DM, T], F32, kind="ExternalInput").ap()
    W_in = dram("W_in", [DM, 2 * DI], F32, kind="ExternalInput").ap()
    conv_w = dram("conv_w", [DI, 4], F32, kind="ExternalInput").ap()
    conv_bc = dram("conv_bc", [128, DIT], F32, kind="ExternalInput").ap()
    W_x = dram("W_x", [DI, 2 * S], F32, kind="ExternalInput").ap()
    W_dt = dram("W_dt", [DI, DI], F32, kind="ExternalInput").ap()
    b_dtc = dram("b_dtc", [128, DIT], F32, kind="ExternalInput").ap()
    A_log = dram("A_log", [DI, S], F32, kind="ExternalInput").ap()
    D_c = dram("D_c", [128, DIT], F32, kind="ExternalInput").ap()
    W_out = dram("W_out", [DI, DM], F32, kind="ExternalInput").ap()
    Wg = dram("Wg", [DM, 8], F32, kind="ExternalInput").ap()
    Ws1 = dram("Ws1", [DM, FF], F32, kind="ExternalInput").ap()
    Ws3 = dram("Ws3", [DM, FF], F32, kind="ExternalInput").ap()
    Ws2 = dram("Ws2", [FF, DM], F32, kind="ExternalInput").ap()
    We1 = dram("We1", [8, DM, FF], F32, kind="ExternalInput").ap()
    We3 = dram("We3", [8, DM, FF], F32, kind="ExternalInput").ap()
    We2 = dram("We2", [8, FF, DM], F32, kind="ExternalInput").ap()

    x1_out = dram("x1_out", [DM, T], F32, kind="ExternalOutput").ap() if standalone else None

    dbg_outs = {}

    def mkdbg(name, shape, dtyp=F32):
        if name in dbg:
            dbg_outs[name] = dram("dbg_" + name, shape, dtyp, kind="ExternalOutput").ap()
        return dbg_outs.get(name)

    d_xn = mkdbg("xn", [DM, T], BF16)
    d_xc = mkdbg("xc", [DI, T], BF16)
    d_dt = mkdbg("dt", [DI, T], BF16)
    d_gz = mkdbg("gz", [DI, T], BF16)
    d_bc = mkdbg("bc", [2 * S, T], BF16)
    d_mo = mkdbg("mo", [DM, T])
    d_w = mkdbg("w", [T, 8])
    d_gT = mkdbg("gT", [8, T])
    d_gtm = mkdbg("gtm", [128, 128])

    xn_sp = dram("xn_sp", [DM, T], BF16, kind="Internal").ap()
    xc_sp = dram("xc_sp", [DI, T], BF16, kind="Internal").ap()
    dt_sp = dram("dt_sp", [DI, T], BF16, kind="Internal").ap()
    gz_sp = dram("gz_sp", [DI, T], BF16, kind="Internal").ap()
    bc_sp = dram("bc_sp", [2 * S, T], BF16, kind="Internal").ap()
    mo_sp = dram("mo_sp", [DM, T], F32, kind="Internal").ap()

    if True:
        xn16 = P.bf16(DMT * T)
        A_sb = P.f32(DIT * S)
        conv_w_sb = P.f32(DIT * 4)
        conv_b_sb = P.f32(DIT)
        b_dt_sb = P.f32(DIT)
        D_sb = P.f32(DIT)
        carry = P.f32(DIT * S)
        ones_col_sb = P.f32(1)
        ones_row_sb = P.f32(128, rows=1)
        tmp_pa = P.f32(DIT * S)

        dma("gpsimd", qG, conv_w_sb.rearrange("p (k c) -> p k c", c=4),
            conv_w.rearrange("(k p) c -> p k c", p=128))
        dma("gpsimd", qG, conv_b_sb, conv_bc)
        dma("gpsimd", qG, b_dt_sb, b_dtc)
        dma("gpsimd", qG, D_sb, D_c)
        dma("gpsimd", qG, tmp_pa.rearrange("p (k s) -> p k s", s=S),
            A_log.rearrange("(k p) s -> p k s", p=128))
        dma("gpsimd", qG, ones_col_sb, ones_col)
        dma("gpsimd", qG, ones_row_sb, ones_row)
        init_loads = qG.c
        plan.wait("scalar", qG, init_loads)
        plan.op("scalar", lambda: nc.scalar.activation(out=A_sb, in_=tmp_pa, func=AF.Exp), [(eS, 1)])
        plan.wait("vector", eS)
        plan.op("vector", lambda: nc.vector.tensor_scalar_mul(A_sb, A_sb, -1.0), [(eV, 1)])
        aexp_done_v = eV.c
        plan.op("vector", lambda: nc.vector.memset(carry, 0.0), [(eV, 1)])
        carry_init_v = eV.c

        xtiles = [M.f32(T) for _ in range(DMT)]
        m_row = M.f32(T, rows=1)
        rstd_row = M.f32(T, rows=1)
        m_b = M.f32(T)
        rstd_b = M.f32(T)
        sqbuf = M.f32(T)

        for dmt in range(DMT):
            dma("sync", qS, xtiles[dmt], xT[dmt * 128:(dmt + 1) * 128, :])
        x_loaded = qS.c
        plan.wait("tensor", qS, x_loaded)

        mean_banks = []
        for fc in range(NFC):
            b = bk.acquire()
            mean_banks.append(b)
            for dmt in range(DMT):
                plan.op("tensor", lambda b=b, dmt=dmt, fc=fc: nc.tensor.matmul(
                    b["ap"][0:1, :], ones_col_sb, xtiles[dmt][:, fc * FC:(fc + 1) * FC],
                    start=(dmt == 0), stop=(dmt == DMT - 1)),
                    [(eP, 1)] if dmt == DMT - 1 else ())
        for fc in range(NFC):
            plan.wait("vector", eP)
            plan.op("vector", lambda fc=fc, b=mean_banks[fc]: nc.vector.tensor_scalar_mul(
                m_row[:, fc * FC:(fc + 1) * FC], b["ap"][0:1, :], 1.0 / DM), [(eV, 1)])
            bk.release(mean_banks[fc], eV)
        m_row_v = eV.c
        plan.wait("tensor", eV, m_row_v)
        for fc in range(NFC):
            b = bk.acquire()
            plan.op("tensor", lambda b=b, fc=fc: nc.tensor.matmul(
                b["ap"], ones_row_sb, m_row[:, fc * FC:(fc + 1) * FC], start=True, stop=True), [(eP, 1)])
            plan.wait("vector", eP)
            plan.op("vector", lambda b=b, fc=fc: nc.vector.tensor_copy(
                out=m_b[:, fc * FC:(fc + 1) * FC], in_=b["ap"]), [(eV, 1)])
            bk.release(b, eV)
        mb_done_v = eV.c
        for dmt in range(DMT):
            plan.op("vector", lambda dmt=dmt: nc.vector.tensor_sub(
                xtiles[dmt], xtiles[dmt], m_b), [(eV, 1)])
        xc_done_v = eV.c
        var_banks = [bk.acquire() for _ in range(NFC)]
        for dmt in range(DMT):
            plan.wait("scalar", eV, xc_done_v)
            if dmt > 0:
                plan.wait("scalar", eP)
            plan.op("scalar", lambda dmt=dmt: nc.scalar.activation(
                out=sqbuf, in_=xtiles[dmt], func=AF.Square), [(eS, 1)])
            plan.wait("tensor", eS)
            for fc in range(NFC):
                plan.op("tensor", lambda b=var_banks[fc], fc=fc, dmt=dmt: nc.tensor.matmul(
                    b["ap"][0:1, :], ones_col_sb, sqbuf[:, fc * FC:(fc + 1) * FC],
                    start=(dmt == 0), stop=(dmt == DMT - 1)),
                    [(eP, 1)])
        c_invdm = ccol(1.0 / DM)
        c_eps = ccol(1e-5)
        c_nhalf = ccol(-0.5)
        ccol_wait("scalar", 1.0 / DM)
        ccol_wait("scalar", 1e-5)
        ccol_wait("scalar", -0.5)
        plan.wait("scalar", eP)
        for fc in range(NFC):
            plan.op("scalar", lambda b=var_banks[fc], fc=fc: nc.scalar.activation(
                out=rstd_row[:, fc * FC:(fc + 1) * FC], in_=b["ap"][0:1, :], func=AF.Ln,
                scale=c_invdm[0:1, :], bias=c_eps[0:1, :]), [(eS, 1)])
        plan.wait("scalar", eS)
        plan.op("scalar", lambda: nc.scalar.activation(
            out=rstd_row, in_=rstd_row, func=AF.Exp, scale=c_nhalf[0:1, :]), [(eS, 1)])
        rstd_done_s = eS.c
        for fc in range(NFC):
            bk.release(var_banks[fc], eS)
        plan.wait("tensor", eS, rstd_done_s)
        for fc in range(NFC):
            b = bk.acquire()
            plan.op("tensor", lambda b=b, fc=fc: nc.tensor.matmul(
                b["ap"], ones_row_sb, rstd_row[:, fc * FC:(fc + 1) * FC], start=True, stop=True), [(eP, 1)])
            plan.wait("vector", eP)
            plan.op("vector", lambda b=b, fc=fc: nc.vector.tensor_copy(
                out=rstd_b[:, fc * FC:(fc + 1) * FC], in_=b["ap"]), [(eV, 1)])
            bk.release(b, eV)
        for dmt in range(DMT):
            plan.op("vector", lambda dmt=dmt: nc.vector.tensor_mul(
                xn16[:, dmt * T:(dmt + 1) * T], xtiles[dmt], rstd_b), [(eV, 1)])
            plan.wait("gpsimd", eV)
            dma("gpsimd", qG, xn_sp[dmt * 128:(dmt + 1) * 128, :], xn16[:, dmt * T:(dmt + 1) * T])
        xn_done_v = eV.c

        M.reset()
        wstg = [M.f32(DIT * 128) for _ in range(2)]
        wstg16 = [M.bf16(DIT * 128) for _ in range(2)]
        xc16 = M.bf16(DIT * T)
        xp_sb = M.f32(T)
        xcb_sb = M.f32(T)
        gz16_t = M.bf16(T)
        dt16_t = M.bf16(T)
        e_sb = M.f32(FC)
        bc_sb = M.bf16(T, rows=32)

        stg_free_v = [eV.c, eV.c]
        stg_free_p = [eP.c, eP.c]

        def stream_w(idx, wap, c0, cw, kt):
            i = idx % 2
            plan.wait("sync", eV, stg_free_v[i])
            dma("sync", qS, wstg[i][:, :kt * cw].rearrange("p (k c) -> p k c", c=cw),
                _w_load_view(wap, c0, cw, kt))
            plan.wait("vector", qS)
            plan.wait("vector", eP, stg_free_p[i])
            plan.op("vector", lambda i=i, n=kt * cw: nc.vector.tensor_copy(
                out=wstg16[i][:, :n], in_=wstg[i][:, :n]), [(eV, 1)])
            stg_free_v[i] = eV.c
            return i, eV.c

        def mm_chain(bankobj, w16ap, rhs_fn, nkt, fcw, wait_v, wcol=128):
            plan.wait("tensor", eV, wait_v)
            for kt in range(nkt):
                plan.op("tensor", lambda b=bankobj, w=w16ap, kt=kt, f=rhs_fn, fcw=fcw, wcol=wcol: nc.tensor.matmul(
                    b["ap"][0:wcol, :fcw], w[:, kt * wcol:(kt + 1) * wcol], f(kt),
                    start=(kt == 0), stop=(kt == nkt - 1)), [(eP, 1)] if kt == nkt - 1 else ())

        for ot in range(32):
            i, wv = stream_w(ot, W_in, ot * 128, 128, DMT)
            fcbanks = []
            for fc in range(NFC):
                b = bk.acquire()
                fcbanks.append(b)
                mm_chain(b, wstg16[i], lambda kt, fc=fc: xn16[:, kt * T + fc * FC: kt * T + fc * FC + FC],
                         DMT, FC, wv)
            stg_free_p[i] = eP.c
            if ot < DIT:
                dit = ot
                for fc in range(NFC):
                    plan.wait("scalar", eP)
                    plan.op("scalar", lambda b=fcbanks[fc], fc=fc: nc.scalar.copy(
                        out=xp_sb[:, fc * FC:(fc + 1) * FC], in_=b["ap"]), [(eS, 1)])
                    bk.release(fcbanks[fc], eS)
                xp_done = eS.c
                plan.wait("vector", eS, xp_done)
                plan.wait("vector", qG, init_loads)
                plan.op("vector", lambda dit=dit: nc.vector.tensor_scalar(
                    out=xcb_sb, in0=xp_sb, scalar1=conv_w_sb[:, dit * 4 + 3:dit * 4 + 4],
                    scalar2=conv_b_sb[:, dit:dit + 1], op0=ALU.mult, op1=ALU.add), [(eV, 1)])
                for k in (2, 1, 0):
                    sh = 3 - k
                    plan.op("vector", lambda dit=dit, k=k, sh=sh: nc.vector.scalar_tensor_tensor(
                        out=xcb_sb[:, sh:T], in0=xp_sb[:, 0:T - sh],
                        scalar=conv_w_sb[:, dit * 4 + k:dit * 4 + k + 1],
                        in1=xcb_sb[:, sh:T], op0=ALU.mult, op1=ALU.add), [(eV, 1)])
                conv_done = eV.c
                plan.wait("scalar", eV, conv_done)
                plan.op("scalar", lambda dit=dit: nc.scalar.activation(
                    out=xc16[:, dit * T:(dit + 1) * T], in_=xcb_sb, func=AF.Silu), [(eS, 1)])
                plan.wait("gpsimd", eS)
                dma("gpsimd", qG, xc_sp[dit * 128:(dit + 1) * 128, :], xc16[:, dit * T:(dit + 1) * T])
            else:
                zt = ot - DIT
                for fc in range(NFC):
                    plan.wait("scalar", eP)
                    if fc == 0:
                        plan.wait("scalar", qG)
                    plan.op("scalar", lambda b=fcbanks[fc], fc=fc: nc.scalar.activation(
                        out=gz16_t[:, fc * FC:(fc + 1) * FC], in_=b["ap"], func=AF.Silu), [(eS, 1)])
                    bk.release(fcbanks[fc], eS)
                plan.wait("gpsimd", eS)
                dma("gpsimd", qG, gz_sp[zt * 128:(zt + 1) * 128, :], gz16_t)

        plan.wait("tensor", eS)
        for ot in range(DIT):
            i, wv = stream_w(ot, W_dt, ot * 128, 128, DIT)
            for fc in range(NFC):
                b = bk.acquire()
                mm_chain(b, wstg16[i], lambda kt, fc=fc: xc16[:, kt * T + fc * FC: kt * T + fc * FC + FC],
                         DIT, FC, wv)
                if fc == NFC - 1:
                    stg_free_p[i] = eP.c
                plan.wait("scalar", eP)
                if fc == 0:
                    plan.wait("scalar", qG)
                plan.op("scalar", lambda b=b, ot=ot: nc.scalar.activation(
                    out=e_sb, in_=b["ap"], func=AF.Exp, bias=b_dt_sb[:, ot:ot + 1]), [(eS, 1)])
                bk.release(b, eS)
                plan.wait("scalar", eS)
                plan.op("scalar", lambda fc=fc: nc.scalar.activation(
                    out=dt16_t[:, fc * FC:(fc + 1) * FC], in_=e_sb, func=AF.Ln, bias=1.0), [(eS, 1)])
            plan.wait("gpsimd", eS)
            dma("gpsimd", qG, dt_sp[ot * 128:(ot + 1) * 128, :], dt16_t)

        i, wv = stream_w(0, W_x, 0, 2 * S, DIT)
        for fc in range(NFC):
            b = bk.acquire()
            mm_chain(b, wstg16[i],
                     lambda kt, fc=fc: xc16[:, kt * T + fc * FC: kt * T + fc * FC + FC],
                     DIT, FC, wv, wcol=2 * S)
            plan.wait("vector", eP)
            plan.op("vector", lambda b=b, fc=fc: nc.vector.tensor_copy(
                out=bc_sb[0:32, fc * FC:(fc + 1) * FC], in_=b["ap"][0:32, :]), [(eV, 1)])
            bk.release(b, eV)
        plan.wait("gpsimd", eV)
        dma("gpsimd", qG, bc_sp, bc_sb[0:32, :])
        bc_spilled = qG.c

        M.reset()
        eG = C["eG"]
        wo_stg = M.f32(DIT * 128)
        Wout16 = M.bf16(DMT * DIT * 128)
        Bb = M.bf16(S * TSC)
        Cb = M.bf16(S * TSC)
        dA = [M.bf16(S * TSC) for _ in range(2)]
        dtuB = [M.bf16(S * TSC) for _ in range(2)]
        hbuf = [M.bf16(S * TSC) for _ in range(2)]
        hC = [M.bf16(S * TSC) for _ in range(2)]
        fold16 = M.bf16(S)
        uc = wo_stg.bitcast(BF16)
        dtc = M.bf16(DIT * TSC)
        gzc = M.bf16(DIT * TSC)
        dtu2 = [M.bf16(TSC) for _ in range(2)]
        y_dit = M.f32(TSC)
        ygz16 = M.bf16(DIT * TSC)
        mo_t = [M.f32(TSC) for _ in range(2)]

        wo_cast_v = eV.c
        for ot in range(DMT):
            plan.wait("sync", eV, wo_cast_v)
            dma("sync", qS, wo_stg.rearrange("p (k c) -> p k c", c=128),
                _w_load_view(W_out, ot * 128, 128, DIT))
            plan.wait("vector", qS)
            plan.op("vector", lambda ot=ot: nc.vector.tensor_copy(
                out=Wout16[:, ot * DIT * 128:(ot + 1) * DIT * 128], in_=wo_stg), [(eV, 1)])
            wo_cast_v = eV.c

        def v3(ap, n=S, t=TSC):
            return ap.rearrange("p (s t) -> p s t", s=n)

        plan.wait("sync", qG)
        scans_done = {}
        carry_ready = {}
        fold_done = {}
        out_done = {}
        mo_free = [0, 0]
        for tc in range(NSC):
            c0 = tc * TSC
            plan.wait("sync", eV)
            plan.wait("sync", eS)
            dma("sync", qS, uc.rearrange("p (k t) -> p k t", t=TSC),
                xc_sp[:, c0:c0 + TSC].rearrange("(k p) t -> p k t", p=128))
            dma("sync", qS, dtc.rearrange("p (k t) -> p k t", t=TSC),
                dt_sp[:, c0:c0 + TSC].rearrange("(k p) t -> p k t", p=128))
            dma("sync", qS, gzc.rearrange("p (k t) -> p k t", t=TSC),
                gz_sp[:, c0:c0 + TSC].rearrange("(k p) t -> p k t", p=128))
            chunk_q = qS.c
            if tc == 0:
                plan.wait("gpsimd", qG, bc_spilled)
            plan.wait("gpsimd", eV)
            plan.wait("gpsimd", eG)
            dma("gpsimd", qG, v3(Bb), bc_sp[0:S, c0:c0 + TSC].unsqueeze(0).partition_broadcast(128))
            dma("gpsimd", qG, v3(Cb), bc_sp[S:2 * S, c0:c0 + TSC].unsqueeze(0).partition_broadcast(128))
            chunk_g = qG.c
            plan.wait("vector", qS, chunk_q)
            if tc == 0:
                plan.wait("vector", eV, max(aexp_done_v, carry_init_v))
                plan.wait("scalar", eV, aexp_done_v)

            hC_g = {}

            def st_prep(dit):
                pi = dit % 2
                plan.wait("scalar", qS, chunk_q)
                plan.wait("scalar", eV, scans_done.get(dit - 2, 0))
                plan.wait("scalar", eG, fold_done.get(dit - 2, 0))
                for s in range(S):
                    plan.op("scalar", lambda s=s, dit=dit, pi=pi: nc.scalar.activation(
                        out=dA[pi][:, s * TSC:(s + 1) * TSC],
                        in_=dtc[:, dit * TSC:(dit + 1) * TSC], func=AF.Exp,
                        scale=A_sb[:, dit * S + s:dit * S + s + 1]), [(eS, 1)])
                dA_s = eS.c
                plan.wait("gpsimd", eV, scans_done.get(dit - 2, 0))
                plan.wait("gpsimd", qG, chunk_g)
                plan.wait("gpsimd", qS, chunk_q)
                plan.op("gpsimd", lambda dit=dit, pi=pi: nc.gpsimd.tensor_mul(
                    dtu2[pi], dtc[:, dit * TSC:(dit + 1) * TSC],
                    uc[:, dit * TSC:(dit + 1) * TSC]), [(eG, 1)])
                plan.op("gpsimd", lambda dit=dit, pi=pi: nc.gpsimd.tensor_tensor(
                    out=v3(dtuB[pi]),
                    in0=dtu2[pi].unsqueeze(1).to_broadcast([128, S, TSC]),
                    in1=v3(Bb), op=ALU.mult), [(eG, 1)])
                plan.wait("gpsimd", eS, dA_s)
                plan.wait("gpsimd", eV, carry_ready.get(dit, 0))
                plan.op("gpsimd", lambda dit=dit, pi=pi: nc.gpsimd.tensor_mul(
                    fold16, v3(dA[pi])[:, :, 0:1].squeeze(2),
                    carry[:, dit * S:(dit + 1) * S]), [(eG, 1)])
                plan.op("gpsimd", lambda dit=dit, pi=pi: nc.gpsimd.tensor_add(
                    v3(dtuB[pi])[:, :, 0:1].squeeze(2),
                    v3(dtuB[pi])[:, :, 0:1].squeeze(2), fold16), [(eG, 1)])
                plan.op("gpsimd", lambda pi=pi: nc.gpsimd.memset(
                    v3(dA[pi])[:, :, 0:1].squeeze(2), 0.0), [(eG, 1)])
                fold_done[dit] = eG.c

            def st_hC(dit):
                pi = dit % 2
                plan.op("vector", lambda pi=pi: nc.vector.tensor_mul(
                    hC[pi], hbuf[pi], Cb), [(eV, 1)])

            def st_scan(dit):
                pi = dit % 2
                plan.wait("vector", eG, fold_done[dit])
                plan.op("vector", lambda dit=dit, pi=pi: nc.vector.tensor_tensor_scan(
                    out=hbuf[pi], data0=dA[pi], data1=dtuB[pi], initial=0.0,
                    op0=ALU.mult, op1=ALU.add), ())
                plan.op("vector", lambda dit=dit, pi=pi: nc.vector.tensor_copy(
                    out=carry[:, dit * S:(dit + 1) * S],
                    in_=v3(hbuf[pi])[:, :, TSC - 1:TSC].squeeze(2)), [(eV, 1)])
                carry_ready[dit] = eV.c
                carry_ready[(tc, dit)] = eV.c
                scans_done[dit] = eV.c

            def st_out(dit):
                pi = dit % 2
                uc_s = uc[:, dit * TSC:(dit + 1) * TSC]
                h = hC[pi]
                plan.op("vector", lambda h=h: nc.vector.tensor_reduce(
                    out=y_dit, in_=h.rearrange("p (s t) -> p t s", s=S),
                    axis=AX.X, op=ALU.add), [(eV, 1)])
                plan.op("vector", lambda dit=dit, uc_s=uc_s: nc.vector.scalar_tensor_tensor(
                    out=y_dit, in0=uc_s, scalar=D_sb[:, dit:dit + 1],
                    in1=y_dit, op0=ALU.mult, op1=ALU.add), [(eV, 1)])
                plan.op("vector", lambda dit=dit: nc.vector.tensor_mul(
                    ygz16[:, dit * TSC:(dit + 1) * TSC], y_dit,
                    gzc[:, dit * TSC:(dit + 1) * TSC]), [(eV, 1)])
                out_done[dit] = eV.c

            for dit in range(DIT):
                st_prep(dit)
                if dit >= 1:
                    st_hC(dit - 1)
                if dit >= 2:
                    st_out(dit - 2)
                st_scan(dit)
            st_hC(DIT - 1)
            st_out(DIT - 2)
            st_out(DIT - 1)
            ygz_v = eV.c
            for ot in range(DMT):
                b = bk.acquire()
                plan.wait("tensor", eV, ygz_v)
                for kt in range(DIT):
                    plan.op("tensor", lambda b=b, ot=ot, kt=kt: nc.tensor.matmul(
                        b["ap"][:, :TSC], Wout16[:, ot * DIT * 128 + kt * 128: ot * DIT * 128 + (kt + 1) * 128],
                        ygz16[:, kt * TSC:(kt + 1) * TSC],
                        start=(kt == 0), stop=(kt == DIT - 1)), [(eP, 1)] if kt == DIT - 1 else ())
                mi = ot % 2
                plan.wait("scalar", eP)
                plan.wait("scalar", qG, mo_free[mi])
                plan.op("scalar", lambda b=b, mi=mi: nc.scalar.copy(out=mo_t[mi], in_=b["ap"][:, :TSC]), [(eS, 1)])
                bk.release(b, eS)
                plan.wait("gpsimd", eS)
                dma("gpsimd", qG, mo_sp[ot * 128:(ot + 1) * 128, c0:c0 + TSC], mo_t[mi])
                mo_free[mi] = qG.c

        M.reset()
        x1_acc = M.f32(DMT * T)
        h1reg = M.f32(8 * T // 2)
        h1buf = h1reg.bitcast(BF16)
        w1stg = [M.f32(DMT * 128) for _ in range(3)]
        w3stg = [M.f32(DMT * 128) for _ in range(3)]
        w1c16 = [M.bf16(DMT * 128) for _ in range(3)]
        w3c16 = [M.bf16(DMT * 128) for _ in range(3)]
        wgstg = M.f32(DMT * 8)
        wgc16 = M.bf16(DMT * 8)
        w_b16 = M.bf16(T)
        s1_sb = M.f32(FC)
        tmp16 = M.bf16(FC)
        gatesT = h1reg[0:8, 0:T]
        gates_tm = M.f32(128)
        w_tm = M.f32(128)
        gt3 = M.f32(64)
        eq3 = M.f32(64)
        rank8 = M.f32(8)
        kw8 = M.f32(8)
        col1 = M.f32(1)
        tri_sb = M.f32(64)
        ident_sb = M.f32(128)
        mo_rd = h1reg[:, 0:T]
        w_row = mo_rd[0:1, :]

        plan.wait("sync", qG)
        dma("sync", qS, tri_sb, tri_lt)
        dma("sync", qS, ident_sb, ident_f32)
        for dmt in range(DMT):
            plan.wait("sync", eV)
            dma("sync", qS, x1_acc[:, dmt * T:(dmt + 1) * T], xT[dmt * 128:(dmt + 1) * 128, :])
            dma("sync", qS, mo_rd, mo_sp[dmt * 128:(dmt + 1) * 128, :])
            plan.wait("vector", qS)
            plan.op("vector", lambda dmt=dmt: nc.vector.tensor_add(
                x1_acc[:, dmt * T:(dmt + 1) * T], x1_acc[:, dmt * T:(dmt + 1) * T], mo_rd), [(eV, 1)])
        x1_init_v = eV.c

        dma("sync", qS, wgstg.rearrange("p (k c) -> p k c", c=8), _w_load_view(Wg, 0, 8, DMT))
        plan.wait("vector", qS)
        plan.op("vector", lambda: nc.vector.tensor_copy(out=wgc16, in_=wgstg), [(eV, 1)])
        wg_v = eV.c
        for fc in range(NFC):
            b = bk.acquire()
            mm_chain(b, wgc16, lambda kt, fc=fc: xn16[:, kt * T + fc * FC: kt * T + fc * FC + FC],
                     DMT, FC, wg_v, wcol=8)
            plan.wait("scalar", eP)
            plan.op("scalar", lambda b=b, fc=fc: nc.scalar.activation(
                out=gatesT[0:8, fc * FC:(fc + 1) * FC], in_=b["ap"][0:8, :], func=AF.Sigmoid), [(eS, 1)])
            bk.release(b, eS)
        gates_s = eS.c
        plan.wait("tensor", eS, gates_s)
        plan.wait("tensor", qS)
        bT = bk.acquire()
        for tt in range(16):
            plan.op("tensor", lambda tt=tt, b=bT: nc.tensor.transpose(
                out=b["ap"][:, tt * 8:(tt + 1) * 8], in_=gatesT[0:8, tt * 128:(tt + 1) * 128],
                identity=ident_sb[0:8, 0:8]), [(eP, 1)] if tt == 15 else ())
        plan.wait("vector", eP)
        plan.op("vector", lambda b=bT: nc.vector.tensor_copy(out=gates_tm, in_=b["ap"][:, 0:128]), [(eV, 1)])
        bk.release(bT, eV)
        if d_gT is not None:
            plan.wait("gpsimd", eS, gates_s)
            dma("gpsimd", qG, d_gT, gatesT)
        if d_gtm is not None:
            plan.wait("gpsimd", eV)
            dma("gpsimd", qG, d_gtm, gates_tm)
        plan.wait("vector", qS)
        for tt in range(16):
            g = gates_tm[:, tt * 8:(tt + 1) * 8]
            g_i = g.unsqueeze(1).to_broadcast([128, 8, 8])
            g_j = g.unsqueeze(2).to_broadcast([128, 8, 8])
            t3 = lambda ap: ap.rearrange("p (j i) -> p j i", j=8)
            plan.op("vector", lambda g_i=g_i, g_j=g_j: nc.vector.tensor_tensor(
                out=t3(gt3), in0=g_i, in1=g_j, op=ALU.is_gt), [(eV, 1)])
            plan.op("vector", lambda g_i=g_i, g_j=g_j: nc.vector.tensor_tensor(
                out=t3(eq3), in0=g_i, in1=g_j, op=ALU.is_equal), [(eV, 1)])
            plan.op("vector", lambda: nc.vector.tensor_mul(eq3, eq3, tri_sb), [(eV, 1)])
            plan.op("vector", lambda: nc.vector.tensor_add(gt3, gt3, eq3), [(eV, 1)])
            plan.op("vector", lambda: nc.vector.tensor_reduce(
                out=rank8, in_=t3(gt3), axis=AX.X, op=ALU.add), [(eV, 1)])
            plan.op("vector", lambda: nc.vector.tensor_scalar(
                out=rank8, in0=rank8, scalar1=1.5, scalar2=None, op0=ALU.is_le), [(eV, 1)])
            plan.op("vector", lambda g=g: nc.vector.tensor_mul(kw8, rank8, g), [(eV, 1)])
            plan.op("vector", lambda: nc.vector.tensor_reduce(
                out=col1, in_=kw8, axis=AX.X, op=ALU.add), [(eV, 1)])
            plan.op("vector", lambda: nc.vector.reciprocal(col1, col1), [(eV, 1)])
            plan.op("vector", lambda: nc.vector.tensor_scalar_mul(col1, col1, 1.0 / 3.0), [(eV, 1)])
            rden_v = eV.c
            plan.wait("vector", eV, rden_v)
            plan.op("vector", lambda tt=tt: nc.vector.tensor_scalar_mul(
                w_tm[:, tt * 8:(tt + 1) * 8], kw8, col1), [(eV, 1)])
        wtm_v = eV.c
        if d_w is not None:
            plan.wait("gpsimd", eV, wtm_v)
            dma("gpsimd", qG, d_w.rearrange("(tt p) e -> p tt e", p=128),
                w_tm.rearrange("p (tt e) -> p tt e", e=8))

        stg_v = {(n_, i_): 0 for n_ in ("w1", "w3") for i_ in range(3)}
        stg_p = {(n_, i_): 0 for n_ in ("w1", "w3") for i_ in range(3)}
        stg_i = {"w1": 0, "w3": 0}

        def stage(name, stgaps, c16aps, view, n):
            i = stg_i[name] % len(stgaps)
            stg_i[name] += 1
            plan.wait("sync", eV, stg_v[(name, i)])
            dma("sync", qS, stgaps[i][:, :n].rearrange("p (k c) -> p k c", c=128), view)
            plan.wait("vector", qS)
            plan.wait("vector", eP, stg_p[(name, i)])
            plan.op("vector", lambda a=c16aps[i], b=stgaps[i], n=n: nc.vector.tensor_copy(
                out=a[:, :n], in_=b[:, :n]), [(eV, 1)])
            stg_v[(name, i)] = eV.c
            return i, eV.c

        def stage_done(name, i):
            stg_p[(name, i)] = eP.c

        for ei in range(9):
            shared = (ei == 0)
            eidx = ei - 1
            if not shared:
                plan.wait("tensor", eV, wtm_v)
                for half4 in range(4):
                    bT = bk.acquire()
                    for q in range(4):
                        tt = half4 * 4 + q
                        plan.op("tensor", lambda tt=tt, q=q, b=bT, eidx=eidx: nc.tensor.transpose(
                            out=b["ap"][0:1, q * 128:(q + 1) * 128],
                            in_=w_tm[:, tt * 8 + eidx:tt * 8 + eidx + 1],
                            identity=ident_sb), [(eP, 1)] if q == 3 else ())
                    plan.wait("scalar", eP)
                    plan.wait("scalar", eV, x1_init_v)
                    plan.op("scalar", lambda b=bT, half4=half4: nc.scalar.copy(
                        out=w_row[:, half4 * FC:(half4 + 1) * FC], in_=b["ap"][0:1, :]), [(eS, 1)])
                    bk.release(bT, eS)
                wrow_s = eS.c
                plan.wait("tensor", eS, wrow_s)
                for fc in range(NFC):
                    b = bk.acquire()
                    plan.op("tensor", lambda b=b, fc=fc: nc.tensor.matmul(
                        b["ap"], ones_row_sb, w_row[:, fc * FC:(fc + 1) * FC],
                        start=True, stop=True), [(eP, 1)])
                    plan.wait("scalar", eP)
                    plan.op("scalar", lambda b=b, fc=fc: nc.scalar.copy(
                        out=w_b16[:, fc * FC:(fc + 1) * FC], in_=b["ap"]), [(eS, 1)])
                    bk.release(b, eS)
                wb_s = eS.c
            for half in range(2):
                for fsl in range(8):
                    fs = half * 8 + fsl
                    if shared:
                        v1 = _w_load_view(Ws1, fs * 128, 128, DMT)
                        v3w = _w_load_view(Ws3, fs * 128, 128, DMT)
                    else:
                        v1 = _w_load_view(We1[eidx], fs * 128, 128, DMT)
                        v3w = _w_load_view(We3[eidx], fs * 128, 128, DMT)
                    i1, r1 = stage("w1", w1stg, w1c16, v1, DMT * 128)
                    i3, r3 = stage("w3", w3stg, w3c16, v3w, DMT * 128)
                    for fc in range(NFC):
                        ba = bk.acquire()
                        mm_chain(ba, w1c16[i1], lambda kt, fc=fc: xn16[:, kt * T + fc * FC: kt * T + fc * FC + FC],
                                 DMT, FC, r1)
                        bb = bk.acquire()
                        mm_chain(bb, w3c16[i3], lambda kt, fc=fc: xn16[:, kt * T + fc * FC: kt * T + fc * FC + FC],
                                 DMT, FC, r3)
                        plan.wait("scalar", eP)
                        plan.op("scalar", lambda ba=ba: nc.scalar.activation(
                            out=s1_sb, in_=ba["ap"], func=AF.Silu), [(eS, 1)])
                        bk.release(ba, eS)
                        s1_s = eS.c
                        plan.wait("vector", eS, s1_s)
                        dst = h1buf[:, fsl * T + fc * FC: fsl * T + fc * FC + FC]
                        if shared:
                            plan.op("vector", lambda bb=bb, dst=dst: nc.vector.scalar_tensor_tensor(
                                out=dst, in0=s1_sb, scalar=1.0 / 3.0, in1=bb["ap"],
                                op0=ALU.mult, op1=ALU.mult), [(eV, 1)])
                        else:
                            plan.op("vector", lambda bb=bb: nc.vector.tensor_mul(
                                tmp16, s1_sb, bb["ap"]), [(eV, 1)])
                            plan.wait("vector", eV)
                            plan.wait("vector", eS, wb_s)
                            plan.op("vector", lambda dst=dst, fc=fc: nc.vector.tensor_mul(
                                dst, tmp16, w_b16[:, fc * FC:(fc + 1) * FC]), [(eV, 1)])
                        bk.release(bb, eV)
                    stage_done("w1", i1)
                    stage_done("w3", i3)
                h1_v = eV.c
                for ot in range(DMT):
                    if shared:
                        v2 = Ws2[half * 1024:(half + 1) * 1024, ot * 128:(ot + 1) * 128].rearrange(
                            "(k p) c -> p k c", p=128)
                    else:
                        v2 = We2[eidx][half * 1024:(half + 1) * 1024, ot * 128:(ot + 1) * 128].rearrange(
                            "(k p) c -> p k c", p=128)
                    i2, r2 = stage("w1", w1stg, w1c16, v2, DMT * 128)
                    for tc4 in range(NFC):
                        b = bk.acquire()
                        mm_chain(b, w1c16[i2], lambda kt, tc4=tc4: h1buf[:, kt * T + tc4 * FC: kt * T + tc4 * FC + FC],
                                 8, FC, max(r2, h1_v))
                        plan.wait("vector", eP)
                        plan.wait("vector", eV, x1_init_v)
                        plan.op("vector", lambda b=b, ot=ot, tc4=tc4: nc.vector.tensor_add(
                            x1_acc[:, ot * T + tc4 * FC: ot * T + tc4 * FC + FC],
                            x1_acc[:, ot * T + tc4 * FC: ot * T + tc4 * FC + FC],
                            b["ap"]), [(eV, 1)])
                        bk.release(b, eV)
                    stage_done("w1", i2)

        if standalone:
            plan.wait("gpsimd", eV)
            for dmt in range(DMT):
                dma("gpsimd", qG, x1_out[dmt * 128:(dmt + 1) * 128, :], x1_acc[:, dmt * T:(dmt + 1) * T])

        plan.wait("gpsimd", qG)
        plan.wait("gpsimd", qS)
        if d_xn is not None:
            dma("gpsimd", qG, d_xn, xn_sp)
        if d_xc is not None:
            dma("gpsimd", qG, d_xc, xc_sp)
        if d_dt is not None:
            dma("gpsimd", qG, d_dt, dt_sp)
        if d_gz is not None:
            dma("gpsimd", qG, d_gz, gz_sp)
        if d_bc is not None:
            dma("gpsimd", qG, d_bc, bc_sp)
        if d_mo is not None:
            dma("gpsimd", qG, d_mo, mo_sp)

        plan.wait("gpsimd", qG)
        plan.wait("gpsimd", qS)

    return {"x1_acc": x1_acc}


def build_l0(dbg=()):
    nc = bass.Bass("TRN2", target_bir_lowering=False)
    with ExitStack() as ctx:
        fc = _mk_fc(nc, ctx.enter_context, maincols=43264)
        _emit_l0(fc, dbg, standalone=True)
        _run_block(fc, ctx.enter_context)
    return nc


def _emit_l1(C, dbg=(), standalone=True):
    nc = C["nc"]; plan = C["plan"]; M = C["M"]; P = C["P"]
    bk = C["bk"]; bkT = C["bkT"]
    qS, qG, qC = C["qS"], C["qG"], C["qC"]
    eP, eV, eS = C["eP"], C["eV"], C["eS"]
    dma, ccol, ccol_wait = C["dma"], C["ccol"], C["ccol_wait"]
    ones_col, ones_row = C["ones_col"], C["ones_row"]
    ident_f32, ident_b16, tri_lt = C["ident_f32"], C["ident_b16"], C["tri_lt"]
    dram = nc.dram_tensor
    M.reset(0)

    x1T = dram("x1T", [DM, T], F32, kind="ExternalInput").ap() if standalone else None
    Wq = dram("Wq", [DM, DM], F32, kind="ExternalInput").ap()
    Wk = dram("Wk", [DM, DM], F32, kind="ExternalInput").ap()
    Wv = dram("Wv", [DM, DM], F32, kind="ExternalInput").ap()
    Wo = dram("Wo", [DM, DM], F32, kind="ExternalInput").ap()
    bq_c = dram("bq_c", [128, DMT], F32, kind="ExternalInput").ap()
    bk_c = dram("bk_c", [128, DMT], F32, kind="ExternalInput").ap()
    bv_c = dram("bv_c", [128, DMT], F32, kind="ExternalInput").ap()
    bo_c = dram("bo_c", [128, DMT], F32, kind="ExternalInput").ap()
    Mmask = dram("Mmask", [128, 128], F32, kind="ExternalInput").ap()
    nw_c = dram("nw_c", [128, DMT], F32, kind="ExternalInput").ap()
    gateWT = dram("gateWT", [DM, 8], BF16, kind="ExternalInput").ap()
    gateB = dram("gateB", [8, 1], BF16, kind="ExternalInput").ap()
    m1gT = dram("m1gT", [8, DM, DM], BF16, kind="ExternalInput").ap()
    m1lT = dram("m1lT", [8, DM, DM], BF16, kind="ExternalInput").ap()
    b1g_c = dram("b1g_c", [8, 128, DMT], BF16, kind="ExternalInput").ap()
    b1l_c = dram("b1l_c", [8, 128, DMT], BF16, kind="ExternalInput").ap()
    m2T = dram("m2T", [8, DM, DM], BF16, kind="ExternalInput").ap()
    b2_c = dram("b2_c", [8, 128, DMT], BF16, kind="ExternalInput").ap()

    x2_out = dram("x2_out", [DM, T], F32, kind="ExternalOutput").ap()

    dbg_outs = {}

    def mkdbg(name, shape, dtyp=F32):
        if name in dbg:
            dbg_outs[name] = dram("dbg_" + name, shape, dtyp, kind="ExternalOutput").ap()
        return dbg_outs.get(name)

    d_xn1 = mkdbg("xn1", [DM, T], BF16)
    d_at = mkdbg("at", [DM, T])
    d_nrm = mkdbg("nrm", [DM, T], BF16)
    d_cw = mkdbg("cw", [T, 8])

    xn1_sp = dram("xn1_sp", [DM, T], BF16, kind="Internal").ap()

    if True:
        ones_col_sb = P.f32(1)
        ones_row_sb = P.f32(128, rows=1)
        ident_sb = P.f32(128)
        identb_sb = P.bf16(128)
        tri_sb = P.f32(64)
        mask_sb = P.f32(128)
        bq_sb = P.f32(DMT)
        bk_sb = P.f32(DMT)
        bv_sb = P.f32(DMT)
        bo_sb = P.f32(DMT)
        nw_sb = P.f32(DMT)
        gb_sb = P.f32(1, rows=8)
        gb16_sb = P.bf16(2, rows=8)[:, 0:1]

        dma("gpsimd", qG, ones_col_sb, ones_col)
        dma("gpsimd", qG, ones_row_sb, ones_row)
        dma("gpsimd", qG, ident_sb, ident_f32)
        dma("gpsimd", qG, identb_sb, ident_b16)
        dma("gpsimd", qG, tri_sb, tri_lt)
        dma("gpsimd", qG, mask_sb, Mmask)
        dma("gpsimd", qG, bq_sb, bq_c)
        dma("gpsimd", qG, bk_sb, bk_c)
        dma("gpsimd", qG, bv_sb, bv_c)
        dma("gpsimd", qG, bo_sb, bo_c)
        dma("gpsimd", qG, nw_sb, nw_c)
        dma("gpsimd", qG, gb16_sb, gateB)
        init_q = qG.c
        plan.wait("vector", qG, init_q)
        plan.op("vector", lambda: nc.vector.tensor_copy(out=gb_sb, in_=gb16_sb), [(eV, 1)])

        x1g = M.f32(DMT * T)
        xn1 = M.bf16(DMT * T)
        ph0 = M.mark()

        m_b = M.f32(T)
        rstd_b = M.f32(T)
        sqbuf = M.f32(T)
        m_row = m_b[0:1, :]
        rstd_row = rstd_b[0:1, :]

        if standalone:
            for dmt in range(DMT):
                dma("sync", qS, x1g[:, dmt * T:(dmt + 1) * T], x1T[dmt * 128:(dmt + 1) * 128, :])
            x_loaded = qS.c
            plan.wait("tensor", qS, x_loaded)
            plan.wait("vector", qS, x_loaded)
        else:
            x1_acc = C["x1_acc"]
            a2a_in = dram("a2a_in", [NCORES * DM, T // NCORES], F32, kind="Internal").ap()
            a2a_out = dram("a2a_out", [NCORES * DM, T // NCORES], F32, kind="Internal").ap()
            a2a_in_v = a2a_in.rearrange("(j d) b -> j d b", j=NCORES)
            a2a_out_v = a2a_out.rearrange("(l d) b -> l d b", l=NCORES)
            plan.wait("sync", eV)
            for dmt in range(DMT):
                dma("sync", qS, a2a_in_v[:, dmt * 128:(dmt + 1) * 128, :].transpose([1, 0, 2]),
                    x1_acc[:, dmt * T:(dmt + 1) * T].rearrange("p (j b) -> p j b", j=NCORES))
            plan.wait("gpsimd", qS)

            def _cc():
                with allow_ici_all_to_all():
                    return nc.gpsimd.collective_compute(
                        "AllToAll", ALU.bypass, replica_groups=[list(range(NCORES))],
                        ins=[a2a_in.opt()], outs=[a2a_out.opt()])

            plan.op("gpsimd", _cc, [(qC, 1)])
            plan.wait("gpsimd", qC)
            for dmt in range(DMT):
                dma("gpsimd", qG, x1g[:, dmt * T:(dmt + 1) * T].rearrange("p (l b) -> p l b", l=NCORES),
                    a2a_out_v[:, dmt * 128:(dmt + 1) * 128, :].transpose([1, 0, 2]))
            x_loaded = qG.c
            plan.wait("tensor", qG, x_loaded)
            plan.wait("vector", qG, x_loaded)
            plan.wait("scalar", qG, x_loaded)
        mean_banks = []
        for fc in range(NFC):
            b = bk.acquire()
            mean_banks.append(b)
            for dmt in range(DMT):
                plan.op("tensor", lambda b=b, dmt=dmt, fc=fc: nc.tensor.matmul(
                    b["ap"][0:1, :], ones_col_sb, x1g[:, dmt * T + fc * FC: dmt * T + fc * FC + FC],
                    start=(dmt == 0), stop=(dmt == DMT - 1)),
                    [(eP, 1)] if dmt == DMT - 1 else ())
        for fc in range(NFC):
            plan.wait("vector", eP)
            plan.op("vector", lambda fc=fc, b=mean_banks[fc]: nc.vector.tensor_scalar_mul(
                m_row[:, fc * FC:(fc + 1) * FC], b["ap"][0:1, :], 1.0 / DM), [(eV, 1)])
            bk.release(mean_banks[fc], eV)
        m_row_v = eV.c
        plan.wait("tensor", eV, m_row_v)
        for fc in range(NFC):
            b = bk.acquire()
            plan.op("tensor", lambda b=b, fc=fc: nc.tensor.matmul(
                b["ap"], ones_row_sb, m_row[:, fc * FC:(fc + 1) * FC], start=True, stop=True), [(eP, 1)])
            plan.wait("vector", eP)
            plan.op("vector", lambda b=b, fc=fc: nc.vector.tensor_copy(
                out=m_b[:, fc * FC:(fc + 1) * FC], in_=b["ap"]), [(eV, 1)])
            bk.release(b, eV)
        mb_v = eV.c
        var_banks = [bk.acquire() for _ in range(NFC)]
        for dmt in range(DMT):
            plan.wait("vector", eV, mb_v)
            if dmt > 0:
                plan.wait("vector", eP)
            plan.op("vector", lambda dmt=dmt: nc.vector.tensor_sub(
                sqbuf, x1g[:, dmt * T:(dmt + 1) * T], m_b), [(eV, 1)])
            plan.wait("scalar", eV)
            plan.op("scalar", lambda: nc.scalar.activation(
                out=sqbuf, in_=sqbuf, func=AF.Square), [(eS, 1)])
            plan.wait("tensor", eS)
            for fc in range(NFC):
                plan.op("tensor", lambda b=var_banks[fc], fc=fc, dmt=dmt: nc.tensor.matmul(
                    b["ap"][0:1, :], ones_col_sb, sqbuf[:, fc * FC:(fc + 1) * FC],
                    start=(dmt == 0), stop=(dmt == DMT - 1)), [(eP, 1)])
        c_invdm = ccol(1.0 / DM)
        c_eps = ccol(1e-5)
        c_nhalf = ccol(-0.5)
        plan.wait("scalar", eV)
        plan.wait("scalar", eP)
        for fc in range(NFC):
            plan.op("scalar", lambda b=var_banks[fc], fc=fc: nc.scalar.activation(
                out=rstd_row[:, fc * FC:(fc + 1) * FC], in_=b["ap"][0:1, :], func=AF.Ln,
                scale=c_invdm[0:1, :], bias=c_eps[0:1, :]), [(eS, 1)])
        plan.op("scalar", lambda: nc.scalar.activation(
            out=rstd_row, in_=rstd_row, func=AF.Exp, scale=c_nhalf[0:1, :]), [(eS, 1)])
        rstd_s = eS.c
        for fc in range(NFC):
            bk.release(var_banks[fc], eS)
        plan.wait("tensor", eS, rstd_s)
        for fc in range(NFC):
            b = bk.acquire()
            plan.op("tensor", lambda b=b, fc=fc: nc.tensor.matmul(
                b["ap"], ones_row_sb, rstd_row[:, fc * FC:(fc + 1) * FC], start=True, stop=True), [(eP, 1)])
            plan.wait("vector", eP)
            plan.op("vector", lambda b=b, fc=fc: nc.vector.tensor_copy(
                out=rstd_b[:, fc * FC:(fc + 1) * FC], in_=b["ap"]), [(eV, 1)])
            bk.release(b, eV)
        for dmt in range(DMT):
            plan.op("vector", lambda dmt=dmt: nc.vector.tensor_sub(
                sqbuf, x1g[:, dmt * T:(dmt + 1) * T], m_b), [(eV, 1)])
            plan.op("vector", lambda dmt=dmt: nc.vector.tensor_mul(
                xn1[:, dmt * T:(dmt + 1) * T], sqbuf, rstd_b), [(eV, 1)])
            plan.wait("gpsimd", eV)
            dma("gpsimd", qG, xn1_sp[dmt * 128:(dmt + 1) * 128, :], xn1[:, dmt * T:(dmt + 1) * T])
        xn1_v = eV.c
        if d_xn1 is not None:
            plan.wait("gpsimd", qG)
            dma("gpsimd", qG, d_xn1, xn1_sp)

        M.reset(ph0)
        wstgf = M.f32(DMT * 128)
        wstg16 = M.bf16(DMT * 128)
        q16 = M.bf16(T)
        k16 = M.bf16(T)
        v16 = M.bf16(T)
        atn16 = M.bf16(T)
        sf_all = M.f32(T)
        mx16 = M.f32(16)
        pt_all = M.bf16(T)
        ptT_all = M.bf16(T)
        vt_all = M.bf16(1024)

        c_scale = ccol(0.125)

        def bmaj_out(ap, fc):
            return ap.rearrange("p (b l) -> p l b", l=8)[:, 2 * fc:2 * fc + 2, :]

        stw_v, stw_p = xn1_v, 0
        for hg in range(DMT):
            outs = []
            for (wap, bias_sb, dst) in ((Wq, bq_sb, q16), (Wk, bk_sb, k16), (Wv, bv_sb, v16)):
                plan.wait("sync", eV, stw_v)
                dma("sync", qS, wstgf.rearrange("p (k c) -> p k c", c=128),
                    _w_load_view(wap, hg * 128, 128, DMT))
                plan.wait("vector", qS)
                plan.wait("vector", eP, stw_p)
                plan.op("vector", lambda: nc.vector.tensor_copy(out=wstg16, in_=wstgf), [(eV, 1)])
                stw_v = eV.c
                plan.wait("tensor", eV, stw_v)
                plan.wait("tensor", eV, xn1_v)
                for fc in range(NFC):
                    b = bk.acquire()
                    for kt in range(DMT):
                        plan.op("tensor", lambda b=b, kt=kt, fc=fc: nc.tensor.matmul(
                            b["ap"], wstg16[:, kt * 128:(kt + 1) * 128],
                            xn1[:, kt * T + fc * FC: kt * T + fc * FC + FC],
                            start=(kt == 0), stop=(kt == DMT - 1)), [(eP, 1)] if kt == DMT - 1 else ())
                    plan.wait("scalar", eP)
                    plan.op("scalar", lambda b=b, dst=dst, fc=fc, bias_sb=bias_sb, hg=hg: nc.scalar.activation(
                        out=bmaj_out(dst, fc), in_=b["ap"], func=AF.Identity,
                        bias=bias_sb[:, hg:hg + 1]), [(eS, 1)])
                    bk.release(b, eS)
                stw_p = eP.c
            qkv_s = eS.c
            for h2 in range(2):
                lo, hi = h2 * 64, h2 * 64 + 64
                idnt = identb_sb[lo:hi, 0:64] if lo == 0 else identb_sb[lo:hi, 64:128]
                sbanks = []
                for b4 in range(4):
                    b = bk.acquire()
                    sbanks.append(b)
                    plan.wait("tensor", eS, qkv_s)
                    for q4 in range(4):
                        qt = b4 * 4 + q4
                        plan.op("tensor", lambda b=b, lo=lo, hi=hi, qt=qt, q4=q4: nc.tensor.matmul(
                            b["ap"][:, q4 * 128:(q4 + 1) * 128], q16[lo:hi, qt * 128:(qt + 1) * 128],
                            k16[lo:hi, qt * 128:(qt + 1) * 128], start=True, stop=True),
                            [(eP, 1)] if q4 == 3 else ())
                plan.wait("vector", eP)
                plan.wait("vector", qG, init_q)
                for b4 in range(4):
                    plan.op("vector", lambda b=sbanks[b4], b4=b4: nc.vector.tensor_tensor(
                        out=sf_all.rearrange("p (q c) -> p q c", c=128)[:, b4 * 4:(b4 + 1) * 4, :],
                        in0=b["ap"].rearrange("p (q c) -> p q c", c=128),
                        in1=mask_sb.unsqueeze(1).to_broadcast([128, 4, 128]),
                        op=ALU.add), [(eV, 1)])
                    bk.release(sbanks[b4], eV)
                plan.op("vector", lambda: nc.vector.tensor_reduce(
                    out=mx16, in_=sf_all.rearrange("p (q c) -> p q c", c=128),
                    axis=AX.X, op=ALU.max), [(eV, 1)])
                plan.op("vector", lambda: nc.vector.tensor_tensor(
                    out=sf_all.rearrange("p (q c) -> p q c", c=128),
                    in0=sf_all.rearrange("p (q c) -> p q c", c=128),
                    in1=mx16.unsqueeze(2).to_broadcast([128, 16, 128]),
                    op=ALU.subtract), [(eV, 1)])
                sub_v = eV.c
                plan.wait("scalar", eV, sub_v)
                plan.op("scalar", lambda: nc.scalar.activation(
                    out=sf_all, in_=sf_all, func=AF.Exp, scale=c_scale[:, :]), [(eS, 1)])
                exp_s = eS.c
                plan.wait("vector", eS, exp_s)
                plan.op("vector", lambda: nc.vector.tensor_reduce(
                    out=mx16, in_=sf_all.rearrange("p (q c) -> p q c", c=128),
                    axis=AX.X, op=ALU.add), [(eV, 1)])
                plan.op("vector", lambda: nc.vector.reciprocal(mx16, mx16), [(eV, 1)])
                plan.op("vector", lambda: nc.vector.tensor_tensor(
                    out=pt_all.rearrange("p (q c) -> p q c", c=128),
                    in0=sf_all.rearrange("p (q c) -> p q c", c=128),
                    in1=mx16.unsqueeze(2).to_broadcast([128, 16, 128]),
                    op=ALU.mult), [(eV, 1)])
                p_v = eV.c
                for half8 in range(2):
                    bT = bkT.acquire()
                    plan.wait("tensor", eV, p_v)
                    for t8 in range(8):
                        qt = half8 * 8 + t8
                        plan.op("tensor", lambda bT=bT, qt=qt, t8=t8: nc.tensor.transpose(
                            out=bT["ap"][:, t8 * 128:(t8 + 1) * 128],
                            in_=pt_all[:, qt * 128:(qt + 1) * 128], identity=identb_sb),
                            [(eP, 1)] if t8 == 7 else ())
                    plan.wait("vector", eP)
                    plan.op("vector", lambda bT=bT, half8=half8: nc.vector.tensor_copy(
                        out=ptT_all[:, half8 * 1024:(half8 + 1) * 1024], in_=bT["ap"][:, 0:1024]), [(eV, 1)])
                    bkT.release(bT, eV)
                bT = bkT.acquire()
                plan.wait("tensor", eS, qkv_s)
                for qt in range(16):
                    plan.op("tensor", lambda bT=bT, lo=lo, hi=hi, qt=qt, idnt=idnt: nc.tensor.transpose(
                        out=bT["ap"][:, qt * 64:(qt + 1) * 64],
                        in_=v16[lo:hi, qt * 128:(qt + 1) * 128], identity=idnt),
                        [(eP, 1)] if qt == 15 else ())
                plan.wait("vector", eP)
                plan.op("vector", lambda bT=bT: nc.vector.tensor_copy(
                    out=vt_all, in_=bT["ap"][:, 0:1024]), [(eV, 1)])
                bkT.release(bT, eV)
                pv_v = eV.c
                for b4 in range(4):
                    b = bk.acquire()
                    plan.wait("tensor", eV, pv_v)
                    for q4 in range(4):
                        qt = b4 * 4 + q4
                        plan.op("tensor", lambda b=b, qt=qt, q4=q4: nc.tensor.matmul(
                            b["ap"][0:64, q4 * 128:(q4 + 1) * 128], vt_all[:, qt * 64:(qt + 1) * 64],
                            ptT_all[:, qt * 128:(qt + 1) * 128], start=True, stop=True),
                            [(eP, 1)] if q4 == 3 else ())
                    plan.wait("scalar", eP)
                    plan.op("scalar", lambda b=b, lo=lo, hi=hi, b4=b4: nc.scalar.copy(
                        out=atn16[lo:hi, b4 * 512:(b4 + 1) * 512], in_=b["ap"][0:64, 0:512]), [(eS, 1)])
                    bk.release(b, eS)
            atn_s = eS.c
            plan.wait("sync", eV, stw_v)
            dma("sync", qS, wstgf.rearrange("p (k c) -> p k c", c=128),
                Wo[hg * 128:(hg + 1) * 128, :].rearrange("p (k c) -> p k c", c=128))
            plan.wait("vector", qS)
            plan.wait("vector", eP, stw_p)
            plan.op("vector", lambda: nc.vector.tensor_copy(out=wstg16, in_=wstgf), [(eV, 1)])
            stw_v = eV.c
            plan.wait("tensor", eV, stw_v)
            plan.wait("tensor", eS, atn_s)
            for ot in range(DMT):
                for fc in range(NFC):
                    b = bk.acquire()
                    plan.op("tensor", lambda b=b, ot=ot, fc=fc: nc.tensor.matmul(
                        b["ap"], wstg16[:, ot * 128:(ot + 1) * 128],
                        atn16[:, fc * FC:(fc + 1) * FC], start=True, stop=True), [(eP, 1)])
                    plan.wait("vector", eP)

                    def x1gv(ot=ot, fc=fc):
                        return x1g[:, ot * T:(ot + 1) * T].rearrange(
                            "p (l b) -> p b l", l=8)[:, 64 * fc:64 * (fc + 1), :]

                    plan.op("vector", lambda b=b, x1gv=x1gv: nc.vector.tensor_tensor(
                        out=x1gv(), in0=x1gv(),
                        in1=b["ap"].rearrange("p (b l) -> p b l", l=8),
                        op=ALU.add), [(eV, 1)])
                    bk.release(b, eV)
            stw_p = eP.c
        for dmt in range(DMT):
            plan.op("vector", lambda dmt=dmt: nc.vector.tensor_scalar_add(
                x1g[:, dmt * T:(dmt + 1) * T], x1g[:, dmt * T:(dmt + 1) * T],
                bo_sb[:, dmt:dmt + 1]), [(eV, 1)])
        attn_done_v = eV.c
        if d_at is not None:
            plan.wait("gpsimd", eV, attn_done_v)
            for dmt in range(DMT):
                dma("gpsimd", qG, d_at[dmt * 128:(dmt + 1) * 128, :], x1g[:, dmt * T:(dmt + 1) * T])

        M.reset(ph0)
        sqb = M.f32(T)
        rms_b = M.f32(T)
        rms_row = rms_b[0:1, :]
        w1g_16 = M.bf16(DMT * 128)
        w1l_16 = M.bf16(DMT * 128)
        w2_16 = M.bf16(DMT * 128)
        pg_all = M.f32(T)
        pl_all = M.f32(T)
        sg_all = M.f32(T)
        actreg = M.f32(T)
        act16 = [actreg.bitcast(BF16), M.bf16(2 * T)]
        glT = actreg[0:8, :]
        normed = xn1
        cb16 = rms_b[:, 0:T // 2].bitcast(BF16)
        wg16 = rms_b[:, T // 2:T // 2 + DMT * 4].bitcast(BF16)
        b1g_sb = P.f32(DMT)
        b1l_sb = P.f32(DMT)
        b2_sb = P.f32(DMT)
        b1g16 = P.bf16(DMT)
        b1l16 = P.bf16(DMT)
        b216 = P.bf16(DMT)
        tmp16g = M.bf16(FC, rows=8)
        g_tm = M.f32(128)
        c_tm = M.f32(128)
        gt3 = M.f32(64)
        eq3 = M.f32(64)
        rank8 = M.f32(8)
        kw8 = M.f32(8)
        col1 = M.f32(1)
        colv1 = M.f32(1)
        w_row = sqb[0:1, :]

        w1p_st = {"p": stw_p}
        stw_p_st = {"p": stw_p}
        var_banks = [bk.acquire() for _ in range(NFC)]
        for dmt in range(DMT):
            plan.wait("scalar", eP)
            plan.wait("scalar", eV, xn1_v)
            plan.op("scalar", lambda dmt=dmt: nc.scalar.activation(
                out=sqb, in_=xn1[:, dmt * T:(dmt + 1) * T], func=AF.Square), [(eS, 1)])
            plan.wait("tensor", eS)
            for fc in range(NFC):
                plan.op("tensor", lambda b=var_banks[fc], fc=fc, dmt=dmt: nc.tensor.matmul(
                    b["ap"][0:1, :], ones_col_sb, sqb[:, fc * FC:(fc + 1) * FC],
                    start=(dmt == 0), stop=(dmt == DMT - 1)), [(eP, 1)])
        plan.wait("scalar", eP)
        for fc in range(NFC):
            plan.op("scalar", lambda b=var_banks[fc], fc=fc: nc.scalar.activation(
                out=rms_row[:, fc * FC:(fc + 1) * FC], in_=b["ap"][0:1, :], func=AF.Ln,
                scale=c_invdm[0:1, :], bias=c_eps[0:1, :]), [(eS, 1)])
        plan.op("scalar", lambda: nc.scalar.activation(
            out=rms_row, in_=rms_row, func=AF.Exp, scale=c_nhalf[0:1, :]), [(eS, 1)])
        rms_s = eS.c
        for fc in range(NFC):
            bk.release(var_banks[fc], eS)
        plan.wait("tensor", eS, rms_s)
        for fc in range(NFC):
            b = bk.acquire()
            plan.op("tensor", lambda b=b, fc=fc: nc.tensor.matmul(
                b["ap"], ones_row_sb, rms_row[:, fc * FC:(fc + 1) * FC], start=True, stop=True), [(eP, 1)])
            plan.wait("vector", eP)
            plan.op("vector", lambda b=b, fc=fc: nc.vector.tensor_copy(
                out=rms_b[:, fc * FC:(fc + 1) * FC], in_=b["ap"]), [(eV, 1)])
            bk.release(b, eV)
        plan.wait("vector", qG, init_q)
        for dmt in range(DMT):
            plan.op("vector", lambda dmt=dmt: nc.vector.tensor_add(
                x1g[:, dmt * T:(dmt + 1) * T], x1g[:, dmt * T:(dmt + 1) * T],
                xn1[:, dmt * T:(dmt + 1) * T]), [(eV, 1)])
            plan.op("vector", lambda dmt=dmt: nc.vector.tensor_mul(
                sqb, xn1[:, dmt * T:(dmt + 1) * T], rms_b), [(eV, 1)])
            plan.op("vector", lambda dmt=dmt: nc.vector.tensor_scalar_mul(
                normed[:, dmt * T:(dmt + 1) * T], sqb, nw_sb[:, dmt:dmt + 1]), [(eV, 1)])
        normed_v = eV.c
        if d_nrm is not None:
            plan.wait("gpsimd", eV, normed_v)
            for dmt in range(DMT):
                dma("gpsimd", qG, d_nrm[dmt * 128:(dmt + 1) * 128, :], normed[:, dmt * T:(dmt + 1) * T])

        plan.wait("sync", eV, normed_v)
        dma("sync", qS, wg16.rearrange("p (k c) -> p k c", c=8),
            gateWT.rearrange("(k p) c -> p k c", p=128))
        wg_q = qS.c
        plan.wait("tensor", qS, wg_q)
        plan.wait("tensor", eV, normed_v)
        for fc in range(NFC):
            b = bk.acquire()
            for kt in range(DMT):
                plan.op("tensor", lambda b=b, kt=kt, fc=fc: nc.tensor.matmul(
                    b["ap"][0:8, :], wg16[:, kt * 8:(kt + 1) * 8],
                    normed[:, kt * T + fc * FC: kt * T + fc * FC + FC],
                    start=(kt == 0), stop=(kt == DMT - 1)), [(eP, 1)] if kt == DMT - 1 else ())
            plan.wait("scalar", eP)
            plan.op("scalar", lambda b=b: nc.scalar.activation(
                out=tmp16g, in_=b["ap"][0:8, :], func=AF.Identity, bias=gb_sb), [(eS, 1)])
            bk.release(b, eS)
            plan.op("scalar", lambda fc=fc: nc.scalar.copy(
                out=glT[:, fc * FC:(fc + 1) * FC], in_=tmp16g), [(eS, 1)])
        logits_s = eS.c

        plan.wait("tensor", eS, logits_s)
        bT = bk.acquire()
        for tt in range(16):
            plan.op("tensor", lambda tt=tt, b=bT: nc.tensor.transpose(
                out=b["ap"][:, tt * 8:(tt + 1) * 8], in_=glT[0:8, tt * 128:(tt + 1) * 128],
                identity=ident_sb[0:8, 0:8]), [(eP, 1)] if tt == 15 else ())
        plan.wait("vector", eP)
        plan.op("vector", lambda b=bT: nc.vector.tensor_copy(out=g_tm, in_=b["ap"][:, 0:128]), [(eV, 1)])
        bk.release(bT, eV)
        for tt in range(16):
            g = g_tm[:, tt * 8:(tt + 1) * 8]
            g_i = g.unsqueeze(1).to_broadcast([128, 8, 8])
            g_j = g.unsqueeze(2).to_broadcast([128, 8, 8])

            def t3(ap):
                return ap.rearrange("p (j i) -> p j i", j=8)

            plan.op("vector", lambda g_i=g_i, g_j=g_j: nc.vector.tensor_tensor(
                out=t3(gt3), in0=g_i, in1=g_j, op=ALU.is_gt), [(eV, 1)])
            plan.op("vector", lambda g_i=g_i, g_j=g_j: nc.vector.tensor_tensor(
                out=t3(eq3), in0=g_i, in1=g_j, op=ALU.is_equal), [(eV, 1)])
            plan.op("vector", lambda: nc.vector.tensor_mul(eq3, eq3, tri_sb), [(eV, 1)])
            plan.op("vector", lambda: nc.vector.tensor_add(gt3, gt3, eq3), [(eV, 1)])
            plan.op("vector", lambda: nc.vector.tensor_reduce(
                out=rank8, in_=t3(gt3), axis=AX.X, op=ALU.add), [(eV, 1)])
            plan.op("vector", lambda: nc.vector.tensor_scalar(
                out=rank8, in0=rank8, scalar1=1.5, scalar2=None, op0=ALU.is_le), [(eV, 1)])
            plan.op("vector", lambda g=g: nc.vector.tensor_reduce(
                out=colv1, in_=g, axis=AX.X, op=ALU.max), [(eV, 1)])
            plan.op("vector", lambda: nc.vector.tensor_scalar_mul(colv1, colv1, -1.0), [(eV, 1)])
            plan.wait("scalar", eV)
            plan.op("scalar", lambda g=g: nc.scalar.activation(
                out=kw8, in_=g, func=AF.Exp, bias=colv1), [(eS, 1)])
            plan.wait("vector", eS)
            plan.op("vector", lambda: nc.vector.tensor_mul(kw8, kw8, rank8), [(eV, 1)])
            plan.op("vector", lambda: nc.vector.tensor_reduce(
                out=col1, in_=kw8, axis=AX.X, op=ALU.add), [(eV, 1)])
            plan.op("vector", lambda: nc.vector.reciprocal(col1, col1), [(eV, 1)])
            plan.op("vector", lambda tt=tt: nc.vector.tensor_scalar_mul(
                c_tm[:, tt * 8:(tt + 1) * 8], kw8, col1), [(eV, 1)])
        ctm_v = eV.c
        if d_cw is not None:
            plan.wait("gpsimd", eV, ctm_v)
            dma("gpsimd", qG, d_cw.rearrange("(tt p) e -> p tt e", p=128),
                c_tm.rearrange("p (tt e) -> p tt e", e=8))

        w1_p = stw_p
        for eidx in range(8):
            plan.wait("tensor", eV, ctm_v)
            plan.wait("scalar", eV)
            for half4 in range(4):
                bT = bk.acquire()
                for q in range(4):
                    tt = half4 * 4 + q
                    plan.op("tensor", lambda tt=tt, q=q, b=bT, eidx=eidx: nc.tensor.transpose(
                        out=b["ap"][0:1, q * 128:(q + 1) * 128],
                        in_=c_tm[:, tt * 8 + eidx:tt * 8 + eidx + 1],
                        identity=ident_sb), [(eP, 1)] if q == 3 else ())
                plan.wait("scalar", eP)
                plan.op("scalar", lambda b=bT, half4=half4: nc.scalar.copy(
                    out=w_row[:, half4 * FC:(half4 + 1) * FC], in_=b["ap"][0:1, :]), [(eS, 1)])
                bk.release(bT, eS)
            wrow_s = eS.c
            plan.wait("tensor", eS, wrow_s)
            for fc in range(NFC):
                b = bk.acquire()
                plan.op("tensor", lambda b=b, fc=fc: nc.tensor.matmul(
                    b["ap"], ones_row_sb, w_row[0:1, fc * FC:(fc + 1) * FC],
                    start=True, stop=True), [(eP, 1)])
                plan.wait("scalar", eP)
                plan.op("scalar", lambda b=b, fc=fc: nc.scalar.copy(
                    out=cb16[:, fc * FC:(fc + 1) * FC], in_=b["ap"]), [(eS, 1)])
                bk.release(b, eS)
            cb_s = eS.c
            plan.wait("gpsimd", eV)
            dma("gpsimd", qG, b1g16, b1g_c[eidx])
            dma("gpsimd", qG, b1l16, b1l_c[eidx])
            dma("gpsimd", qG, b216, b2_c[eidx])
            bias_q = qG.c
            plan.wait("vector", qG, bias_q)
            plan.op("vector", lambda: nc.vector.tensor_copy(out=b1g_sb, in_=b1g16), [(eV, 1)])
            plan.op("vector", lambda: nc.vector.tensor_copy(out=b1l_sb, in_=b1l16), [(eV, 1)])
            plan.op("vector", lambda: nc.vector.tensor_copy(out=b2_sb, in_=b216), [(eV, 1)])
            bias_v = eV.c
            def ffn_mlp1(quarter):
                ab = act16[quarter % 2]
                for il in range(2):
                    it = quarter * 2 + il
                    plan.wait("sync", eP, w1p_st["p"])
                    dma("sync", qS, w1g_16.rearrange("p (k c) -> p k c", c=128),
                        m1gT[eidx][:, it * 128:(it + 1) * 128].rearrange("(k p) c -> p k c", p=128))
                    dma("sync", qS, w1l_16.rearrange("p (k c) -> p k c", c=128),
                        m1lT[eidx][:, it * 128:(it + 1) * 128].rearrange("(k p) c -> p k c", p=128))
                    w1_q = qS.c
                    plan.wait("tensor", qS, w1_q)
                    plan.wait("tensor", eV, normed_v)
                    c_alpha = ccol(1.702)
                    plan.wait("scalar", eV)
                    for fc in range(NFC):
                        bg = bk.acquire()
                        for kt in range(DMT):
                            plan.op("tensor", lambda b=bg, kt=kt, fc=fc: nc.tensor.matmul(
                                b["ap"], w1g_16[:, kt * 128:(kt + 1) * 128],
                                normed[:, kt * T + fc * FC: kt * T + fc * FC + FC],
                                start=(kt == 0), stop=(kt == DMT - 1)), [(eP, 1)] if kt == DMT - 1 else ())
                        bl = bk.acquire()
                        for kt in range(DMT):
                            plan.op("tensor", lambda b=bl, kt=kt, fc=fc: nc.tensor.matmul(
                                b["ap"], w1l_16[:, kt * 128:(kt + 1) * 128],
                                normed[:, kt * T + fc * FC: kt * T + fc * FC + FC],
                                start=(kt == 0), stop=(kt == DMT - 1)), [(eP, 1)] if kt == DMT - 1 else ())
                        plan.wait("scalar", eP)
                        plan.op("scalar", lambda b=bg, fc=fc: nc.scalar.copy(
                            out=pg_all[:, fc * FC:(fc + 1) * FC], in_=b["ap"]), [(eS, 1)])
                        bk.release(bg, eS)
                        plan.op("scalar", lambda b=bl, fc=fc: nc.scalar.copy(
                            out=pl_all[:, fc * FC:(fc + 1) * FC], in_=b["ap"]), [(eS, 1)])
                        bk.release(bl, eS)
                    w1p_st["p"] = eP.c
                    evac_s = eS.c
                    plan.wait("vector", eS, evac_s)
                    plan.wait("vector", eV, bias_v)
                    plan.op("vector", lambda it=it: nc.vector.tensor_scalar(
                        out=pg_all, in0=pg_all, scalar1=b1g_sb[:, it:it + 1], scalar2=7.0,
                        op0=ALU.add, op1=ALU.min), [(eV, 1)])
                    plan.op("vector", lambda it=it: nc.vector.tensor_scalar(
                        out=pl_all, in0=pl_all, scalar1=b1l_sb[:, it:it + 1], scalar2=7.0,
                        op0=ALU.add, op1=ALU.min), [(eV, 1)])
                    plan.op("vector", lambda: nc.vector.tensor_scalar(
                        out=pl_all, in0=pl_all, scalar1=-7.0, scalar2=1.0,
                        op0=ALU.max, op1=ALU.add), [(eV, 1)])
                    g_v = eV.c
                    plan.wait("scalar", eV, g_v)
                    plan.op("scalar", lambda: nc.scalar.activation(
                        out=sg_all, in_=pg_all, func=AF.Sigmoid, scale=c_alpha[:, :]), [(eS, 1)])
                    plan.wait("vector", eS)
                    plan.op("vector", lambda: nc.vector.tensor_mul(pg_all, pg_all, sg_all), [(eV, 1)])
                    plan.op("vector", lambda il=il, ab=ab: nc.vector.tensor_mul(
                        ab[:, il * T:(il + 1) * T], pg_all, pl_all), [(eV, 1)])
                for il in range(2):
                    plan.wait("vector", eS, cb_s)
                    plan.op("vector", lambda il=il, ab=ab: nc.vector.tensor_mul(
                        ab[:, il * T:(il + 1) * T], ab[:, il * T:(il + 1) * T], cb16), [(eV, 1)])
                return eV.c

            def ffn_p2(quarter, actc_v):
                ab = act16[quarter % 2]
                for ot in range(DMT):
                    plan.wait("sync", eP, stw_p_st["p"])
                    dma("sync", qS, w2_16[:, 0:2 * 128].rearrange("p (k c) -> p k c", c=128),
                        m2T[eidx][quarter * 256:(quarter + 1) * 256, ot * 128:(ot + 1) * 128].rearrange(
                            "(k p) c -> p k c", p=128))
                    w2_q = qS.c
                    plan.wait("tensor", qS, w2_q)
                    plan.wait("tensor", eV, actc_v)
                    for fc in range(NFC):
                        b = bk.acquire()
                        for kt in range(2):
                            plan.op("tensor", lambda b=b, kt=kt, fc=fc, ab=ab: nc.tensor.matmul(
                                b["ap"], w2_16[:, kt * 128:(kt + 1) * 128],
                                ab[:, kt * T + fc * FC: kt * T + fc * FC + FC],
                                start=(kt == 0), stop=(kt == 1)), [(eP, 1)] if kt == 1 else ())
                        plan.wait("vector", eP)
                        plan.op("vector", lambda b=b, ot=ot, fc=fc: nc.vector.tensor_add(
                            x1g[:, ot * T + fc * FC: ot * T + fc * FC + FC],
                            x1g[:, ot * T + fc * FC: ot * T + fc * FC + FC], b["ap"]), [(eV, 1)])
                        bk.release(b, eV)
                    stw_p_st["p"] = eP.c

            prev = None
            for quarter in range(4):
                av = ffn_mlp1(quarter)
                if prev is not None:
                    ffn_p2(prev[0], prev[1])
                prev = (quarter, av)
            ffn_p2(prev[0], prev[1])
            for ot in range(DMT):
                plan.wait("vector", eS, cb_s)
                plan.op("vector", lambda ot=ot: nc.vector.scalar_tensor_tensor(
                    out=x1g[:, ot * T:(ot + 1) * T], in0=cb16,
                    scalar=b2_sb[:, ot:ot + 1], in1=x1g[:, ot * T:(ot + 1) * T],
                    op0=ALU.mult, op1=ALU.add), [(eV, 1)])

        plan.wait("gpsimd", eV)
        for dmt in range(DMT):
            dma("gpsimd", qG, x2_out[dmt * 128:(dmt + 1) * 128, :], x1g[:, dmt * T:(dmt + 1) * T])

        plan.wait("gpsimd", qG)
        plan.wait("gpsimd", qS)

    return {}


def build_l1(dbg=()):
    nc = bass.Bass("TRN2", target_bir_lowering=False)
    with ExitStack() as ctx:
        fc = _mk_fc(nc, ctx.enter_context, maincols=43264)
        _emit_l1(fc, dbg, standalone=True)
        _run_block(fc, ctx.enter_context)
    return nc


def build_fused(dbg=()):
    nc = bass.Bass("TRN2", target_bir_lowering=False)
    with ExitStack() as ctx:
        fc = _mk_fc(nc, ctx.enter_context, maincols=43264)
        r0 = _emit_l0(fc, dbg=(), standalone=False)
        fc["x1_acc"] = r0["x1_acc"]
        _emit_l1(fc, dbg=dbg, standalone=False)
        _run_block(fc, ctx.enter_context)
    return nc


def _col_layout(v, ntiles):
    return np.ascontiguousarray(np.asarray(v, dtype=np.float32).reshape(ntiles, 128).T)


def prep_l0_inputs(inputs):
    shared = {
        "W_in": np.ascontiguousarray(inputs["L0_attn_W_in"]),
        "conv_w": np.ascontiguousarray(inputs["L0_attn_conv_w"]),
        "conv_bc": _col_layout(inputs["L0_attn_conv_b"], DIT),
        "W_x": np.ascontiguousarray(inputs["L0_attn_W_x"]),
        "W_dt": np.ascontiguousarray(inputs["L0_attn_W_dt"]),
        "b_dtc": _col_layout(inputs["L0_attn_b_dt"], DIT),
        "A_log": np.ascontiguousarray(inputs["L0_attn_A_log"]),
        "D_c": _col_layout(inputs["L0_attn_D"], DIT),
        "W_out": np.ascontiguousarray(inputs["L0_attn_W_out"]),
        "Wg": np.ascontiguousarray(inputs["L0_ffn_Wg"]),
        "Ws1": np.ascontiguousarray(inputs["L0_ffn_Ws1"]),
        "Ws3": np.ascontiguousarray(inputs["L0_ffn_Ws3"]),
        "Ws2": np.ascontiguousarray(inputs["L0_ffn_Ws2"]),
        "We1": np.ascontiguousarray(inputs["L0_ffn_We1"]),
        "We3": np.ascontiguousarray(inputs["L0_ffn_We3"]),
        "We2": np.ascontiguousarray(inputs["L0_ffn_We2"]),
        "ones_col": np.ones((128, 1), np.float32),
        "ones_row": np.ones((1, 128), np.float32),
        "ident_f32": np.eye(128, dtype=np.float32),
        "ident_b16": np.eye(128, dtype=BF),
        "tri_lt": np.broadcast_to(np.tril(np.ones((8, 8), np.float32), -1).reshape(64), (128, 64)).copy(),
    }
    x = np.asarray(inputs["L0_attn_x"])
    maps = []
    for i in range(NCORES):
        m = dict(shared)
        m["xT"] = np.ascontiguousarray(x[i].T)
        maps.append(m)
    return maps


def prep_l1_inputs(inputs, x1s=None):
    rel = np.asarray(inputs["L1_attn_rel"], np.float32)
    li = np.arange(8)
    bi = np.arange(16)
    ii_b = np.repeat(bi, 8)
    ii_l = np.tile(li, 16)
    samb = ii_b[:, None] == ii_b[None, :]
    relidx = (ii_l[None, :] - ii_l[:, None]) + 2047
    Mm = np.where(samb, rel[relidx], np.float32(NEG)).astype(np.float32)

    m1 = np.asarray(inputs["L1_ffn_mlp1_weight"])
    b1 = np.asarray(inputs["L1_ffn_mlp1_bias"])
    m2 = np.asarray(inputs["L1_ffn_mlp2_weight"])
    b2 = np.asarray(inputs["L1_ffn_mlp2_bias"])

    def colb(b):
        return np.ascontiguousarray(b.reshape(8, 8, 128).transpose(0, 2, 1))

    shared = {
        "Wq": np.ascontiguousarray(inputs["L1_attn_Wq"]),
        "Wk": np.ascontiguousarray(inputs["L1_attn_Wk"]),
        "Wv": np.ascontiguousarray(inputs["L1_attn_Wv"]),
        "Wo": np.ascontiguousarray(inputs["L1_attn_Wo"]),
        "bq_c": _col_layout(inputs["L1_attn_bq"], DMT),
        "bk_c": _col_layout(inputs["L1_attn_bk"], DMT),
        "bv_c": _col_layout(inputs["L1_attn_bv"], DMT),
        "bo_c": _col_layout(inputs["L1_attn_bo"], DMT),
        "Mmask": Mm,
        "nw_c": _col_layout(inputs["L1_ffn_norm_weight"], DMT),
        "gateWT": np.ascontiguousarray(np.asarray(inputs["L1_ffn_gate_weight"]).T),
        "gateB": np.ascontiguousarray(np.asarray(inputs["L1_ffn_gate_bias"]).reshape(8, 1)),
        "m1gT": np.ascontiguousarray(m1[:, 0::2, :].transpose(0, 2, 1)),
        "m1lT": np.ascontiguousarray(m1[:, 1::2, :].transpose(0, 2, 1)),
        "b1g_c": colb(b1[:, 0::2]),
        "b1l_c": colb(b1[:, 1::2]),
        "m2T": np.ascontiguousarray(m2.transpose(0, 2, 1)),
        "b2_c": colb(b2),
        "ones_col": np.ones((128, 1), np.float32),
        "ones_row": np.ones((1, 128), np.float32),
        "ident_f32": np.eye(128, dtype=np.float32),
        "ident_b16": np.eye(128, dtype=BF),
        "tri_lt": np.broadcast_to(np.tril(np.ones((8, 8), np.float32), -1).reshape(64), (128, 64)).copy(),
    }
    maps = []
    for j in range(NCORES):
        m = dict(shared)
        if x1s is not None:
            m["x1T"] = np.ascontiguousarray(
                np.concatenate([x1s[l][:, j * 256:(j + 1) * 256] for l in range(NCORES)], axis=1))
        maps.append(m)
    return maps


_NC_CACHE = {}


def kernel(**inputs):
    if "fused" not in _NC_CACHE:
        _NC_CACHE["fused"] = build_fused()
    mapsA = prep_l0_inputs(inputs)
    mapsB = prep_l1_inputs(inputs)
    maps = [{**a, **b} for a, b in zip(mapsA, mapsB)]
    res = run_bass_kernel_spmd(_NC_CACHE["fused"], maps, core_ids=list(range(NCORES)))
    out = np.empty((NCORES, T, DM), np.float32)
    for j in range(NCORES):
        xj = np.asarray(res.results[j]["x2_out"])
        for l in range(NCORES):
            out[l, j * 256:(j + 1) * 256, :] = xj[:, l * 256:(l + 1) * 256].T
    return out


def kernel_unfused(**inputs):
    if "l0" not in _NC_CACHE:
        _NC_CACHE["l0"] = build_l0()
    if "l1" not in _NC_CACHE:
        _NC_CACHE["l1"] = build_l1()
    mapsA = prep_l0_inputs(inputs)
    resA = run_bass_kernel_spmd(_NC_CACHE["l0"], mapsA, core_ids=list(range(NCORES)))
    x1s = [np.asarray(resA.results[i]["x1_out"]) for i in range(NCORES)]
    mapsB = prep_l1_inputs(inputs, x1s)
    resB = run_bass_kernel_spmd(_NC_CACHE["l1"], mapsB, core_ids=list(range(NCORES)))
    out = np.empty((NCORES, T, DM), np.float32)
    for j in range(NCORES):
        xj = np.asarray(resB.results[j]["x2_out"])
        for l in range(NCORES):
            out[l, j * 256:(j + 1) * 256, :] = xj[:, l * 256:(l + 1) * 256].T
    return out
